# version 24
# baseline (speedup 1.0000x reference)
import numpy as np
import ml_dtypes
import concourse.bass as bass
import concourse.mybir as mybir
from concourse.bass_utils import run_bass_kernel_spmd

F32 = mybir.dt.float32
BF16 = mybir.dt.bfloat16
AF = mybir.ActivationFunctionType
ALU = mybir.AluOpType
BF = ml_dtypes.bfloat16

NCORES = 8
SEQS = 2
T = 2048
C = 1024
NCH = 8
H = 16
DH = 64
FC = 4096
FCH = 32
G = 4
GT = 512
GF = 4
GTF = 512
NB = 512
ESH = -24.0
SC = 0.125

ENGS = ("tensor", "vector", "scalar", "gpsimd", "sync")


class Buf:
    __slots__ = ("name", "writer", "readers")

    def __init__(self, name):
        self.name = name
        self.writer = None
        self.readers = []


class Op:
    __slots__ = ("fn", "inc", "line")

    def __init__(self, fn, inc):
        self.fn = fn
        self.inc = inc
        import sys as _s
        f = _s._getframe(2)
        self.line = f.f_lineno


class Wait:
    __slots__ = ("src", "val")

    def __init__(self, src, val):
        self.src = src
        self.val = val


class SB:

    def __init__(self):
        self.plan = {e: [] for e in ENGS}
        self.cnt = {e: 0 for e in ENGS}
        self.waited = {e: {} for e in ENGS}

    def op(self, eng, fn, reads=(), writes=(), inc=1):
        need = {}
        dma_q = eng in ("sync", "gpsimd")

        def add(pe, pv):
            if (pe != eng or dma_q) and pv > need.get(pe, 0):
                need[pe] = pv

        for b in reads:
            if b.writer is not None:
                add(*b.writer)
        for b in writes:
            if b.writer is not None:
                add(*b.writer)
            for r in b.readers:
                add(*r)
        for pe in sorted(need):
            pv = need[pe]
            if self.waited[eng].get(pe, 0) < pv:
                self.plan[eng].append(Wait(pe, pv))
                self.waited[eng][pe] = pv
        self.plan[eng].append(Op(fn, inc))
        self.cnt[eng] += inc
        newv = self.cnt[eng]
        for b in writes:
            b.writer = (eng, newv)
            b.readers = []
        for b in reads:
            b.readers.append((eng, newv))


def build_program(dbg=False):
    nc = bass.Bass("TRN2", target_bir_lowering=False, debug=False)

    x_d = nc.dram_tensor("x", [SEQS, C, T], F32, kind="ExternalInput").ap()
    y_d = nc.dram_tensor("y", [SEQS, C, T], F32, kind="ExternalOutput").ap()
    wts = {}
    for nm, k, f in [("wt_attn", C, 3 * C), ("wt_proj", C, C), ("wt_fc", C, FC),
                     ("wt_mlp", FC, C), ("wt_p", C, C), ("wt_q", C, C),
                     ("wt_k", C, C), ("wt_v", C, C), ("wt_o", C, C),
                     ("wt_ffn_fc", C, FC), ("wt_ffn_proj", FC, C)]:
        wts[nm] = nc.dram_tensor(nm, [k, f], F32, kind="ExternalInput").ap()
    wcols = {}
    woff = [0]

    def walloc(key, width):
        wcols[key] = (woff[0], width)
        woff[0] += width
    for _nm, _kd, _fd in [("wt_attn", C, 2 * C), ("wt_proj", C, C),
                          ("wt_fc", C, FC), ("wt_p", C, C), ("wt_q", C, C),
                          ("wt_k", C, C), ("wt_v", C, C), ("wt_o", C, C),
                          ("wt_ffn_fc", C, FC)]:
        for _m in range(_fd // 128):
            walloc((_nm, _m), (_kd // 128) * 128)
    for _nm in ("wt_mlp", "wt_ffn_proj"):
        for _m in range(NCH):
            for _h in range(2):
                walloc((_nm, _m, _h), 16 * 128)
    for _h in range(4):
        walloc(("wvs", _h), NCH * 256)
    wscr = nc.dram_tensor("wscr", [128, woff[0]], BF16, kind="Internal").ap()

    bss = {}
    for nm, f in [("b_attn", 3 * C), ("b_proj", C), ("b_fc", FC), ("b_mlp", C),
                  ("b_p", C), ("b_q", C), ("b_k", C), ("b_v", C), ("b_o", C)]:
        bss[nm] = nc.dram_tensor(nm, [128, f // 128], F32,
                                 kind="ExternalInput").ap()
    bv_row_d = nc.dram_tensor("b_attn_v", [1, C], F32, kind="ExternalInput").ap()

    ii = np.arange(896)[None, :]
    pp = np.arange(128)[:, None]
    mask_np = (ii >= pp + 384).astype(BF)
    e_np = np.zeros((128, NCH, 16), dtype=BF)
    f_np = np.zeros((16, NCH, 128), dtype=BF)
    for c in range(NCH):
        for p in range(128):
            h = 2 * c + p // 64
            e_np[p, c, h] = 1
            f_np[h, c, p] = 1
    ones_col_d = nc.inline_tensor(np.ones((128, 1), dtype=BF), name="d_onc").ap()
    esh_d = nc.inline_tensor(np.full((128, 1), ESH, dtype=np.float32),
                             name="d_esh").ap()
    eps5_d = nc.inline_tensor(np.full((128, 1), 1e-5, dtype=np.float32),
                              name="d_eps5").ap()
    eps10_d = nc.inline_tensor(np.full((128, 1), 1e-10, dtype=np.float32),
                               name="d_eps10").ap()
    ones_row_d = nc.inline_tensor(np.ones((1, 128), dtype=BF), name="d_onr").ap()
    mask_d = nc.inline_tensor(np.ascontiguousarray(mask_np), name="d_mask").ap()
    e_d = nc.inline_tensor(np.ascontiguousarray(e_np), name="d_eind").ap()
    f_d = nc.inline_tensor(np.ascontiguousarray(f_np), name="d_find").ap()

    K = 1024
    SB_BASE = 16512
    A = nc.alloc_sbuf_tensor_at

    def at(name, shape, dtype, off):
        return A(name, shape, dtype, offset=SB_BASE + off)

    XS = at("XS", [128, NCH, T], F32, 0)
    KB = at("KB", [128, NCH, T], BF16, 64 * K)
    SPKYA = at("SPKYA", [128, NCH, GTF], BF16, 64 * K)
    SPKYB = at("SPKYB", [128, NCH, GTF], BF16, 64 * K + 8 * K)
    SPKM = at("SPKM", [128, NCH, GTF], F32, 64 * K + 16 * K)
    VB = at("VB", [128, 16, C], BF16, 96 * K)
    HBF = at("HBF", [128, FCH, GTF], BF16, 96 * K)
    VC = at("VC", [128, NCH, T], BF16, 96 * K)
    UG = at("UG", [128, NCH, GT], BF16, 130 * K)
    YG = at("YG", [128, NCH, GT], BF16, 138 * K)
    XBG = at("XBG", [128, NCH, GT], BF16, 146 * K)
    WROW = [at(f"WROW{i}", [16, GT], BF16, 146 * K + 5 * K + i * K)
            for i in range(3)]
    SQG = at("SQG", [128, NCH, GT], BF16, 154 * K)
    SRND = at("SRND", [33, NB], F32, 154 * K)
    SRNDB = at("SRNDB", [33, NB], BF16, 154 * K + 2 * K)
    SRNDB2 = at("SRNDB2", [1, NB], BF16, 154 * K + 3 * K)
    ABS0 = at("ABS0", [16, GT], F32, 154 * K)
    ABS1 = at("ABS1", [16, GT], F32, 154 * K + 2 * K)
    ABS2 = at("ABS2", [16, GT], F32, 154 * K + 4 * K)
    ABSM = at("ABSM", [16, GT], F32, 154 * K + 6 * K)
    QP = at("QP", [128, 2, NB], BF16, 162 * K)
    BVR = at("BVR", [1, C], BF16, 162 * K)
    PHAT = at("PHAT", [128, 4, NB], BF16, 164 * K)
    RS32 = at("RS32", [128, 2, NB], F32, 168 * K)
    BVRF = at("BVRF", [1, C], F32, 168 * K)
    RSB = at("RSB", [128, NB], F32, 172 * K)
    MRB = at("MRB", [128, 2, GT], F32, 174 * K)
    ST_S = at("ST_S", [1, GT], F32, 178 * K)
    ST_Q = at("ST_Q", [1, GT], F32, 180 * K)
    ST_R = at("ST_R", [1, GT], F32, 182 * K)
    ST_T = at("ST_T", [1, GT], F32, 154 * K)
    STB_M = at("STB_M", [1, GT], BF16, 184 * K)
    STB_R = at("STB_R", [1, GT], BF16, 185 * K)
    SBF = at("SBF", [128, NCH, GTF], BF16, 138 * K)
    XGC = at("XGC", [128, 2, GTF], F32, 186 * K)
    PSTG = at("PSTG", [128, 2, NCH, 128], BF16, 186 * K)
    WR = [at(f"WR{i}", [128, 16, 128], BF16, 190 * K + 4 * K * i) for i in range(3)]
    WR3 = at("WR3", [128, 16, 128], BF16, 164 * K)
    cb = 202 * K
    ONESC = at("ONESC", [128, 1], BF16, cb); cb += 32
    ONESR = at("ONESR", [1, 128], BF16, cb); cb += 256
    EIND = at("EIND", [128, NCH, 16], BF16, cb); cb += 256
    BIAS = at("BIAS", [128, 112], F32, cb); cb += 448
    MASK = at("MASK", [128, 896], BF16, cb); cb += 1792
    ESHC = at("ESHC", [128, 1], F32, cb); cb += 32
    EPS5C = at("EPS5C", [128, 1], F32, cb); cb += 32
    EPS10C = at("EPS10C", [128, 1], F32, cb); cb += 32
    FIND = at("FIND", [16, NCH, 128], BF16, cb); cb += 2048
    assert cb <= 208 * K - 128, cb

    bias_off = {}
    off = 0
    for nm, f in [("b_attn", 3 * C), ("b_proj", C), ("b_fc", FC), ("b_mlp", C),
                  ("b_p", C), ("b_q", C), ("b_k", C), ("b_v", C), ("b_o", C)]:
        bias_off[nm] = off
        off += f // 128

    PSA = nc.alloc_psum_tensor("PSA", [128, 2, NB], F32)
    PSB = nc.alloc_psum_tensor("PSB", [128, 2, NB], F32)
    PS47 = [nc.alloc_psum_tensor(f"PS{i}", [128, NB], F32) for i in range(4, 8)]
    PS = [PSA[:, 0, :], PSA[:, 1, :], PSB[:, 0, :], PSB[:, 1, :]] + \
         [t[:, :] for t in PS47]

    sb = SB()
    bufs = {}

    def B(name):
        if name not in bufs:
            bufs[name] = Buf(name)
        return bufs[name]

    alias = { "HBF": "VB", "VC": "VB",
             "WROW0": "XBG", "WROW1": "XBG", "WROW2": "XBG",
             "SRND": "SQG", "SRNDB": "SQG", "SRNDB2": "SQG", "ST_T": "SQG",
             "ABS0": "SQG", "ABS1": "SQG", "ABS2": "SQG", "ABSM": "SQG",
             "BVR": "QP", "SBF": "YG", "PSTG": "XGC", "BVRF": "RS32",
             "WR3": "PHAT"}

    def RB(name):
        return B(alias.get(name, name))

    psb = [B(f"ps{i}") for i in range(8)]
    XSB = [B(f"XS_{i}") for i in range(G)]
    UGC = [B(f"UG_{c}") for c in range(NCH)]

    def xsg(lo):
        return XSB[lo // GT]

    for dst, src in [(ONESC, ones_col_d), (ONESR, ones_row_d), (MASK, mask_d),
                     (EIND, e_d), (FIND, f_d), (ESHC, esh_d),
                     (EPS5C, eps5_d), (EPS10C, eps10_d)]:
        sb.op("sync", lambda e, dst=dst, src=src: e.dma_start(out=dst[:], in_=src),
              writes=[B(dst.name)], inc=16)
    for nm in bias_off:
        o, w = bias_off[nm], bss[nm].shape[1]
        src = bss[nm]
        sb.op("sync", lambda e, o=o, w=w, src=src:
              e.dma_start(out=BIAS[:, o:o + w], in_=src),
              writes=[B("BIAS")], inc=16)

    def fill_order():
        yield ("wt_attn", C, 2 * C)
        yield ("wvs", None, None)
        yield ("wt_proj", C, C)
        yield ("wt_fc", C, FC)
        yield ("wt_mlp", None, None)
        yield ("wt_ffn_fc", C, FC)
        yield ("wt_ffn_proj", None, None)
        yield ("wt_p", C, C)
        yield ("wt_q", C, C)
        yield ("wt_k", C, C)
        yield ("wt_v", C, C)
        yield ("wt_o", C, C)

    fill_i = [0]
    pending_store = [None]

    def fill_tile(load_fn, store_fn):
        load_fn()
        if pending_store[0] is not None:
            pending_store[0]()
        pending_store[0] = store_fn

    for nm, kd, fd in fill_order():
        if nm == "wvs":
            for h in range(4):
                i = fill_i[0] % len(WR)
                fill_i[0] += 1
                slot, wbuf = WR[i], RB(f"WR{i}")
                o, w = wcols[("wvs", h)]
                src = wts["wt_attn"][:, 2 * C + h * 256:2 * C + (h + 1) * 256]
                src = src.rearrange("(c p) f -> p c f", p=128)
                fill_tile(
                    lambda slot=slot, src=src, wbuf=wbuf: sb.op(
                        "gpsimd", lambda e: e.dma_start(
                            out=slot.ap().rearrange("p a b -> p (a b)")
                            .rearrange("p (c f) -> p c f", c=NCH)[:, :, :], in_=src),
                        writes=[wbuf], inc=16),
                    lambda slot=slot, o=o, w=w, wbuf=wbuf, h=h: sb.op(
                        "gpsimd", lambda e: e.dma_start(
                            out=wscr[:, o:o + w],
                            in_=slot.ap().rearrange("p a b -> p (a b)")[:, 0:w]),
                        reads=[wbuf], writes=[B(("wvs", h))], inc=16))
            continue
        if nm in ("wt_mlp", "wt_ffn_proj"):
            for m in range(NCH):
                for h in range(2):
                    for hh in range(2):
                        i = fill_i[0] % 2
                        fill_i[0] += 1
                        stg = PSTG[:, i, :, :]
                        sbuf_ = RB("PSTG")
                        r0 = h * 2048 + hh * 1024
                        src = wts[nm][r0:r0 + 1024, m * 128:(m + 1) * 128]
                        src = src.rearrange("(c p) f -> p c f", p=128)
                        o, w = wcols[(nm, m, h)]
                        oo = o + hh * 1024
                        fill_tile(
                            lambda stg=stg, src=src, sbuf_=sbuf_: sb.op(
                                "gpsimd", lambda e: e.dma_start(
                                    out=stg, in_=src),
                                writes=[sbuf_], inc=16),
                            lambda stg=stg, oo=oo, sbuf_=sbuf_, nm=nm, m=m, h=h:
                            sb.op("gpsimd", lambda e: e.dma_start(
                                out=wscr[:, oo:oo + 1024],
                                in_=stg.rearrange("p a b -> p (a b)")),
                                reads=[sbuf_], writes=[B((nm, m, h))], inc=16))
            continue
        use_ring = nm == "wt_attn"
        for m in range(fd // 128):
            src = wts[nm][:, m * 128:(m + 1) * 128]
            src = src.rearrange("(c p) f -> p c f", p=128)
            o, w = wcols[(nm, m)]
            if use_ring:
                i = fill_i[0] % len(WR)
                fill_i[0] += 1
                slot, wbuf = WR[i], RB(f"WR{i}")
                fill_tile(
                    lambda slot=slot, src=src, wbuf=wbuf: sb.op(
                        "gpsimd", lambda e: e.dma_start(
                            out=slot[:, 0:NCH, :], in_=src), writes=[wbuf], inc=16),
                    lambda slot=slot, o=o, w=w, wbuf=wbuf, nm=nm, m=m: sb.op(
                        "gpsimd", lambda e: e.dma_start(
                            out=wscr[:, o:o + w],
                            in_=slot.ap().rearrange("p a b -> p (a b)")[:, 0:w]),
                        reads=[wbuf], writes=[B((nm, m))], inc=16))
            else:
                i = fill_i[0] % 2
                fill_i[0] += 1
                stg = PSTG[:, i, :, :]
                sbuf_ = RB("PSTG")
                fill_tile(
                    lambda stg=stg, src=src, sbuf_=sbuf_: sb.op(
                        "gpsimd", lambda e: e.dma_start(out=stg, in_=src),
                        writes=[sbuf_], inc=16),
                    lambda stg=stg, o=o, w=w, sbuf_=sbuf_, nm=nm, m=m: sb.op(
                        "gpsimd", lambda e: e.dma_start(
                            out=wscr[:, o:o + w],
                            in_=stg.rearrange("p a b -> p (a b)")),
                        reads=[sbuf_], writes=[B((nm, m))], inc=16))
    if pending_store[0] is not None:
        pending_store[0]()
        pending_store[0] = None

    def bias_ap(nm, chunk):
        return BIAS[:, bias_off[nm] + chunk:bias_off[nm] + chunk + 1]

    wr_i = [0]
    RING_N = [3]
    wpair_cache = {}

    def ring_slot():
        i = wr_i[0] % RING_N[0]
        wr_i[0] += 1
        slots = WR + [WR3]
        return slots[i], RB(f"WR{i}" if i < 3 else "WR3")

    def wtile_pair(nm, m0):
        key = (nm, m0)
        if key in wpair_cache:
            return wpair_cache[key]
        slot, wbuf = ring_slot()
        o, w = wcols[(nm, m0)]
        o2, w2 = wcols[(nm, m0 + 1)]
        assert o2 == o + w
        sb.op("sync", lambda e, slot=slot, o=o, w=w, w2=w2: e.dma_start(
            out=slot.ap().rearrange("p a b -> p (a b)")[:, 0:w + w2],
            in_=wscr[:, o:o + w + w2]),
            reads=[B((nm, m0)), B((nm, m0 + 1))], writes=[wbuf], inc=16)
        wpair_cache.clear()
        wpair_cache[key] = (slot, wbuf)
        return slot, wbuf

    def wtile_small(nm, m):
        m0 = m - (m % 2)
        slot, wbuf = wtile_pair(nm, m0)
        sub = slot.ap()[:, 8 * (m % 2):8 * (m % 2) + NCH, :]
        return sub, wbuf

    def wtile_large(nm, m, half):
        slot, wbuf = ring_slot()
        o, w = wcols[(nm, m, half)]
        sb.op("sync", lambda e, slot=slot, o=o, w=w: e.dma_start(
            out=slot.ap().rearrange("p a b -> p (a b)"), in_=wscr[:, o:o + w]),
            reads=[B((nm, m, half))], writes=[wbuf], inc=16)
        wpair_cache.clear()
        return slot, wbuf

    def newton_rsqrt(src, dst, tmp, rbufs, n):
        rbuf = rbufs
        I32 = mybir.dt.int32
        sb.op("vector", lambda e: e.tensor_single_scalar(
            dst.bitcast(I32), src.bitcast(I32), 1, ALU.arith_shift_right),
            reads=rbuf, writes=rbuf)
        sb.op("vector", lambda e: e.tensor_scalar(
            dst.bitcast(I32), dst.bitcast(I32), -1, 0x5F3759DF, ALU.mult, ALU.add),
            reads=rbuf, writes=rbuf)
        for _ in range(2):
            sb.op("vector", lambda e: e.tensor_mul(tmp, dst, dst),
                  reads=rbuf, writes=rbuf)
            sb.op("vector", lambda e: e.tensor_mul(tmp, tmp, src),
                  reads=rbuf, writes=rbuf)
            sb.op("vector", lambda e: e.tensor_scalar(tmp, tmp, -0.5, 1.5,
                                                      ALU.mult, ALU.add),
                  reads=rbuf, writes=rbuf)
            sb.op("vector", lambda e: e.tensor_mul(dst, dst, tmp),
                  reads=rbuf, writes=rbuf)

    def ln_pre(src_t, src_buf, lo, n):
        xbg, sqg = RB("XBG"), RB("SQG")
        sb.op("scalar", lambda e: e.activation(
            XBG[:, :, 0:n], src_t[:, :, lo:lo + n], AF.Copy),
            reads=[src_buf], writes=[xbg])
        sb.op("scalar", lambda e: e.activation(
            SQG[:, :, 0:n], src_t[:, :, lo:lo + n], AF.Square),
            reads=[src_buf], writes=[sqg])

    def ln_stats(src_t, src_buf, lo, n, eps, pre_done=False):
        stat = B("STAT_ROWS")
        xbg, sqg = RB("XBG"), RB("SQG")
        if not pre_done:
            ln_pre(src_t, src_buf, lo, n)
        nblk = (n + NB - 1) // NB
        for bi in range(nblk):
            c0, cw = bi * NB, min(NB, n - bi * NB)
            for c in range(NCH):
                sb.op("tensor", lambda e, c=c, c0=c0, cw=cw: e.matmul(
                    PS[0][0:1, 0:cw], ONESC[:], XBG[:, c, c0:c0 + cw],
                    start=(c == 0), stop=(c == NCH - 1)),
                    reads=[xbg, B("ONESC")], writes=[psb[0]])
            for c in range(NCH):
                sb.op("tensor", lambda e, c=c, c0=c0, cw=cw: e.matmul(
                    PS[1][0:1, 0:cw], ONESC[:], SQG[:, c, c0:c0 + cw],
                    start=(c == 0), stop=(c == NCH - 1)),
                    reads=[sqg, B("ONESC")], writes=[psb[1]])
            epsc = EPS5C if eps > 1e-7 else EPS10C
            sb.op("scalar", lambda e, c0=c0, cw=cw: e.activation(
                ST_S[0:1, c0:c0 + cw], PS[0][0:1, 0:cw], AF.Identity,
                scale=1.0 / C),
                reads=[psb[0]], writes=[stat])
            sb.op("scalar", lambda e, c0=c0, cw=cw, epsc=epsc: e.activation(
                ST_Q[0:1, c0:c0 + cw], PS[1][0:1, 0:cw], AF.Identity,
                bias=epsc[0:1, 0:1], scale=1.0 / C),
                reads=[psb[1], B(epsc.name)], writes=[stat])
        sn = ST_S[0:1, 0:n]
        qn = ST_Q[0:1, 0:n]
        rn = ST_R[0:1, 0:n]
        tn = ST_T[0:1, 0:n]
        sb.op("vector", lambda e: e.tensor_mul(tn, sn, sn),
              reads=[stat, sqg], writes=[stat, sqg])
        sb.op("vector", lambda e: e.tensor_sub(qn, qn, tn),
              reads=[stat, sqg], writes=[stat])
        newton_rsqrt(qn, rn, tn, [stat, sqg], n)
        sb.op("vector", lambda e: e.tensor_copy(STB_M[0:1, 0:n], sn),
              reads=[stat], writes=[stat])
        sb.op("vector", lambda e: e.tensor_copy(STB_R[0:1, 0:n], rn),
              reads=[stat], writes=[stat])

    def ln_finish(src_t, src_buf, lo, n, dst_t, dst_buf, dst_lo=None,
                  f32_dst=False):
        if dst_lo is None:
            dst_lo = 0
        stat = B("STAT_ROWS")
        mrb = RB("MRB")
        nblk = (n + NB - 1) // NB
        for bi in range(nblk):
            c0, cw = bi * NB, min(NB, n - bi * NB)
            sb.op("tensor", lambda e, c0=c0, cw=cw: e.matmul(
                PS[2][:, 0:cw], ONESR[:], STB_M[0:1, c0:c0 + cw],
                start=True, stop=True), reads=[stat, B("ONESR")], writes=[psb[2]])
            sb.op("tensor", lambda e, c0=c0, cw=cw: e.matmul(
                PS[3][:, 0:cw], ONESR[:], STB_R[0:1, c0:c0 + cw],
                start=True, stop=True), reads=[stat, B("ONESR")], writes=[psb[3]])
            sb.op("scalar", lambda e, c0=c0, cw=cw: e.activation(
                MRB[:, 0, c0:c0 + cw], PS[2][:, 0:cw], AF.Copy),
                reads=[psb[2]], writes=[mrb])
            sb.op("scalar", lambda e, c0=c0, cw=cw: e.activation(
                MRB[:, 1, c0:c0 + cw], PS[3][:, 0:cw], AF.Copy),
                reads=[psb[3]], writes=[mrb])
        if dst_t is UG and not f32_dst:
            for c in range(NCH):
                sb.op("vector", lambda e, c=c: e.tensor_sub(
                    dst_t[:, c, dst_lo:dst_lo + n], src_t[:, c, lo:lo + n],
                    MRB[:, 0, 0:n]),
                    reads=[src_buf, mrb], writes=[UGC[c]])
                sb.op("vector", lambda e, c=c: e.tensor_mul(
                    dst_t[:, c, dst_lo:dst_lo + n],
                    dst_t[:, c, dst_lo:dst_lo + n], MRB[:, 1, 0:n]),
                    reads=[mrb, UGC[c]], writes=[UGC[c]])
            return
        mb = MRB[:, 0, 0:n].unsqueeze(1).broadcast_to([128, NCH, n])
        rb = MRB[:, 1, 0:n].unsqueeze(1).broadcast_to([128, NCH, n])
        sb.op("vector", lambda e: e.tensor_sub(
            dst_t[:, :, dst_lo:dst_lo + n], src_t[:, :, lo:lo + n], mb),
            reads=[src_buf, mrb], writes=[dst_buf])
        sb.op("vector", lambda e: e.tensor_mul(
            dst_t[:, :, dst_lo:dst_lo + n], dst_t[:, :, dst_lo:dst_lo + n], rb),
            reads=[mrb, dst_buf], writes=[dst_buf])

    def layernorm(src_t, src_buf, lo, n, eps, dst_t, dst_buf, dst_lo=None,
                  f32_dst=False, pre_done=False):
        ln_stats(src_t, src_buf, lo, n, eps, pre_done=pre_done)
        ln_finish(src_t, src_buf, lo, n, dst_t, dst_buf, dst_lo=dst_lo,
                  f32_dst=f32_dst)

    def proj_fm(nm_wt, src_t, src_buf, n, mlist, evac, banks=(4, 5, 6, 7),
                src_lo=0):
        per_chunk = src_t is UG
        """For each m in mlist: psum = Wt_m^T @ src[:, :, src_lo:src_lo+n]
        (accumulated over K chunks), then evac(m, c0, cw, bank)."""
        kdim = wts[nm_wt].shape[0]
        nblk = (n + NB - 1) // NB
        bi_rr = [0]
        for m in mlist:
            if kdim == C:
                slot, wbuf = wtile_small(nm_wt, m)
                halves = [(slot, wbuf, 0, NCH)]
            else:
                s0, b0 = wtile_large(nm_wt, m, 0)
                s1, b1 = wtile_large(nm_wt, m, 1)
                halves = [(s0, b0, 0, 16), (s1, b1, 16, 16)]
            for bi in range(nblk):
                c0, cw = bi * NB, min(NB, n - bi * NB)
                p = banks[bi_rr[0] % len(banks)]
                bi_rr[0] += 1
                nk = kdim // 128
                ki = 0
                for slot, wbuf, kofs, kn in halves:
                    for k in range(kn):
                        sbf_ = UGC[kofs + k] if per_chunk else src_buf
                        sb.op("tensor", lambda e, slot=slot, k=k, kofs=kofs,
                              c0=c0, cw=cw, p=p, ki=ki: e.matmul(
                                  PS[p][:, 0:cw], slot[:, k, :],
                                  src_t[:, kofs + k, src_lo + c0:src_lo + c0 + cw],
                                  start=(ki == 0), stop=(ki == nk - 1)),
                              reads=[sbf_, wbuf], writes=[psb[p]])
                        ki += 1
                evac(m, c0, cw, p)

    def evac_act(dst_t, dst_buf, nm_bias, func=None, dst_lo=0, scale=1.0):
        def ev(m, c0, cw, p):
            bias = bias_ap(nm_bias, m) if nm_bias else 0.0
            nonlocal func
            f = func if func is not None else (AF.Identity if nm_bias else AF.Copy)
            sb.op("scalar", lambda e, m=m, c0=c0, cw=cw, p=p, f=f, bias=bias:
                  e.activation(
                      dst_t[:, m, dst_lo + c0:dst_lo + c0 + cw], PS[p][:, 0:cw],
                      f, bias=bias, scale=scale),
                  reads=[psb[p], B("BIAS")], writes=[dst_buf])
        return ev

    dbg_outs = []

    def tap(name, sb_tensor, shape, buf):
        if not dbg:
            return
        d = nc.dram_tensor("dbg_" + name, shape, F32, kind="ExternalOutput").ap()
        rb = [B(f"XS_{i}") for i in range(G)] if sb_tensor is XS else [buf]
        sb.op("gpsimd", lambda e: e.dma_start(out=d, in_=sb_tensor[:, :, :]),
              reads=rb, inc=16)
        dbg_outs.append(name)

    def ev_stage_add_xs(nm_bias, xs_lo):
        def ev(m, c0, cw, p):
            sl = m % 2
            sb.op("scalar", lambda e, m=m, c0=c0, cw=cw, p=p, sl=sl: e.activation(
                RS32[:, sl, 0:cw], PS[p][:, 0:cw], AF.Identity,
                bias=bias_ap(nm_bias, m)),
                reads=[psb[p], B("BIAS")], writes=[RB("RS32")])
            sb.op("vector", lambda e, m=m, c0=c0, cw=cw, sl=sl: e.tensor_add(
                XS[:, m, xs_lo + c0:xs_lo + c0 + cw],
                XS[:, m, xs_lo + c0:xs_lo + c0 + cw], RS32[:, sl, 0:cw]),
                reads=[RB("RS32"), xsg(xs_lo)], writes=[xsg(xs_lo)])
        return ev

    def phase_A(s):
        RING_N[0] = 3
        for g in range(G):
            lo = g * GT
            src = x_d[s, :, lo:lo + GT].rearrange("(c p) t -> p c t", p=128)
            sb.op("sync", lambda e, src=src, lo=lo:
                  e.dma_start(out=XS[:, :, lo:lo + GT], in_=src),
                  writes=[xsg(lo)], inc=16)
        ln_pre(XS, xsg(0), 0, GT)
        for g in range(G):
            lo = g * GT
            layernorm(XS, xsg(lo), lo, GT, 1e-5, XS, xsg(lo), dst_lo=lo,
                      f32_dst=True, pre_done=True)
            if g + 1 < G:
                ln_pre(XS, xsg(lo + GT), lo + GT, GT)
        if dbg and s == 0:
            tap("t", XS, [128, NCH, T], B("XS"))
        sb.op("sync", lambda e: e.dma_start(
            out=BVRF[:, :], in_=bv_row_d),
            writes=[RB("BVRF")], inc=16)
        sb.op("vector", lambda e: e.tensor_copy(BVR[:, :], BVRF[:, :]),
              reads=[RB("BVRF")], writes=[RB("BVR")])
        ln_pre(XS, xsg(0), 0, GT)
        layernorm(XS, xsg(0), 0, GT, 1e-10, UG, RB("UG"), pre_done=True)
        for g in range(G):
            lo = g * GT
            if g + 1 < G:
                ln_pre(XS, xsg(lo + GT), lo + GT, GT)

            def ev_k(m, c0, cw, p, lo=lo, g=g):
                er = ([B("SPKYA"), B("SPKYB"), B("SPKM")]
                      if (g == 0 and m == 8) else [])
                sb.op("scalar", lambda e, m=m, c0=c0, cw=cw, p=p: e.activation(
                    KB[:, m - 8, lo + c0:lo + c0 + cw], PS[p][:, 0:cw],
                    AF.Identity, bias=bias_ap("b_attn", m)),
                    reads=[psb[p], B("BIAS")] + er, writes=[RB("KB")])
            proj_fm("wt_attn", UG, RB("UG"), GT, range(8, 16), ev_k)
            if g + 1 < G:
                ln_stats(XS, xsg(lo + GT), lo + GT, GT, 1e-10, pre_done=True)
            for nb in range(4):
                i = wr_i[0] % len(WR)
                wr_i[0] += 1
                vslot, vbuf = WR[i], RB(f"WR{i}")
                wpair_cache.clear()
                o, w = wcols[("wvs", nb)]
                sb.op("sync", lambda e, vslot=vslot, o=o, w=w: e.dma_start(
                    out=vslot.ap().rearrange("p a b -> p (a b)")[:, 0:w],
                    in_=wscr[:, o:o + w]),
                    reads=[B(("wvs", nb))], writes=[vbuf], inc=16)
                vv = vslot.ap().rearrange("p a b -> p (a b)")[:, 0:2048]
                vv = vv.rearrange("p (c f) -> p c f", c=NCH)
                for ti in range(4):
                    gti = 4 * g + ti
                    p = 4 + ti % 2
                    for k in range(NCH):
                        sb.op("tensor", lambda e, k=k, ti=ti, p=p, vv=vv: e.matmul(
                            PS[p][:, 0:256], UG[:, k, ti * 128:(ti + 1) * 128],
                            vv[:, k, :], start=(k == 0), stop=False),
                            reads=[RB("UG"), vbuf], writes=[psb[p]])
                    sb.op("tensor", lambda e, nb=nb, p=p: e.matmul(
                        PS[p][:, 0:256], ONESR[:, :], BVR[0:1, nb * 256:(nb + 1) * 256],
                        start=False, stop=True),
                        reads=[RB("BVR"), B("ONESR")], writes=[psb[p]])
                    sb.op("scalar", lambda e, gti=gti, nb=nb, p=p: e.activation(
                        VB[:, gti, nb * 256:(nb + 1) * 256], PS[p][:, 0:256], AF.Copy),
                        reads=[psb[p]], writes=[RB("VB")])
            if g + 1 < G:
                ln_finish(XS, xsg(lo + GT), lo + GT, GT, UG, RB("UG"))
        if dbg and s == 0:
            tap("k", KB, [128, NCH, T], RB("KB"))
            tap("v", VB, [128, 16, C], RB("VB"))
        for g in range(G):
            lo = g * GT
            cmax = 4 * g + 4
            layernorm(XS, xsg(lo), lo, GT, 1e-10, UG, RB("UG"))
            if dbg and s == 0 and g == 0:
                tap("ua2g0", UG, [128, NCH, GT], RB("UG"))
            pending_ep = [None]
            for hp in range(NCH):
                qs = hp % 2
                pv = 4 + hp % 2
                slot, wbuf = wtile_small("wt_attn", hp)
                for k in range(NCH):
                    sb.op("tensor", lambda e, slot=slot, k=k: e.matmul(
                        PS[7][:, :], slot[:, k, :], UG[:, k, 0:GT],
                        start=(k == 0), stop=(k == NCH - 1)),
                        reads=[RB("UG"), wbuf], writes=[psb[7]])
                sb.op("scalar", lambda e, hp=hp, qs=qs: e.activation(
                    QP[:, qs, :], PS[7][:, :], AF.Identity,
                    bias=bias_ap("b_attn", hp)),
                    reads=[psb[7], B("BIAS")], writes=[RB("QP")])
                if dbg and s == 0 and g == 0 and hp == 0:
                    tap("qp0", QP, [128, 2, NB], RB("QP"))

                def st_pair(c, hp=hp, qs=qs):
                    pe_ = 2 * (c % 2)
                    po_ = pe_ + 1
                    sb.op("tensor", lambda e, c=c, hp=hp, qs=qs, pe_=pe_: e.matmul(
                        PS[pe_][:, :], KB[0:64, hp, c * 128:(c + 1) * 128],
                        QP[0:64, qs, :], start=True, stop=True,
                        tile_position=(0, 0)),
                        reads=[RB("KB"), RB("QP")], writes=[psb[pe_]])
                    sb.op("tensor", lambda e, c=c, hp=hp, qs=qs, po_=po_: e.matmul(
                        PS[po_][:, :], KB[64:128, hp, c * 128:(c + 1) * 128],
                        QP[64:128, qs, :], start=True, stop=True,
                        tile_position=(64, 0)),
                        reads=[RB("KB"), RB("QP")], writes=[psb[po_]])

                def finish_c(c, hp=hp, pv=pv, cmax=cmax, g=g):
                    pe_ = 2 * (c % 2)
                    po_ = pe_ + 1
                    je, jo = pe_, po_
                    pair = PSA if c % 2 == 0 else PSB
                    sb.op("scalar", lambda e, pair=pair, je=je: e.activation(
                        PHAT[:, je:je + 2, :], pair[:, :, :], AF.Exp,
                        bias=ESHC[:, 0:1], scale=SC),
                        reads=[psb[pe_], psb[po_]], writes=[RB("PHAT")])
                    if c >= 4 * g:
                        k0 = 128 * c - 512 * g
                        ms = MASK[:, 384 - k0:384 - k0 + NB]
                        msb = ms.unsqueeze(1).broadcast_to([128, 2, NB])
                        sb.op("vector", lambda e, je=je, msb=msb: e.tensor_mul(
                            PHAT[:, je:je + 2, :], PHAT[:, je:je + 2, :], msb),
                            reads=[RB("PHAT"), B("MASK")], writes=[RB("PHAT")])
                    st = (c == 0)
                    sp = (c == cmax - 1)
                    sb.op("tensor", lambda e, c=c, hp=hp, je=je, st=st, sp=sp,
                          pv=pv: e.matmul(
                        PS[pv][0:64, :], VB[:, c, 128 * hp:128 * hp + 64],
                        PHAT[:, je, :], start=st, stop=sp, tile_position=(0, 0),
                        skip_group_check=True),
                        reads=[RB("VB"), RB("PHAT")], writes=[psb[pv]])
                    sb.op("tensor", lambda e, c=c, hp=hp, jo=jo, st=st, sp=sp,
                          pv=pv: e.matmul(
                        PS[pv][64:128, :], VB[:, c, 128 * hp + 64:128 * hp + 128],
                        PHAT[:, jo, :], start=st, stop=sp, tile_position=(0, 64),
                        skip_group_check=True),
                        reads=[RB("VB"), RB("PHAT")], writes=[psb[pv]])
                    sb.op("tensor", lambda e, je=je, st=st, sp=sp: e.matmul(
                        PS[6][0:1, :], ONESC[:, :], PHAT[:, je, :],
                        start=st, stop=sp, tile_position=(0, 0),
                        skip_group_check=True),
                        reads=[B("ONESC"), RB("PHAT")], writes=[psb[6]])
                    sb.op("tensor", lambda e, jo=jo, st=st, sp=sp: e.matmul(
                        PS[6][32:33, :], ONESC[:, :], PHAT[:, jo, :],
                        start=st, stop=sp, tile_position=(0, 32),
                        skip_group_check=True),
                        reads=[B("ONESC"), RB("PHAT")], writes=[psb[6]])

                st_pair(0)
                if pending_ep[0] is not None:
                    pending_ep[0]()
                    pending_ep[0] = None
                for c in range(cmax):
                    if c + 1 < cmax:
                        st_pair(c + 1)
                    finish_c(c)

                def epilogue(hp=hp, pv=pv):
                    sb.op("vector", lambda e: e.tensor_copy(
                        SRND[0:1, :], PS[6][0:1, :]),
                        reads=[psb[6]], writes=[RB("SRND")])
                    sb.op("vector", lambda e: e.tensor_copy(
                        SRND[32:33, :], PS[6][32:33, :]),
                        reads=[psb[6]], writes=[RB("SRND")])
                    sb.op("vector", lambda e: e.reciprocal(
                        SRND[0:33, :], SRND[0:33, :]),
                        reads=[RB("SRND")], writes=[RB("SRND")])
                    sb.op("vector", lambda e: e.tensor_copy(
                        SRNDB[0:33, :], SRND[0:33, :]),
                        reads=[RB("SRND")], writes=[RB("SRND")])
                    sb.op("vector", lambda e: e.tensor_copy(
                        SRNDB2[0:1, :], SRNDB[32:33, :]),
                        reads=[RB("SRND")], writes=[RB("SRND")])
                    sb.op("tensor", lambda e: e.matmul(
                        PS[7][0:64, :], ONESR[0:1, 0:64], SRNDB[0:1, :],
                        start=True, stop=True, tile_position=(0, 0)),
                        reads=[RB("SRND"), B("ONESR")], writes=[psb[7]])
                    sb.op("tensor", lambda e: e.matmul(
                        PS[7][64:128, :], ONESR[0:1, 0:64], SRNDB2[0:1, :],
                        start=True, stop=True, tile_position=(0, 64)),
                        reads=[RB("SRND"), B("ONESR")], writes=[psb[7]])
                    sb.op("scalar", lambda e: e.activation(
                        RSB[:, :], PS[7][:, :], AF.Copy),
                        reads=[psb[7]], writes=[B("RSB")])
                    sb.op("vector", lambda e, hp=hp, pv=pv: e.tensor_mul(
                        YG[:, hp, :], PS[pv][:, :], RSB[:, :]),
                        reads=[psb[pv], B("RSB")], writes=[RB("YG")])
                pending_ep[0] = epilogue
            pending_ep[0]()
            if dbg and s == 0 and g == 0:
                tap("yg0", YG, [128, NCH, GT], RB("YG"))
            proj_fm("wt_proj", YG, RB("YG"), GT, range(NCH),
                    ev_stage_add_xs("b_proj", lo))
        if dbg and s == 0:
            tap("t2", XS, [128, NCH, T], B("XS"))
        RING_N[0] = 4
        ln_pre(XS, xsg(0), 0, GTF)
        layernorm(XS, xsg(0), 0, GTF, 1e-10, UG, RB("UG"), pre_done=True)
        for gf in range(GF):
            lo = gf * GTF
            if gf + 1 < GF:
                ln_pre(XS, xsg(lo + GTF), lo + GTF, GTF)
            if dbg and s == 0 and gf == 0:
                tap("u2g0", UG, [128, NCH, GT], RB("UG"))
            proj_fm("wt_fc", UG, RB("UG"), GTF, range(FCH),
                    evac_act(HBF, RB("HBF"), "b_fc", func=AF.Gelu_apprx_tanh))
            if gf + 1 < GF:
                ln_stats(XS, xsg(lo + GTF), lo + GTF, GTF, 1e-10, pre_done=True)
            if dbg and s == 0 and gf == 0:
                tap("hg0", HBF, [128, FCH, GTF], RB("HBF"))
            proj_fm("wt_mlp", HBF, RB("HBF"), GTF, range(NCH),
                    ev_stage_add_xs("b_mlp", lo))
            if gf + 1 < GF:
                ln_finish(XS, xsg(lo + GTF), lo + GTF, GTF, UG, RB("UG"))
            if dbg and s == 0 and gf == 0:
                tap("t3g0", XS, [128, NCH, T], B("XS"))
            for c in range(NCH):
                src = x_d[s, c * 128:(c + 1) * 128, lo:lo + GTF]
                sl = c % 2
                sb.op("gpsimd", lambda e, src=src, sl=sl:
                      e.dma_start(out=XGC[:, sl, :], in_=src),
                      writes=[RB("XGC")], inc=16)
                sb.op("vector", lambda e, c=c, sl=sl, lo=lo: e.tensor_add(
                    XS[:, c, lo:lo + GTF], XS[:, c, lo:lo + GTF], XGC[:, sl, :]),
                    reads=[RB("XGC"), xsg(lo)], writes=[xsg(lo)])

    def phase_FFN(s, last):
        RING_N[0] = 4
        ln_pre(XS, xsg(0), 0, GTF)
        layernorm(XS, xsg(0), 0, GTF, 1e-5, UG, RB("UG"), pre_done=True)
        for gf in range(GF):
            lo = gf * GTF
            xsb = xsg(lo)
            if gf + 1 < GF:
                ln_pre(XS, xsg(lo + GTF), lo + GTF, GTF)

            def ev_relu2(m, c0, cw, p):
                sl = m % 2
                sb.op("scalar", lambda e, m=m, cw=cw, p=p, sl=sl: e.activation(
                    RS32[:, sl, 0:cw], PS[p][:, 0:cw], AF.Relu),
                    reads=[psb[p]], writes=[RB("RS32")])
                sb.op("vector", lambda e, m=m, cw=cw, sl=sl: e.tensor_mul(
                    HBF[:, m, 0:cw], RS32[:, sl, 0:cw], RS32[:, sl, 0:cw]),
                    reads=[RB("RS32")], writes=[RB("HBF")])
            proj_fm("wt_ffn_fc", UG, RB("UG"), GTF, range(FCH), ev_relu2)
            if gf + 1 < GF:
                ln_stats(XS, xsg(lo + GTF), lo + GTF, GTF, 1e-5, pre_done=True)

            SPKY = SPKYA if gf % 2 == 0 else SPKYB
            yb = RB(SPKY.name)
            extra0 = [B("KB")] if gf == 0 else []

            def ev_y(m, c0, cw, p):
                er = extra0 if m == 0 else []
                sb.op("scalar", lambda e, m=m, cw=cw, p=p, SPKY=SPKY: e.activation(
                    SPKY[:, m, 0:cw], PS[p][:, 0:cw], AF.Copy),
                    reads=[psb[p]] + er, writes=[yb])
            proj_fm("wt_ffn_proj", HBF, RB("HBF"), GTF, range(NCH), ev_y)
            if gf + 1 < GF:
                ln_finish(XS, xsg(lo + GTF), lo + GTF, GTF, UG, RB("UG"))
            mb, sbb = B("SPKM"), RB("SBF")
            Y, M_, S_ = SPKY[:, :, :], SPKM[:, :, :], SBF[:, :, :]
            XSg = XS[:, :, lo:lo + GTF]

            def gt(src, src_b):
                sb.op("vector", lambda e, S_=S_, src=src: e.tensor_single_scalar(
                    S_, src, 1.0, ALU.is_gt), reads=[src_b], writes=[sbb])

            def acc_xs(XSg=XSg, S_=S_):
                sb.op("vector", lambda e, XSg=XSg, S_=S_: e.scalar_tensor_tensor(
                    XSg, S_, 0.2, XSg, ALU.mult, ALU.add),
                    reads=[sbb, xsb], writes=[xsb])

            gt(Y, yb)
            acc_xs()
            sb.op("vector", lambda e, M_=M_, Y=Y: e.scalar_tensor_tensor(
                M_, Y, 0.9, Y, ALU.mult, ALU.add), reads=[yb] + extra0, writes=[mb])
            sb.op("vector", lambda e, M_=M_, S_=S_: e.tensor_sub(M_, M_, S_),
                  reads=[sbb, mb], writes=[mb])
            for _t in range(3):
                gt(M_, mb)
                acc_xs()
                sb.op("vector", lambda e, M_=M_, Y=Y: e.scalar_tensor_tensor(
                    M_, M_, 0.9, Y, ALU.mult, ALU.add),
                    reads=[yb, mb], writes=[mb])
                sb.op("vector", lambda e, M_=M_, S_=S_: e.tensor_sub(M_, M_, S_),
                      reads=[sbb, mb], writes=[mb])
            gt(M_, mb)
            acc_xs()
            if last:
                dst = y_d[s, :, lo:lo + GTF].rearrange("(c p) t -> p c t", p=128)
                sb.op("gpsimd", lambda e, dst=dst, lo=lo: e.dma_start(
                    out=dst, in_=XS[:, :, lo:lo + GTF]), reads=[xsb], inc=16)

    def phase_C(s):
        RING_N[0] = 3
        for g in range(G):
            lo = g * GT
            layernorm(XS, xsg(lo), lo, GT, 1e-5, UG, RB("UG"))
            proj_fm("wt_p", UG, RB("UG"), GT, range(NCH),
                    evac_act(XBG, RB("XBG"), "b_p"))

            def ev_into(dst_t, dst_b, nmb, lo=lo, g=g):
                def ev(m, c0, cw, p):
                    er = ([B("SPKYA"), B("SPKYB"), B("SPKM")]
                          if (g == 0 and m == 0) else [])
                    sb.op("scalar", lambda e, m=m, c0=c0, cw=cw, p=p: e.activation(
                        dst_t[:, m, lo + c0:lo + c0 + cw], PS[p][:, 0:cw],
                        AF.Identity, bias=bias_ap(nmb, m)),
                        reads=[psb[p], B("BIAS")] + er, writes=[dst_b])
                return ev
            proj_fm("wt_k", XBG, RB("XBG"), GT, range(NCH),
                    ev_into(KB, RB("KB"), "b_k"))
            proj_fm("wt_v", XBG, RB("XBG"), GT, range(NCH),
                    ev_into(VC, RB("VB"), "b_v"))
        for g in range(G):
            lo = g * GT
            layernorm(XS, xsg(lo), lo, GT, 1e-5, UG, RB("UG"))
            proj_fm("wt_p", UG, RB("UG"), GT, range(NCH),
                    evac_act(XBG, RB("XBG"), "b_p"))
            proj_fm("wt_q", XBG, RB("XBG"), GT, range(NCH),
                    evac_act(YG, RB("YG"), "b_q"))
            for n_idx, dlt in ((0, -1), (1, 0), (2, 1)):
                for c in range(NCH):
                    jj = c % 2
                    lo_s = lo + dlt
                    if 0 <= lo_s and lo_s + GT <= T:
                        sb.op("vector", lambda e, c=c, jj=jj, lo_s=lo_s: e.tensor_mul(
                            PHAT[:, jj, :], YG[:, c, :], KB[:, c, lo_s:lo_s + GT]),
                            reads=[RB("YG"), RB("KB")], writes=[RB("PHAT")])
                    elif lo_s < 0:
                        sb.op("vector", lambda e, c=c, jj=jj: e.tensor_mul(
                            PHAT[:, jj, 1:GT], YG[:, c, 1:GT], KB[:, c, 0:GT - 1]),
                            reads=[RB("YG"), RB("KB")], writes=[RB("PHAT")])
                        sb.op("vector", lambda e, c=c, jj=jj: e.tensor_mul(
                            PHAT[:, jj, 0:1], YG[:, c, 0:1], KB[:, c, T - 1:T]),
                            reads=[RB("YG"), RB("KB")], writes=[RB("PHAT")])
                    else:
                        sb.op("vector", lambda e, c=c, jj=jj, lo_s=lo_s: e.tensor_mul(
                            PHAT[:, jj, 0:GT - 1], YG[:, c, 0:GT - 1],
                            KB[:, c, lo_s:T]),
                            reads=[RB("YG"), RB("KB")], writes=[RB("PHAT")])
                        sb.op("vector", lambda e, c=c, jj=jj: e.tensor_mul(
                            PHAT[:, jj, GT - 1:GT], YG[:, c, GT - 1:GT],
                            KB[:, c, 0:1]),
                            reads=[RB("YG"), RB("KB")], writes=[RB("PHAT")])
                    sb.op("tensor", lambda e, c=c, jj=jj, n_idx=n_idx: e.matmul(
                        PS[n_idx][0:16, :], EIND[:, c, :], PHAT[:, jj, :],
                        start=(c == 0), stop=(c == NCH - 1)),
                        reads=[RB("PHAT"), B("EIND")], writes=[psb[n_idx]])
            ABSn = (ABS0, ABS1, ABS2)
            for n in range(3):
                sb.op("vector", lambda e, n=n: e.tensor_copy(
                    ABSn[n][:, :], PS[n][0:16, :]),
                    reads=[psb[n]], writes=[RB("ABS0")])
            sb.op("vector", lambda e: e.tensor_max(ABSM[:, :], ABS0[:, :], ABS1[:, :]),
                  reads=[RB("ABS0")], writes=[RB("ABSM")])
            sb.op("vector", lambda e: e.tensor_max(ABSM[:, :], ABSM[:, :], ABS2[:, :]),
                  reads=[RB("ABS0")], writes=[RB("ABSM")])
            for n in range(3):
                sb.op("vector", lambda e, n=n: e.tensor_sub(
                    ABSn[n][:, :], ABSn[n][:, :], ABSM[:, :]),
                    reads=[RB("ABS0"), RB("ABSM")], writes=[RB("ABS0")])
                sb.op("scalar", lambda e, n=n: e.activation(
                    ABSn[n][:, :], ABSn[n][:, :], AF.Exp, scale=SC),
                    reads=[RB("ABS0")], writes=[RB("ABS0")])
            sb.op("vector", lambda e: e.tensor_add(ABSM[:, :], ABS0[:, :], ABS1[:, :]),
                  reads=[RB("ABS0")], writes=[RB("ABSM")])
            sb.op("vector", lambda e: e.tensor_add(ABSM[:, :], ABSM[:, :], ABS2[:, :]),
                  reads=[RB("ABS0")], writes=[RB("ABSM")])
            sb.op("vector", lambda e: e.reciprocal(
                RS32[0:16, 0, :], ABSM[:, :]),
                reads=[RB("ABSM")], writes=[RB("RS32")])
            for n in range(3):
                sb.op("vector", lambda e, n=n: e.tensor_mul(
                    WROW[n][:, :], ABSn[n][:, :], RS32[0:16, 0, :]),
                    reads=[RB("ABS0"), RB("RS32")], writes=[RB("WROW0")])
            for c in range(NCH):
                for n, dlt in ((0, -1), (1, 0), (2, 1)):
                    wslot = 2 + n % 2
                    sb.op("tensor", lambda e, c=c, n=n, wslot=wslot: e.matmul(
                        PS[3][:, :], FIND[:, c, :], WROW[n][:, :],
                        start=True, stop=True),
                        reads=[RB("WROW0"), B("FIND")], writes=[psb[3]])
                    sb.op("scalar", lambda e, wslot=wslot: e.activation(
                        PHAT[:, wslot, :], PS[3][:, :], AF.Copy),
                        reads=[psb[3]], writes=[RB("PHAT")])
                    lo_s = lo + dlt
                    dsti = 0 if n == 0 else 1
                    dst = RS32[:, dsti, :]

                    def ml(out_ap, w_ap, v_ap, extra_reads=()):
                        sb.op("vector", lambda e, out_ap=out_ap, w_ap=w_ap,
                              v_ap=v_ap: e.tensor_mul(out_ap, w_ap, v_ap),
                              reads=[RB("PHAT"), RB("VB")], writes=[RB("RS32")])
                    if 0 <= lo_s and lo_s + GT <= T:
                        ml(dst, PHAT[:, wslot, :], VC[:, c, lo_s:lo_s + GT])
                    elif lo_s < 0:
                        ml(RS32[:, dsti, 1:GT], PHAT[:, wslot, 1:GT],
                           VC[:, c, 0:GT - 1])
                        ml(RS32[:, dsti, 0:1], PHAT[:, wslot, 0:1],
                           VC[:, c, T - 1:T])
                    else:
                        ml(RS32[:, dsti, 0:GT - 1], PHAT[:, wslot, 0:GT - 1],
                           VC[:, c, lo_s:T])
                        ml(RS32[:, dsti, GT - 1:GT], PHAT[:, wslot, GT - 1:GT],
                           VC[:, c, 0:1])
                    if n == 1:
                        sb.op("vector", lambda e: e.tensor_add(
                            RS32[:, 0, :], RS32[:, 0, :], RS32[:, 1, :]),
                            reads=[RB("RS32")], writes=[RB("RS32")])
                    elif n == 2:
                        sb.op("vector", lambda e, c=c: e.tensor_add(
                            UG[:, c, :], RS32[:, 0, :], RS32[:, 1, :]),
                            reads=[RB("RS32")], writes=[RB("UG")])
            proj_fm("wt_o", UG, RB("UG"), GT, range(NCH),
                    ev_stage_add_xs("b_o", lo))

    import os as _os
    order = _os.environ.get("MK_PHASES", "ABCD")
    for s in range(SEQS):
        for ph in order:
            if ph == "A":
                phase_A(s)
                if s == 0:
                    tap("xs_A", XS, [128, NCH, T], B("XS"))
            elif ph == "B":
                phase_FFN(s, last=False)
                if s == 0:
                    tap("xs_B", XS, [128, NCH, T], B("XS"))
            elif ph == "C":
                phase_C(s)
                if s == 0:
                    tap("xs_C", XS, [128, NCH, T], B("XS"))
            elif ph == "D":
                phase_FFN(s, last=True)
                if s == 0:
                    tap("xs_D", XS, [128, NCH, T], B("XS"))
    sb.plan["sync"].append(Wait("sync", sb.cnt["sync"]))
    sb.plan["sync"].append(Wait("gpsimd", sb.cnt["gpsimd"]))

    sems = {e: nc.alloc_semaphore(f"sem_{e}") for e in ENGS}
    nops = {e: sum(1 for it in sb.plan[e] if isinstance(it, Op)) for e in ENGS}
    print("plan ops:", nops, "waits:",
          {e: len(sb.plan[e]) - nops[e] for e in ENGS})

    import os as _os2
    _imap = {}

    def replay(name):
        def run(eng):
            for idx, it in enumerate(sb.plan[name]):
                if isinstance(it, Wait):
                    ins = eng.wait_ge(sems[it.src], it.val)
                    if _os2.environ.get("MK_IMAP"):
                        _imap[ins.ins.name] = (name, idx, f"WAIT {it.src}>={it.val}")
                else:
                    ins = it.fn(eng)
                    ins.then_inc(sems[name], it.inc)
                    if _os2.environ.get("MK_IMAP"):
                        _imap[ins.ins.name] = (name, idx, "op")
        return run

    with nc.Block() as block:
        block.tensor(replay("tensor"))
        block.vector(replay("vector"))
        block.scalar(replay("scalar"))
        block.gpsimd(replay("gpsimd"))
        block.sync(replay("sync"))
    nc._mk_imap = _imap
    nc._mk_plan = sb.plan
    return nc


_NC_CACHE = {}


def kernel(**inputs):
    def tr(a):
        return np.ascontiguousarray(np.asarray(a, dtype=np.float32).T)

    def vec(a):
        a = np.asarray(a, dtype=np.float32)
        return np.ascontiguousarray(a.reshape(-1, 128).T)

    x = np.asarray(inputs["L0_attn_a_x"], dtype=np.float32)
    xT = np.ascontiguousarray(np.swapaxes(x, 1, 2))
    common = {
        "wt_attn": tr(inputs["L0_attn_a_w_attn"]),
        "wt_proj": tr(inputs["L0_attn_a_w_proj"]),
        "wt_fc": tr(inputs["L0_attn_a_w_fc"]),
        "wt_mlp": tr(inputs["L0_attn_a_w_mlp_proj"]),
        "wt_p": tr(inputs["L0_attn_b_Wp"]),
        "wt_q": tr(inputs["L0_attn_b_Wq"]),
        "wt_k": tr(inputs["L0_attn_b_Wk"]),
        "wt_v": tr(inputs["L0_attn_b_Wv"]),
        "wt_o": tr(inputs["L0_attn_b_Wo"]),
        "wt_ffn_fc": tr(inputs["L0_ffn_W_fc"]),
        "wt_ffn_proj": tr(inputs["L0_ffn_W_proj"]),
        "b_attn": vec(inputs["L0_attn_a_b_attn"]),
        "b_proj": vec(inputs["L0_attn_a_b_proj"]),
        "b_fc": vec(inputs["L0_attn_a_b_fc"]),
        "b_mlp": vec(inputs["L0_attn_a_b_mlp_proj"]),
        "b_p": vec(inputs["L0_attn_b_bp"]),
        "b_q": vec(inputs["L0_attn_b_bq"]),
        "b_k": vec(inputs["L0_attn_b_bk"]),
        "b_v": vec(inputs["L0_attn_b_bv"]),
        "b_o": vec(inputs["L0_attn_b_bo"]),
        "b_attn_v": np.ascontiguousarray(
            np.asarray(inputs["L0_attn_a_b_attn"], dtype=np.float32)[2048:].reshape(1, -1)),
    }
    if "nc" not in _NC_CACHE:
        _NC_CACHE["nc"] = build_program()
    nc = _NC_CACHE["nc"]
    in_maps = [dict(common, x=np.ascontiguousarray(xT[2 * c:2 * c + 2]))
               for c in range(NCORES)]
    res = run_bass_kernel_spmd(nc, in_maps, core_ids=list(range(NCORES)))
    yT = np.concatenate([np.asarray(r["y"], dtype=np.float32)
                         for r in res.results], axis=0)
    return np.ascontiguousarray(np.swapaxes(yT, 1, 2))


# version 25
# speedup vs baseline: 1.0198x; 1.0198x over previous
import numpy as np
import ml_dtypes
import concourse.bass as bass
import concourse.mybir as mybir
from concourse.bass_utils import run_bass_kernel_spmd

F32 = mybir.dt.float32
BF16 = mybir.dt.bfloat16
AF = mybir.ActivationFunctionType
ALU = mybir.AluOpType
BF = ml_dtypes.bfloat16

NCORES = 8
SEQS = 2
T = 2048
C = 1024
NCH = 8
H = 16
DH = 64
FC = 4096
FCH = 32
G = 4
GT = 512
GF = 4
GTF = 512
NB = 512
ESH = -24.0
SC = 0.125

ENGS = ("tensor", "vector", "scalar", "gpsimd", "sync")


class Buf:
    __slots__ = ("name", "writer", "readers")

    def __init__(self, name):
        self.name = name
        self.writer = None
        self.readers = []


class Op:
    __slots__ = ("fn", "inc", "line")

    def __init__(self, fn, inc):
        self.fn = fn
        self.inc = inc
        import sys as _s
        f = _s._getframe(2)
        self.line = f.f_lineno


class Wait:
    __slots__ = ("src", "val")

    def __init__(self, src, val):
        self.src = src
        self.val = val


class SB:

    def __init__(self):
        self.plan = {e: [] for e in ENGS}
        self.cnt = {e: 0 for e in ENGS}
        self.waited = {e: {} for e in ENGS}

    def op(self, eng, fn, reads=(), writes=(), inc=1):
        need = {}
        dma_q = eng in ("sync", "gpsimd")

        def add(pe, pv):
            if (pe != eng or dma_q) and pv > need.get(pe, 0):
                need[pe] = pv

        for b in reads:
            if b.writer is not None:
                add(*b.writer)
        for b in writes:
            if b.writer is not None:
                add(*b.writer)
            for r in b.readers:
                add(*r)
        for pe in sorted(need):
            pv = need[pe]
            if self.waited[eng].get(pe, 0) < pv:
                self.plan[eng].append(Wait(pe, pv))
                self.waited[eng][pe] = pv
        self.plan[eng].append(Op(fn, inc))
        self.cnt[eng] += inc
        newv = self.cnt[eng]
        for b in writes:
            b.writer = (eng, newv)
            b.readers = []
        for b in reads:
            b.readers.append((eng, newv))


def build_program(dbg=False):
    nc = bass.Bass("TRN2", target_bir_lowering=False, debug=False)

    x_d = nc.dram_tensor("x", [SEQS, C, T], F32, kind="ExternalInput").ap()
    y_d = nc.dram_tensor("y", [SEQS, C, T], F32, kind="ExternalOutput").ap()
    wts = {}
    for nm, k, f in [("wt_attn", C, 3 * C), ("wt_proj", C, C), ("wt_fc", C, FC),
                     ("wt_mlp", FC, C), ("wt_p", C, C), ("wt_q", C, C),
                     ("wt_k", C, C), ("wt_v", C, C), ("wt_o", C, C),
                     ("wt_ffn_fc", C, FC), ("wt_ffn_proj", FC, C)]:
        wts[nm] = nc.dram_tensor(nm, [k, f], F32, kind="ExternalInput").ap()
    wcols = {}
    woff = [0]

    def walloc(key, width):
        wcols[key] = (woff[0], width)
        woff[0] += width
    for _nm, _kd, _fd in [("wt_attn", C, 2 * C), ("wt_proj", C, C),
                          ("wt_fc", C, FC), ("wt_p", C, C), ("wt_q", C, C),
                          ("wt_k", C, C), ("wt_v", C, C), ("wt_o", C, C),
                          ("wt_ffn_fc", C, FC)]:
        for _m in range(_fd // 128):
            walloc((_nm, _m), (_kd // 128) * 128)
    for _nm in ("wt_mlp", "wt_ffn_proj"):
        for _m in range(NCH):
            for _h in range(2):
                walloc((_nm, _m, _h), 16 * 128)
    for _h in range(4):
        walloc(("wvs", _h), NCH * 256)
    wscr = nc.dram_tensor("wscr", [128, woff[0]], BF16, kind="Internal").ap()

    bss = {}
    for nm, f in [("b_attn", 3 * C), ("b_proj", C), ("b_fc", FC), ("b_mlp", C),
                  ("b_p", C), ("b_q", C), ("b_k", C), ("b_v", C), ("b_o", C)]:
        bss[nm] = nc.dram_tensor(nm, [128, f // 128], F32,
                                 kind="ExternalInput").ap()
    bv_row_d = nc.dram_tensor("b_attn_v", [1, C], F32, kind="ExternalInput").ap()

    ii = np.arange(896)[None, :]
    pp = np.arange(128)[:, None]
    mask_np = (ii >= pp + 384).astype(BF)
    e_np = np.zeros((128, NCH, 16), dtype=BF)
    f_np = np.zeros((16, NCH, 128), dtype=BF)
    for c in range(NCH):
        for p in range(128):
            h = 2 * c + p // 64
            e_np[p, c, h] = 1
            f_np[h, c, p] = 1
    ones_col_d = nc.inline_tensor(np.ones((128, 1), dtype=BF), name="d_onc").ap()
    esh_d = nc.inline_tensor(np.full((128, 1), ESH, dtype=np.float32),
                             name="d_esh").ap()
    eps5_d = nc.inline_tensor(np.full((128, 1), 1e-5, dtype=np.float32),
                              name="d_eps5").ap()
    eps10_d = nc.inline_tensor(np.full((128, 1), 1e-10, dtype=np.float32),
                               name="d_eps10").ap()
    ones_row_d = nc.inline_tensor(np.ones((1, 128), dtype=BF), name="d_onr").ap()
    mask_d = nc.inline_tensor(np.ascontiguousarray(mask_np), name="d_mask").ap()
    e_d = nc.inline_tensor(np.ascontiguousarray(e_np), name="d_eind").ap()
    f_d = nc.inline_tensor(np.ascontiguousarray(f_np), name="d_find").ap()

    K = 1024
    SB_BASE = 16512
    A = nc.alloc_sbuf_tensor_at

    def at(name, shape, dtype, off):
        return A(name, shape, dtype, offset=SB_BASE + off)

    XS = at("XS", [128, NCH, T], F32, 0)
    KB = at("KB", [128, NCH, T], BF16, 64 * K)
    SPKYA = at("SPKYA", [128, NCH, GTF], BF16, 64 * K)
    SPKYB = at("SPKYB", [128, NCH, GTF], BF16, 64 * K + 8 * K)
    SPKM = at("SPKM", [128, NCH, GTF], F32, 64 * K + 16 * K)
    VB = at("VB", [128, 16, C], BF16, 96 * K)
    HBF = at("HBF", [128, FCH, GTF], BF16, 96 * K)
    VC = at("VC", [128, NCH, T], BF16, 96 * K)
    UG = at("UG", [128, NCH, GT], BF16, 130 * K)
    YG = at("YG", [128, NCH, GT], BF16, 138 * K)
    XBG = at("XBG", [128, NCH, GT], BF16, 146 * K)
    WROW = [at(f"WROW{i}", [16, GT], BF16, 146 * K + 5 * K + i * K)
            for i in range(3)]
    SQG = at("SQG", [128, NCH, GT], BF16, 154 * K)
    SRND = at("SRND", [33, NB], F32, 154 * K)
    SRNDB = at("SRNDB", [33, NB], BF16, 154 * K + 2 * K)
    SRNDB2 = at("SRNDB2", [1, NB], BF16, 154 * K + 3 * K)
    ABS0 = at("ABS0", [16, GT], F32, 154 * K)
    ABS1 = at("ABS1", [16, GT], F32, 154 * K + 2 * K)
    ABS2 = at("ABS2", [16, GT], F32, 154 * K + 4 * K)
    ABSM = at("ABSM", [16, GT], F32, 154 * K + 6 * K)
    QP = at("QP", [128, 2, NB], BF16, 162 * K)
    BVR = at("BVR", [1, C], BF16, 162 * K)
    PHAT = at("PHAT", [128, 4, NB], BF16, 164 * K)
    RS32 = at("RS32", [128, 2, NB], F32, 168 * K)
    BVRF = at("BVRF", [1, C], F32, 168 * K)
    RSB = at("RSB", [128, NB], F32, 172 * K)
    MRB = at("MRB", [128, 2, GT], F32, 174 * K)
    ST_S = at("ST_S", [1, GT], F32, 178 * K)
    ST_Q = at("ST_Q", [1, GT], F32, 180 * K)
    ST_R = at("ST_R", [1, GT], F32, 182 * K)
    ST_T = at("ST_T", [1, GT], F32, 154 * K)
    STB_M = at("STB_M", [1, GT], BF16, 184 * K)
    STB_R = at("STB_R", [1, GT], BF16, 185 * K)
    SBF = at("SBF", [128, NCH, GTF], BF16, 138 * K)
    XGC = at("XGC", [128, 2, GTF], F32, 186 * K)
    PSTG = at("PSTG", [128, 2, NCH, 128], BF16, 186 * K)
    WR = [at(f"WR{i}", [128, 16, 128], BF16, 190 * K + 4 * K * i) for i in range(3)]
    WR3 = at("WR3", [128, 16, 128], BF16, 164 * K)
    cb = 202 * K
    ONESC = at("ONESC", [128, 1], BF16, cb); cb += 32
    ONESR = at("ONESR", [1, 128], BF16, cb); cb += 256
    EIND = at("EIND", [128, NCH, 16], BF16, cb); cb += 256
    BIAS = at("BIAS", [128, 112], F32, cb); cb += 448
    MASK = at("MASK", [128, 896], BF16, cb); cb += 1792
    ESHC = at("ESHC", [128, 1], F32, cb); cb += 32
    EPS5C = at("EPS5C", [128, 1], F32, cb); cb += 32
    EPS10C = at("EPS10C", [128, 1], F32, cb); cb += 32
    FIND = at("FIND", [16, NCH, 128], BF16, cb); cb += 2048
    assert cb <= 208 * K - 128, cb

    bias_off = {}
    off = 0
    for nm, f in [("b_attn", 3 * C), ("b_proj", C), ("b_fc", FC), ("b_mlp", C),
                  ("b_p", C), ("b_q", C), ("b_k", C), ("b_v", C), ("b_o", C)]:
        bias_off[nm] = off
        off += f // 128

    PSA = nc.alloc_psum_tensor("PSA", [128, 2, NB], F32)
    PSB = nc.alloc_psum_tensor("PSB", [128, 2, NB], F32)
    PS47 = [nc.alloc_psum_tensor(f"PS{i}", [128, NB], F32) for i in range(4, 8)]
    PS = [PSA[:, 0, :], PSA[:, 1, :], PSB[:, 0, :], PSB[:, 1, :]] + \
         [t[:, :] for t in PS47]

    sb = SB()
    bufs = {}

    def B(name):
        if name not in bufs:
            bufs[name] = Buf(name)
        return bufs[name]

    alias = { "HBF": "VB", "VC": "VB",
             "WROW0": "XBG", "WROW1": "XBG", "WROW2": "XBG",
             "SRND": "SQG", "SRNDB": "SQG", "SRNDB2": "SQG", "ST_T": "SQG",
             "ABS0": "SQG", "ABS1": "SQG", "ABS2": "SQG", "ABSM": "SQG",
             "BVR": "QP", "SBF": "YG", "PSTG": "XGC", "BVRF": "RS32",
             "WR3": "PHAT"}

    def RB(name):
        return B(alias.get(name, name))

    psb = [B(f"ps{i}") for i in range(8)]
    XSB = [B(f"XS_{i}") for i in range(G)]
    UGC = [B(f"UG_{c}") for c in range(NCH)]

    def xsg(lo):
        return XSB[lo // GT]

    for dst, src in [(ONESC, ones_col_d), (ONESR, ones_row_d), (MASK, mask_d),
                     (EIND, e_d), (FIND, f_d), (ESHC, esh_d),
                     (EPS5C, eps5_d), (EPS10C, eps10_d)]:
        sb.op("sync", lambda e, dst=dst, src=src: e.dma_start(out=dst[:], in_=src),
              writes=[B(dst.name)], inc=16)
    for nm in bias_off:
        o, w = bias_off[nm], bss[nm].shape[1]
        src = bss[nm]
        sb.op("sync", lambda e, o=o, w=w, src=src:
              e.dma_start(out=BIAS[:, o:o + w], in_=src),
              writes=[B("BIAS")], inc=16)

    def fill_order():
        yield ("wt_attn", C, 2 * C)
        yield ("wvs", None, None)
        yield ("wt_proj", C, C)
        yield ("wt_fc", C, FC)
        yield ("wt_mlp", None, None)
        yield ("wt_ffn_fc", C, FC)
        yield ("wt_ffn_proj", None, None)
        yield ("wt_p", C, C)
        yield ("wt_q", C, C)
        yield ("wt_k", C, C)
        yield ("wt_v", C, C)
        yield ("wt_o", C, C)

    fill_i = [0]
    pending_store = [None]

    def fill_tile(load_fn, store_fn):
        load_fn()
        if pending_store[0] is not None:
            pending_store[0]()
        pending_store[0] = store_fn

    for nm, kd, fd in fill_order():
        if nm == "wvs":
            for h in range(4):
                i = fill_i[0] % len(WR)
                fill_i[0] += 1
                slot, wbuf = WR[i], RB(f"WR{i}")
                o, w = wcols[("wvs", h)]
                src = wts["wt_attn"][:, 2 * C + h * 256:2 * C + (h + 1) * 256]
                src = src.rearrange("(c p) f -> p c f", p=128)
                fill_tile(
                    lambda slot=slot, src=src, wbuf=wbuf: sb.op(
                        "gpsimd", lambda e: e.dma_start(
                            out=slot.ap().rearrange("p a b -> p (a b)")
                            .rearrange("p (c f) -> p c f", c=NCH)[:, :, :], in_=src),
                        writes=[wbuf], inc=16),
                    lambda slot=slot, o=o, w=w, wbuf=wbuf, h=h: sb.op(
                        "gpsimd", lambda e: e.dma_start(
                            out=wscr[:, o:o + w],
                            in_=slot.ap().rearrange("p a b -> p (a b)")[:, 0:w]),
                        reads=[wbuf], writes=[B(("wvs", h))], inc=16))
            continue
        if nm in ("wt_mlp", "wt_ffn_proj"):
            for m in range(NCH):
                for h in range(2):
                    for hh in range(2):
                        i = fill_i[0] % 2
                        fill_i[0] += 1
                        stg = PSTG[:, i, :, :]
                        sbuf_ = RB("PSTG")
                        r0 = h * 2048 + hh * 1024
                        src = wts[nm][r0:r0 + 1024, m * 128:(m + 1) * 128]
                        src = src.rearrange("(c p) f -> p c f", p=128)
                        o, w = wcols[(nm, m, h)]
                        oo = o + hh * 1024
                        fill_tile(
                            lambda stg=stg, src=src, sbuf_=sbuf_: sb.op(
                                "gpsimd", lambda e: e.dma_start(
                                    out=stg, in_=src),
                                writes=[sbuf_], inc=16),
                            lambda stg=stg, oo=oo, sbuf_=sbuf_, nm=nm, m=m, h=h:
                            sb.op("gpsimd", lambda e: e.dma_start(
                                out=wscr[:, oo:oo + 1024],
                                in_=stg.rearrange("p a b -> p (a b)")),
                                reads=[sbuf_], writes=[B((nm, m, h))], inc=16))
            continue
        use_ring = nm == "wt_attn"
        for m in range(fd // 128):
            src = wts[nm][:, m * 128:(m + 1) * 128]
            src = src.rearrange("(c p) f -> p c f", p=128)
            o, w = wcols[(nm, m)]
            if use_ring:
                i = fill_i[0] % len(WR)
                fill_i[0] += 1
                slot, wbuf = WR[i], RB(f"WR{i}")
                fill_tile(
                    lambda slot=slot, src=src, wbuf=wbuf: sb.op(
                        "gpsimd", lambda e: e.dma_start(
                            out=slot[:, 0:NCH, :], in_=src), writes=[wbuf], inc=16),
                    lambda slot=slot, o=o, w=w, wbuf=wbuf, nm=nm, m=m: sb.op(
                        "gpsimd", lambda e: e.dma_start(
                            out=wscr[:, o:o + w],
                            in_=slot.ap().rearrange("p a b -> p (a b)")[:, 0:w]),
                        reads=[wbuf], writes=[B((nm, m))], inc=16))
            else:
                i = fill_i[0] % 2
                fill_i[0] += 1
                stg = PSTG[:, i, :, :]
                sbuf_ = RB("PSTG")
                fill_tile(
                    lambda stg=stg, src=src, sbuf_=sbuf_: sb.op(
                        "gpsimd", lambda e: e.dma_start(out=stg, in_=src),
                        writes=[sbuf_], inc=16),
                    lambda stg=stg, o=o, w=w, sbuf_=sbuf_, nm=nm, m=m: sb.op(
                        "gpsimd", lambda e: e.dma_start(
                            out=wscr[:, o:o + w],
                            in_=stg.rearrange("p a b -> p (a b)")),
                        reads=[sbuf_], writes=[B((nm, m))], inc=16))
    if pending_store[0] is not None:
        pending_store[0]()
        pending_store[0] = None

    def bias_ap(nm, chunk):
        return BIAS[:, bias_off[nm] + chunk:bias_off[nm] + chunk + 1]

    wr_i = [0]
    RING_N = [3]
    wpair_cache = {}

    def ring_slot():
        i = wr_i[0] % RING_N[0]
        wr_i[0] += 1
        slots = WR + [WR3]
        return slots[i], RB(f"WR{i}" if i < 3 else "WR3")

    def wtile_pair(nm, m0):
        key = (nm, m0)
        if key in wpair_cache:
            return wpair_cache[key]
        slot, wbuf = ring_slot()
        o, w = wcols[(nm, m0)]
        o2, w2 = wcols[(nm, m0 + 1)]
        assert o2 == o + w
        eng = "gpsimd" if (RING_N[0] == 4 and (wr_i[0] & 1)) else "sync"
        sb.op(eng, lambda e, slot=slot, o=o, w=w, w2=w2: e.dma_start(
            out=slot.ap().rearrange("p a b -> p (a b)")[:, 0:w + w2],
            in_=wscr[:, o:o + w + w2]),
            reads=[B((nm, m0)), B((nm, m0 + 1))], writes=[wbuf], inc=16)
        wpair_cache.clear()
        wpair_cache[key] = (slot, wbuf)
        return slot, wbuf

    def wtile_small(nm, m):
        m0 = m - (m % 2)
        slot, wbuf = wtile_pair(nm, m0)
        sub = slot.ap()[:, 8 * (m % 2):8 * (m % 2) + NCH, :]
        return sub, wbuf

    def wtile_large(nm, m, half):
        slot, wbuf = ring_slot()
        o, w = wcols[(nm, m, half)]
        eng = "gpsimd" if (RING_N[0] == 4 and half == 1) else "sync"
        sb.op(eng, lambda e, slot=slot, o=o, w=w: e.dma_start(
            out=slot.ap().rearrange("p a b -> p (a b)"), in_=wscr[:, o:o + w]),
            reads=[B((nm, m, half))], writes=[wbuf], inc=16)
        wpair_cache.clear()
        return slot, wbuf

    def newton_rsqrt(src, dst, tmp, rbufs, n):
        rbuf = rbufs
        I32 = mybir.dt.int32
        sb.op("vector", lambda e: e.tensor_single_scalar(
            dst.bitcast(I32), src.bitcast(I32), 1, ALU.arith_shift_right),
            reads=rbuf, writes=rbuf)
        sb.op("vector", lambda e: e.tensor_scalar(
            dst.bitcast(I32), dst.bitcast(I32), -1, 0x5F3759DF, ALU.mult, ALU.add),
            reads=rbuf, writes=rbuf)
        for _ in range(2):
            sb.op("vector", lambda e: e.tensor_mul(tmp, dst, dst),
                  reads=rbuf, writes=rbuf)
            sb.op("vector", lambda e: e.tensor_mul(tmp, tmp, src),
                  reads=rbuf, writes=rbuf)
            sb.op("vector", lambda e: e.tensor_scalar(tmp, tmp, -0.5, 1.5,
                                                      ALU.mult, ALU.add),
                  reads=rbuf, writes=rbuf)
            sb.op("vector", lambda e: e.tensor_mul(dst, dst, tmp),
                  reads=rbuf, writes=rbuf)

    def ln_pre(src_t, src_buf, lo, n):
        xbg, sqg = RB("XBG"), RB("SQG")
        sb.op("scalar", lambda e: e.activation(
            XBG[:, :, 0:n], src_t[:, :, lo:lo + n], AF.Copy),
            reads=[src_buf], writes=[xbg])
        sb.op("scalar", lambda e: e.activation(
            SQG[:, :, 0:n], src_t[:, :, lo:lo + n], AF.Square),
            reads=[src_buf], writes=[sqg])

    def ln_stats(src_t, src_buf, lo, n, eps, pre_done=False):
        stat = B("STAT_ROWS")
        xbg, sqg = RB("XBG"), RB("SQG")
        if not pre_done:
            ln_pre(src_t, src_buf, lo, n)
        nblk = (n + NB - 1) // NB
        for bi in range(nblk):
            c0, cw = bi * NB, min(NB, n - bi * NB)
            for c in range(NCH):
                sb.op("tensor", lambda e, c=c, c0=c0, cw=cw: e.matmul(
                    PS[0][0:1, 0:cw], ONESC[:], XBG[:, c, c0:c0 + cw],
                    start=(c == 0), stop=(c == NCH - 1)),
                    reads=[xbg, B("ONESC")], writes=[psb[0]])
            for c in range(NCH):
                sb.op("tensor", lambda e, c=c, c0=c0, cw=cw: e.matmul(
                    PS[1][0:1, 0:cw], ONESC[:], SQG[:, c, c0:c0 + cw],
                    start=(c == 0), stop=(c == NCH - 1)),
                    reads=[sqg, B("ONESC")], writes=[psb[1]])
            epsc = EPS5C if eps > 1e-7 else EPS10C
            sb.op("scalar", lambda e, c0=c0, cw=cw: e.activation(
                ST_S[0:1, c0:c0 + cw], PS[0][0:1, 0:cw], AF.Identity,
                scale=1.0 / C),
                reads=[psb[0]], writes=[stat])
            sb.op("scalar", lambda e, c0=c0, cw=cw, epsc=epsc: e.activation(
                ST_Q[0:1, c0:c0 + cw], PS[1][0:1, 0:cw], AF.Identity,
                bias=epsc[0:1, 0:1], scale=1.0 / C),
                reads=[psb[1], B(epsc.name)], writes=[stat])
        sn = ST_S[0:1, 0:n]
        qn = ST_Q[0:1, 0:n]
        rn = ST_R[0:1, 0:n]
        tn = ST_T[0:1, 0:n]
        sb.op("vector", lambda e: e.tensor_mul(tn, sn, sn),
              reads=[stat, sqg], writes=[stat, sqg])
        sb.op("vector", lambda e: e.tensor_sub(qn, qn, tn),
              reads=[stat, sqg], writes=[stat])
        newton_rsqrt(qn, rn, tn, [stat, sqg], n)
        sb.op("vector", lambda e: e.tensor_copy(STB_M[0:1, 0:n], sn),
              reads=[stat], writes=[stat])
        sb.op("vector", lambda e: e.tensor_copy(STB_R[0:1, 0:n], rn),
              reads=[stat], writes=[stat])

    def ln_finish(src_t, src_buf, lo, n, dst_t, dst_buf, dst_lo=None,
                  f32_dst=False):
        if dst_lo is None:
            dst_lo = 0
        stat = B("STAT_ROWS")
        mrb = RB("MRB")
        nblk = (n + NB - 1) // NB
        for bi in range(nblk):
            c0, cw = bi * NB, min(NB, n - bi * NB)
            sb.op("tensor", lambda e, c0=c0, cw=cw: e.matmul(
                PS[2][:, 0:cw], ONESR[:], STB_M[0:1, c0:c0 + cw],
                start=True, stop=True), reads=[stat, B("ONESR")], writes=[psb[2]])
            sb.op("tensor", lambda e, c0=c0, cw=cw: e.matmul(
                PS[3][:, 0:cw], ONESR[:], STB_R[0:1, c0:c0 + cw],
                start=True, stop=True), reads=[stat, B("ONESR")], writes=[psb[3]])
            sb.op("scalar", lambda e, c0=c0, cw=cw: e.activation(
                MRB[:, 0, c0:c0 + cw], PS[2][:, 0:cw], AF.Copy),
                reads=[psb[2]], writes=[mrb])
            sb.op("scalar", lambda e, c0=c0, cw=cw: e.activation(
                MRB[:, 1, c0:c0 + cw], PS[3][:, 0:cw], AF.Copy),
                reads=[psb[3]], writes=[mrb])
        if dst_t is UG and not f32_dst:
            for c in range(NCH):
                sb.op("vector", lambda e, c=c: e.tensor_sub(
                    dst_t[:, c, dst_lo:dst_lo + n], src_t[:, c, lo:lo + n],
                    MRB[:, 0, 0:n]),
                    reads=[src_buf, mrb], writes=[UGC[c]])
                sb.op("vector", lambda e, c=c: e.tensor_mul(
                    dst_t[:, c, dst_lo:dst_lo + n],
                    dst_t[:, c, dst_lo:dst_lo + n], MRB[:, 1, 0:n]),
                    reads=[mrb, UGC[c]], writes=[UGC[c]])
            return
        mb = MRB[:, 0, 0:n].unsqueeze(1).broadcast_to([128, NCH, n])
        rb = MRB[:, 1, 0:n].unsqueeze(1).broadcast_to([128, NCH, n])
        sb.op("vector", lambda e: e.tensor_sub(
            dst_t[:, :, dst_lo:dst_lo + n], src_t[:, :, lo:lo + n], mb),
            reads=[src_buf, mrb], writes=[dst_buf])
        sb.op("vector", lambda e: e.tensor_mul(
            dst_t[:, :, dst_lo:dst_lo + n], dst_t[:, :, dst_lo:dst_lo + n], rb),
            reads=[mrb, dst_buf], writes=[dst_buf])

    def layernorm(src_t, src_buf, lo, n, eps, dst_t, dst_buf, dst_lo=None,
                  f32_dst=False, pre_done=False):
        ln_stats(src_t, src_buf, lo, n, eps, pre_done=pre_done)
        ln_finish(src_t, src_buf, lo, n, dst_t, dst_buf, dst_lo=dst_lo,
                  f32_dst=f32_dst)

    def proj_fm(nm_wt, src_t, src_buf, n, mlist, evac, banks=(4, 5, 6, 7),
                src_lo=0):
        per_chunk = src_t is UG
        """For each m in mlist: psum = Wt_m^T @ src[:, :, src_lo:src_lo+n]
        (accumulated over K chunks), then evac(m, c0, cw, bank)."""
        kdim = wts[nm_wt].shape[0]
        nblk = (n + NB - 1) // NB
        bi_rr = [0]
        for m in mlist:
            if kdim == C:
                slot, wbuf = wtile_small(nm_wt, m)
                halves = [(slot, wbuf, 0, NCH)]
            else:
                s0, b0 = wtile_large(nm_wt, m, 0)
                s1, b1 = wtile_large(nm_wt, m, 1)
                halves = [(s0, b0, 0, 16), (s1, b1, 16, 16)]
            for bi in range(nblk):
                c0, cw = bi * NB, min(NB, n - bi * NB)
                p = banks[bi_rr[0] % len(banks)]
                bi_rr[0] += 1
                nk = kdim // 128
                ki = 0
                for slot, wbuf, kofs, kn in halves:
                    for k in range(kn):
                        sbf_ = UGC[kofs + k] if per_chunk else src_buf
                        sb.op("tensor", lambda e, slot=slot, k=k, kofs=kofs,
                              c0=c0, cw=cw, p=p, ki=ki: e.matmul(
                                  PS[p][:, 0:cw], slot[:, k, :],
                                  src_t[:, kofs + k, src_lo + c0:src_lo + c0 + cw],
                                  start=(ki == 0), stop=(ki == nk - 1)),
                              reads=[sbf_, wbuf], writes=[psb[p]])
                        ki += 1
                evac(m, c0, cw, p)

    def evac_act(dst_t, dst_buf, nm_bias, func=None, dst_lo=0, scale=1.0):
        def ev(m, c0, cw, p):
            bias = bias_ap(nm_bias, m) if nm_bias else 0.0
            nonlocal func
            f = func if func is not None else (AF.Identity if nm_bias else AF.Copy)
            sb.op("scalar", lambda e, m=m, c0=c0, cw=cw, p=p, f=f, bias=bias:
                  e.activation(
                      dst_t[:, m, dst_lo + c0:dst_lo + c0 + cw], PS[p][:, 0:cw],
                      f, bias=bias, scale=scale),
                  reads=[psb[p], B("BIAS")], writes=[dst_buf])
        return ev

    dbg_outs = []

    def tap(name, sb_tensor, shape, buf):
        if not dbg:
            return
        d = nc.dram_tensor("dbg_" + name, shape, F32, kind="ExternalOutput").ap()
        rb = [B(f"XS_{i}") for i in range(G)] if sb_tensor is XS else [buf]
        sb.op("gpsimd", lambda e: e.dma_start(out=d, in_=sb_tensor[:, :, :]),
              reads=rb, inc=16)
        dbg_outs.append(name)

    def ev_stage_add_xs(nm_bias, xs_lo):
        def ev(m, c0, cw, p):
            sl = m % 2
            sb.op("scalar", lambda e, m=m, c0=c0, cw=cw, p=p, sl=sl: e.activation(
                RS32[:, sl, 0:cw], PS[p][:, 0:cw], AF.Identity,
                bias=bias_ap(nm_bias, m)),
                reads=[psb[p], B("BIAS")], writes=[RB("RS32")])
            sb.op("vector", lambda e, m=m, c0=c0, cw=cw, sl=sl: e.tensor_add(
                XS[:, m, xs_lo + c0:xs_lo + c0 + cw],
                XS[:, m, xs_lo + c0:xs_lo + c0 + cw], RS32[:, sl, 0:cw]),
                reads=[RB("RS32"), xsg(xs_lo)], writes=[xsg(xs_lo)])
        return ev

    def phase_A(s):
        RING_N[0] = 3
        for g in range(G):
            lo = g * GT
            src = x_d[s, :, lo:lo + GT].rearrange("(c p) t -> p c t", p=128)
            sb.op("sync", lambda e, src=src, lo=lo:
                  e.dma_start(out=XS[:, :, lo:lo + GT], in_=src),
                  writes=[xsg(lo)], inc=16)
        ln_pre(XS, xsg(0), 0, GT)
        for g in range(G):
            lo = g * GT
            layernorm(XS, xsg(lo), lo, GT, 1e-5, XS, xsg(lo), dst_lo=lo,
                      f32_dst=True, pre_done=True)
            if g + 1 < G:
                ln_pre(XS, xsg(lo + GT), lo + GT, GT)
        if dbg and s == 0:
            tap("t", XS, [128, NCH, T], B("XS"))
        sb.op("sync", lambda e: e.dma_start(
            out=BVRF[:, :], in_=bv_row_d),
            writes=[RB("BVRF")], inc=16)
        sb.op("vector", lambda e: e.tensor_copy(BVR[:, :], BVRF[:, :]),
              reads=[RB("BVRF")], writes=[RB("BVR")])
        ln_pre(XS, xsg(0), 0, GT)
        layernorm(XS, xsg(0), 0, GT, 1e-10, UG, RB("UG"), pre_done=True)
        for g in range(G):
            lo = g * GT
            if g + 1 < G:
                ln_pre(XS, xsg(lo + GT), lo + GT, GT)

            def ev_k(m, c0, cw, p, lo=lo, g=g):
                er = ([B("SPKYA"), B("SPKYB"), B("SPKM")]
                      if (g == 0 and m == 8) else [])
                sb.op("scalar", lambda e, m=m, c0=c0, cw=cw, p=p: e.activation(
                    KB[:, m - 8, lo + c0:lo + c0 + cw], PS[p][:, 0:cw],
                    AF.Identity, bias=bias_ap("b_attn", m)),
                    reads=[psb[p], B("BIAS")] + er, writes=[RB("KB")])
            proj_fm("wt_attn", UG, RB("UG"), GT, range(8, 16), ev_k)
            if g + 1 < G:
                ln_stats(XS, xsg(lo + GT), lo + GT, GT, 1e-10, pre_done=True)
            for nb in range(4):
                i = wr_i[0] % len(WR)
                wr_i[0] += 1
                vslot, vbuf = WR[i], RB(f"WR{i}")
                wpair_cache.clear()
                o, w = wcols[("wvs", nb)]
                sb.op("sync", lambda e, vslot=vslot, o=o, w=w: e.dma_start(
                    out=vslot.ap().rearrange("p a b -> p (a b)")[:, 0:w],
                    in_=wscr[:, o:o + w]),
                    reads=[B(("wvs", nb))], writes=[vbuf], inc=16)
                vv = vslot.ap().rearrange("p a b -> p (a b)")[:, 0:2048]
                vv = vv.rearrange("p (c f) -> p c f", c=NCH)
                for ti in range(4):
                    gti = 4 * g + ti
                    p = 4 + ti % 2
                    for k in range(NCH):
                        sb.op("tensor", lambda e, k=k, ti=ti, p=p, vv=vv: e.matmul(
                            PS[p][:, 0:256], UG[:, k, ti * 128:(ti + 1) * 128],
                            vv[:, k, :], start=(k == 0), stop=False),
                            reads=[RB("UG"), vbuf], writes=[psb[p]])
                    sb.op("tensor", lambda e, nb=nb, p=p: e.matmul(
                        PS[p][:, 0:256], ONESR[:, :], BVR[0:1, nb * 256:(nb + 1) * 256],
                        start=False, stop=True),
                        reads=[RB("BVR"), B("ONESR")], writes=[psb[p]])
                    sb.op("scalar", lambda e, gti=gti, nb=nb, p=p: e.activation(
                        VB[:, gti, nb * 256:(nb + 1) * 256], PS[p][:, 0:256], AF.Copy),
                        reads=[psb[p]], writes=[RB("VB")])
            if g + 1 < G:
                ln_finish(XS, xsg(lo + GT), lo + GT, GT, UG, RB("UG"))
        if dbg and s == 0:
            tap("k", KB, [128, NCH, T], RB("KB"))
            tap("v", VB, [128, 16, C], RB("VB"))
        for g in range(G):
            lo = g * GT
            cmax = 4 * g + 4
            layernorm(XS, xsg(lo), lo, GT, 1e-10, UG, RB("UG"))
            if dbg and s == 0 and g == 0:
                tap("ua2g0", UG, [128, NCH, GT], RB("UG"))
            pending_ep = [None]
            for hp in range(NCH):
                qs = hp % 2
                pv = 4 + hp % 2
                slot, wbuf = wtile_small("wt_attn", hp)
                for k in range(NCH):
                    sb.op("tensor", lambda e, slot=slot, k=k: e.matmul(
                        PS[7][:, :], slot[:, k, :], UG[:, k, 0:GT],
                        start=(k == 0), stop=(k == NCH - 1)),
                        reads=[RB("UG"), wbuf], writes=[psb[7]])
                sb.op("scalar", lambda e, hp=hp, qs=qs: e.activation(
                    QP[:, qs, :], PS[7][:, :], AF.Identity,
                    bias=bias_ap("b_attn", hp)),
                    reads=[psb[7], B("BIAS")], writes=[RB("QP")])
                if dbg and s == 0 and g == 0 and hp == 0:
                    tap("qp0", QP, [128, 2, NB], RB("QP"))

                def st_pair(c, hp=hp, qs=qs):
                    pe_ = 2 * (c % 2)
                    po_ = pe_ + 1
                    sb.op("tensor", lambda e, c=c, hp=hp, qs=qs, pe_=pe_: e.matmul(
                        PS[pe_][:, :], KB[0:64, hp, c * 128:(c + 1) * 128],
                        QP[0:64, qs, :], start=True, stop=True,
                        tile_position=(0, 0)),
                        reads=[RB("KB"), RB("QP")], writes=[psb[pe_]])
                    sb.op("tensor", lambda e, c=c, hp=hp, qs=qs, po_=po_: e.matmul(
                        PS[po_][:, :], KB[64:128, hp, c * 128:(c + 1) * 128],
                        QP[64:128, qs, :], start=True, stop=True,
                        tile_position=(64, 0)),
                        reads=[RB("KB"), RB("QP")], writes=[psb[po_]])

                def finish_c(c, hp=hp, pv=pv, cmax=cmax, g=g):
                    pe_ = 2 * (c % 2)
                    po_ = pe_ + 1
                    je, jo = pe_, po_
                    pair = PSA if c % 2 == 0 else PSB
                    sb.op("scalar", lambda e, pair=pair, je=je: e.activation(
                        PHAT[:, je:je + 2, :], pair[:, :, :], AF.Exp,
                        bias=ESHC[:, 0:1], scale=SC),
                        reads=[psb[pe_], psb[po_]], writes=[RB("PHAT")])
                    if c >= 4 * g:
                        k0 = 128 * c - 512 * g
                        ms = MASK[:, 384 - k0:384 - k0 + NB]
                        msb = ms.unsqueeze(1).broadcast_to([128, 2, NB])
                        sb.op("vector", lambda e, je=je, msb=msb: e.tensor_mul(
                            PHAT[:, je:je + 2, :], PHAT[:, je:je + 2, :], msb),
                            reads=[RB("PHAT"), B("MASK")], writes=[RB("PHAT")])
                    st = (c == 0)
                    sp = (c == cmax - 1)
                    sb.op("tensor", lambda e, c=c, hp=hp, je=je, st=st, sp=sp,
                          pv=pv: e.matmul(
                        PS[pv][0:64, :], VB[:, c, 128 * hp:128 * hp + 64],
                        PHAT[:, je, :], start=st, stop=sp, tile_position=(0, 0),
                        skip_group_check=True),
                        reads=[RB("VB"), RB("PHAT")], writes=[psb[pv]])
                    sb.op("tensor", lambda e, c=c, hp=hp, jo=jo, st=st, sp=sp,
                          pv=pv: e.matmul(
                        PS[pv][64:128, :], VB[:, c, 128 * hp + 64:128 * hp + 128],
                        PHAT[:, jo, :], start=st, stop=sp, tile_position=(0, 64),
                        skip_group_check=True),
                        reads=[RB("VB"), RB("PHAT")], writes=[psb[pv]])
                    sb.op("tensor", lambda e, je=je, st=st, sp=sp: e.matmul(
                        PS[6][0:1, :], ONESC[:, :], PHAT[:, je, :],
                        start=st, stop=sp, tile_position=(0, 0),
                        skip_group_check=True),
                        reads=[B("ONESC"), RB("PHAT")], writes=[psb[6]])
                    sb.op("tensor", lambda e, jo=jo, st=st, sp=sp: e.matmul(
                        PS[6][32:33, :], ONESC[:, :], PHAT[:, jo, :],
                        start=st, stop=sp, tile_position=(0, 32),
                        skip_group_check=True),
                        reads=[B("ONESC"), RB("PHAT")], writes=[psb[6]])

                st_pair(0)
                if pending_ep[0] is not None:
                    pending_ep[0]()
                    pending_ep[0] = None
                for c in range(cmax):
                    if c + 1 < cmax:
                        st_pair(c + 1)
                    finish_c(c)

                def epilogue(hp=hp, pv=pv):
                    sb.op("vector", lambda e: e.tensor_copy(
                        SRND[0:1, :], PS[6][0:1, :]),
                        reads=[psb[6]], writes=[RB("SRND")])
                    sb.op("vector", lambda e: e.tensor_copy(
                        SRND[32:33, :], PS[6][32:33, :]),
                        reads=[psb[6]], writes=[RB("SRND")])
                    sb.op("vector", lambda e: e.reciprocal(
                        SRND[0:33, :], SRND[0:33, :]),
                        reads=[RB("SRND")], writes=[RB("SRND")])
                    sb.op("vector", lambda e: e.tensor_copy(
                        SRNDB[0:33, :], SRND[0:33, :]),
                        reads=[RB("SRND")], writes=[RB("SRND")])
                    sb.op("vector", lambda e: e.tensor_copy(
                        SRNDB2[0:1, :], SRNDB[32:33, :]),
                        reads=[RB("SRND")], writes=[RB("SRND")])
                    sb.op("tensor", lambda e: e.matmul(
                        PS[7][0:64, :], ONESR[0:1, 0:64], SRNDB[0:1, :],
                        start=True, stop=True, tile_position=(0, 0)),
                        reads=[RB("SRND"), B("ONESR")], writes=[psb[7]])
                    sb.op("tensor", lambda e: e.matmul(
                        PS[7][64:128, :], ONESR[0:1, 0:64], SRNDB2[0:1, :],
                        start=True, stop=True, tile_position=(0, 64)),
                        reads=[RB("SRND"), B("ONESR")], writes=[psb[7]])
                    sb.op("scalar", lambda e: e.activation(
                        RSB[:, :], PS[7][:, :], AF.Copy),
                        reads=[psb[7]], writes=[B("RSB")])
                    sb.op("vector", lambda e, hp=hp, pv=pv: e.tensor_mul(
                        YG[:, hp, :], PS[pv][:, :], RSB[:, :]),
                        reads=[psb[pv], B("RSB")], writes=[RB("YG")])
                pending_ep[0] = epilogue
            pending_ep[0]()
            if dbg and s == 0 and g == 0:
                tap("yg0", YG, [128, NCH, GT], RB("YG"))
            proj_fm("wt_proj", YG, RB("YG"), GT, range(NCH),
                    ev_stage_add_xs("b_proj", lo))
        if dbg and s == 0:
            tap("t2", XS, [128, NCH, T], B("XS"))
        RING_N[0] = 4
        ln_pre(XS, xsg(0), 0, GTF)
        layernorm(XS, xsg(0), 0, GTF, 1e-10, UG, RB("UG"), pre_done=True)
        for gf in range(GF):
            lo = gf * GTF
            if gf + 1 < GF:
                ln_pre(XS, xsg(lo + GTF), lo + GTF, GTF)
            if dbg and s == 0 and gf == 0:
                tap("u2g0", UG, [128, NCH, GT], RB("UG"))
            proj_fm("wt_fc", UG, RB("UG"), GTF, range(FCH),
                    evac_act(HBF, RB("HBF"), "b_fc", func=AF.Gelu_apprx_tanh))
            if gf + 1 < GF:
                ln_stats(XS, xsg(lo + GTF), lo + GTF, GTF, 1e-10, pre_done=True)
            if dbg and s == 0 and gf == 0:
                tap("hg0", HBF, [128, FCH, GTF], RB("HBF"))
            proj_fm("wt_mlp", HBF, RB("HBF"), GTF, range(NCH),
                    ev_stage_add_xs("b_mlp", lo))
            if gf + 1 < GF:
                ln_finish(XS, xsg(lo + GTF), lo + GTF, GTF, UG, RB("UG"))
            if dbg and s == 0 and gf == 0:
                tap("t3g0", XS, [128, NCH, T], B("XS"))
            for c in range(NCH):
                src = x_d[s, c * 128:(c + 1) * 128, lo:lo + GTF]
                sl = c % 2
                sb.op("gpsimd", lambda e, src=src, sl=sl:
                      e.dma_start(out=XGC[:, sl, :], in_=src),
                      writes=[RB("XGC")], inc=16)
                sb.op("vector", lambda e, c=c, sl=sl, lo=lo: e.tensor_add(
                    XS[:, c, lo:lo + GTF], XS[:, c, lo:lo + GTF], XGC[:, sl, :]),
                    reads=[RB("XGC"), xsg(lo)], writes=[xsg(lo)])

    def phase_FFN(s, last):
        RING_N[0] = 4
        ln_pre(XS, xsg(0), 0, GTF)
        layernorm(XS, xsg(0), 0, GTF, 1e-5, UG, RB("UG"), pre_done=True)
        for gf in range(GF):
            lo = gf * GTF
            xsb = xsg(lo)
            if gf + 1 < GF:
                ln_pre(XS, xsg(lo + GTF), lo + GTF, GTF)

            def ev_relu2(m, c0, cw, p):
                sl = m % 2
                sb.op("scalar", lambda e, m=m, cw=cw, p=p, sl=sl: e.activation(
                    RS32[:, sl, 0:cw], PS[p][:, 0:cw], AF.Relu),
                    reads=[psb[p]], writes=[RB("RS32")])
                sb.op("vector", lambda e, m=m, cw=cw, sl=sl: e.tensor_mul(
                    HBF[:, m, 0:cw], RS32[:, sl, 0:cw], RS32[:, sl, 0:cw]),
                    reads=[RB("RS32")], writes=[RB("HBF")])
            proj_fm("wt_ffn_fc", UG, RB("UG"), GTF, range(FCH), ev_relu2)
            if gf + 1 < GF:
                ln_stats(XS, xsg(lo + GTF), lo + GTF, GTF, 1e-5, pre_done=True)

            SPKY = SPKYA if gf % 2 == 0 else SPKYB
            yb = RB(SPKY.name)
            extra0 = [B("KB")] if gf == 0 else []

            def ev_y(m, c0, cw, p):
                er = extra0 if m == 0 else []
                sb.op("scalar", lambda e, m=m, cw=cw, p=p, SPKY=SPKY: e.activation(
                    SPKY[:, m, 0:cw], PS[p][:, 0:cw], AF.Copy),
                    reads=[psb[p]] + er, writes=[yb])
            proj_fm("wt_ffn_proj", HBF, RB("HBF"), GTF, range(NCH), ev_y)
            if gf + 1 < GF:
                ln_finish(XS, xsg(lo + GTF), lo + GTF, GTF, UG, RB("UG"))
            mb, sbb = B("SPKM"), RB("SBF")
            Y, M_, S_ = SPKY[:, :, :], SPKM[:, :, :], SBF[:, :, :]
            XSg = XS[:, :, lo:lo + GTF]

            def gt(src, src_b):
                sb.op("vector", lambda e, S_=S_, src=src: e.tensor_single_scalar(
                    S_, src, 1.0, ALU.is_gt), reads=[src_b], writes=[sbb])

            def acc_xs(XSg=XSg, S_=S_):
                sb.op("vector", lambda e, XSg=XSg, S_=S_: e.scalar_tensor_tensor(
                    XSg, S_, 0.2, XSg, ALU.mult, ALU.add),
                    reads=[sbb, xsb], writes=[xsb])

            gt(Y, yb)
            acc_xs()
            sb.op("vector", lambda e, M_=M_, Y=Y: e.scalar_tensor_tensor(
                M_, Y, 0.9, Y, ALU.mult, ALU.add), reads=[yb] + extra0, writes=[mb])
            sb.op("vector", lambda e, M_=M_, S_=S_: e.tensor_sub(M_, M_, S_),
                  reads=[sbb, mb], writes=[mb])
            for _t in range(3):
                gt(M_, mb)
                acc_xs()
                sb.op("vector", lambda e, M_=M_, Y=Y: e.scalar_tensor_tensor(
                    M_, M_, 0.9, Y, ALU.mult, ALU.add),
                    reads=[yb, mb], writes=[mb])
                sb.op("vector", lambda e, M_=M_, S_=S_: e.tensor_sub(M_, M_, S_),
                      reads=[sbb, mb], writes=[mb])
            gt(M_, mb)
            acc_xs()
        if last:
            for gf in range(GF):
                lo = gf * GTF
                dst = y_d[s, :, lo:lo + GTF].rearrange("(c p) t -> p c t", p=128)
                sb.op("gpsimd", lambda e, dst=dst, lo=lo: e.dma_start(
                    out=dst, in_=XS[:, :, lo:lo + GTF]), reads=[xsg(lo)], inc=16)

    def phase_C(s):
        RING_N[0] = 3
        for g in range(G):
            lo = g * GT
            layernorm(XS, xsg(lo), lo, GT, 1e-5, UG, RB("UG"))
            proj_fm("wt_p", UG, RB("UG"), GT, range(NCH),
                    evac_act(XBG, RB("XBG"), "b_p"))

            def ev_into(dst_t, dst_b, nmb, lo=lo, g=g):
                def ev(m, c0, cw, p):
                    er = ([B("SPKYA"), B("SPKYB"), B("SPKM")]
                          if (g == 0 and m == 0) else [])
                    sb.op("scalar", lambda e, m=m, c0=c0, cw=cw, p=p: e.activation(
                        dst_t[:, m, lo + c0:lo + c0 + cw], PS[p][:, 0:cw],
                        AF.Identity, bias=bias_ap(nmb, m)),
                        reads=[psb[p], B("BIAS")] + er, writes=[dst_b])
                return ev
            proj_fm("wt_k", XBG, RB("XBG"), GT, range(NCH),
                    ev_into(KB, RB("KB"), "b_k"))
            proj_fm("wt_v", XBG, RB("XBG"), GT, range(NCH),
                    ev_into(VC, RB("VB"), "b_v"))
        for g in range(G):
            lo = g * GT
            layernorm(XS, xsg(lo), lo, GT, 1e-5, UG, RB("UG"))
            proj_fm("wt_p", UG, RB("UG"), GT, range(NCH),
                    evac_act(XBG, RB("XBG"), "b_p"))
            proj_fm("wt_q", XBG, RB("XBG"), GT, range(NCH),
                    evac_act(YG, RB("YG"), "b_q"))
            for n_idx, dlt in ((0, -1), (1, 0), (2, 1)):
                for c in range(NCH):
                    jj = c % 2
                    lo_s = lo + dlt
                    if 0 <= lo_s and lo_s + GT <= T:
                        sb.op("vector", lambda e, c=c, jj=jj, lo_s=lo_s: e.tensor_mul(
                            PHAT[:, jj, :], YG[:, c, :], KB[:, c, lo_s:lo_s + GT]),
                            reads=[RB("YG"), RB("KB")], writes=[RB("PHAT")])
                    elif lo_s < 0:
                        sb.op("vector", lambda e, c=c, jj=jj: e.tensor_mul(
                            PHAT[:, jj, 1:GT], YG[:, c, 1:GT], KB[:, c, 0:GT - 1]),
                            reads=[RB("YG"), RB("KB")], writes=[RB("PHAT")])
                        sb.op("vector", lambda e, c=c, jj=jj: e.tensor_mul(
                            PHAT[:, jj, 0:1], YG[:, c, 0:1], KB[:, c, T - 1:T]),
                            reads=[RB("YG"), RB("KB")], writes=[RB("PHAT")])
                    else:
                        sb.op("vector", lambda e, c=c, jj=jj, lo_s=lo_s: e.tensor_mul(
                            PHAT[:, jj, 0:GT - 1], YG[:, c, 0:GT - 1],
                            KB[:, c, lo_s:T]),
                            reads=[RB("YG"), RB("KB")], writes=[RB("PHAT")])
                        sb.op("vector", lambda e, c=c, jj=jj: e.tensor_mul(
                            PHAT[:, jj, GT - 1:GT], YG[:, c, GT - 1:GT],
                            KB[:, c, 0:1]),
                            reads=[RB("YG"), RB("KB")], writes=[RB("PHAT")])
                    sb.op("tensor", lambda e, c=c, jj=jj, n_idx=n_idx: e.matmul(
                        PS[n_idx][0:16, :], EIND[:, c, :], PHAT[:, jj, :],
                        start=(c == 0), stop=(c == NCH - 1)),
                        reads=[RB("PHAT"), B("EIND")], writes=[psb[n_idx]])
            ABSn = (ABS0, ABS1, ABS2)
            for n in range(3):
                sb.op("vector", lambda e, n=n: e.tensor_copy(
                    ABSn[n][:, :], PS[n][0:16, :]),
                    reads=[psb[n]], writes=[RB("ABS0")])
            sb.op("vector", lambda e: e.tensor_max(ABSM[:, :], ABS0[:, :], ABS1[:, :]),
                  reads=[RB("ABS0")], writes=[RB("ABSM")])
            sb.op("vector", lambda e: e.tensor_max(ABSM[:, :], ABSM[:, :], ABS2[:, :]),
                  reads=[RB("ABS0")], writes=[RB("ABSM")])
            for n in range(3):
                sb.op("vector", lambda e, n=n: e.tensor_sub(
                    ABSn[n][:, :], ABSn[n][:, :], ABSM[:, :]),
                    reads=[RB("ABS0"), RB("ABSM")], writes=[RB("ABS0")])
                sb.op("scalar", lambda e, n=n: e.activation(
                    ABSn[n][:, :], ABSn[n][:, :], AF.Exp, scale=SC),
                    reads=[RB("ABS0")], writes=[RB("ABS0")])
            sb.op("vector", lambda e: e.tensor_add(ABSM[:, :], ABS0[:, :], ABS1[:, :]),
                  reads=[RB("ABS0")], writes=[RB("ABSM")])
            sb.op("vector", lambda e: e.tensor_add(ABSM[:, :], ABSM[:, :], ABS2[:, :]),
                  reads=[RB("ABS0")], writes=[RB("ABSM")])
            sb.op("vector", lambda e: e.reciprocal(
                RS32[0:16, 0, :], ABSM[:, :]),
                reads=[RB("ABSM")], writes=[RB("RS32")])
            for n in range(3):
                sb.op("vector", lambda e, n=n: e.tensor_mul(
                    WROW[n][:, :], ABSn[n][:, :], RS32[0:16, 0, :]),
                    reads=[RB("ABS0"), RB("RS32")], writes=[RB("WROW0")])
            for c in range(NCH):
                for n, dlt in ((0, -1), (1, 0), (2, 1)):
                    wslot = 2 + n % 2
                    sb.op("tensor", lambda e, c=c, n=n, wslot=wslot: e.matmul(
                        PS[3][:, :], FIND[:, c, :], WROW[n][:, :],
                        start=True, stop=True),
                        reads=[RB("WROW0"), B("FIND")], writes=[psb[3]])
                    sb.op("scalar", lambda e, wslot=wslot: e.activation(
                        PHAT[:, wslot, :], PS[3][:, :], AF.Copy),
                        reads=[psb[3]], writes=[RB("PHAT")])
                    lo_s = lo + dlt
                    dsti = 0 if n == 0 else 1
                    dst = RS32[:, dsti, :]

                    def ml(out_ap, w_ap, v_ap, extra_reads=()):
                        sb.op("vector", lambda e, out_ap=out_ap, w_ap=w_ap,
                              v_ap=v_ap: e.tensor_mul(out_ap, w_ap, v_ap),
                              reads=[RB("PHAT"), RB("VB")], writes=[RB("RS32")])
                    if 0 <= lo_s and lo_s + GT <= T:
                        ml(dst, PHAT[:, wslot, :], VC[:, c, lo_s:lo_s + GT])
                    elif lo_s < 0:
                        ml(RS32[:, dsti, 1:GT], PHAT[:, wslot, 1:GT],
                           VC[:, c, 0:GT - 1])
                        ml(RS32[:, dsti, 0:1], PHAT[:, wslot, 0:1],
                           VC[:, c, T - 1:T])
                    else:
                        ml(RS32[:, dsti, 0:GT - 1], PHAT[:, wslot, 0:GT - 1],
                           VC[:, c, lo_s:T])
                        ml(RS32[:, dsti, GT - 1:GT], PHAT[:, wslot, GT - 1:GT],
                           VC[:, c, 0:1])
                    if n == 1:
                        sb.op("vector", lambda e: e.tensor_add(
                            RS32[:, 0, :], RS32[:, 0, :], RS32[:, 1, :]),
                            reads=[RB("RS32")], writes=[RB("RS32")])
                    elif n == 2:
                        sb.op("vector", lambda e, c=c: e.tensor_add(
                            UG[:, c, :], RS32[:, 0, :], RS32[:, 1, :]),
                            reads=[RB("RS32")], writes=[RB("UG")])
            proj_fm("wt_o", UG, RB("UG"), GT, range(NCH),
                    ev_stage_add_xs("b_o", lo))

    import os as _os
    order = _os.environ.get("MK_PHASES", "ABCD")
    for s in range(SEQS):
        for ph in order:
            if ph == "A":
                phase_A(s)
                if s == 0:
                    tap("xs_A", XS, [128, NCH, T], B("XS"))
            elif ph == "B":
                phase_FFN(s, last=False)
                if s == 0:
                    tap("xs_B", XS, [128, NCH, T], B("XS"))
            elif ph == "C":
                phase_C(s)
                if s == 0:
                    tap("xs_C", XS, [128, NCH, T], B("XS"))
            elif ph == "D":
                phase_FFN(s, last=True)
                if s == 0:
                    tap("xs_D", XS, [128, NCH, T], B("XS"))
    sb.plan["sync"].append(Wait("sync", sb.cnt["sync"]))
    sb.plan["sync"].append(Wait("gpsimd", sb.cnt["gpsimd"]))

    sems = {e: nc.alloc_semaphore(f"sem_{e}") for e in ENGS}
    nops = {e: sum(1 for it in sb.plan[e] if isinstance(it, Op)) for e in ENGS}
    print("plan ops:", nops, "waits:",
          {e: len(sb.plan[e]) - nops[e] for e in ENGS})

    import os as _os2
    _imap = {}

    def replay(name):
        def run(eng):
            for idx, it in enumerate(sb.plan[name]):
                if isinstance(it, Wait):
                    ins = eng.wait_ge(sems[it.src], it.val)
                    if _os2.environ.get("MK_IMAP"):
                        _imap[ins.ins.name] = (name, idx, f"WAIT {it.src}>={it.val}")
                else:
                    ins = it.fn(eng)
                    ins.then_inc(sems[name], it.inc)
                    if _os2.environ.get("MK_IMAP"):
                        _imap[ins.ins.name] = (name, idx, "op")
        return run

    with nc.Block() as block:
        block.tensor(replay("tensor"))
        block.vector(replay("vector"))
        block.scalar(replay("scalar"))
        block.gpsimd(replay("gpsimd"))
        block.sync(replay("sync"))
    nc._mk_imap = _imap
    nc._mk_plan = sb.plan
    return nc


_NC_CACHE = {}


def kernel(**inputs):
    def tr(a):
        return np.ascontiguousarray(np.asarray(a, dtype=np.float32).T)

    def vec(a):
        a = np.asarray(a, dtype=np.float32)
        return np.ascontiguousarray(a.reshape(-1, 128).T)

    x = np.asarray(inputs["L0_attn_a_x"], dtype=np.float32)
    xT = np.ascontiguousarray(np.swapaxes(x, 1, 2))
    common = {
        "wt_attn": tr(inputs["L0_attn_a_w_attn"]),
        "wt_proj": tr(inputs["L0_attn_a_w_proj"]),
        "wt_fc": tr(inputs["L0_attn_a_w_fc"]),
        "wt_mlp": tr(inputs["L0_attn_a_w_mlp_proj"]),
        "wt_p": tr(inputs["L0_attn_b_Wp"]),
        "wt_q": tr(inputs["L0_attn_b_Wq"]),
        "wt_k": tr(inputs["L0_attn_b_Wk"]),
        "wt_v": tr(inputs["L0_attn_b_Wv"]),
        "wt_o": tr(inputs["L0_attn_b_Wo"]),
        "wt_ffn_fc": tr(inputs["L0_ffn_W_fc"]),
        "wt_ffn_proj": tr(inputs["L0_ffn_W_proj"]),
        "b_attn": vec(inputs["L0_attn_a_b_attn"]),
        "b_proj": vec(inputs["L0_attn_a_b_proj"]),
        "b_fc": vec(inputs["L0_attn_a_b_fc"]),
        "b_mlp": vec(inputs["L0_attn_a_b_mlp_proj"]),
        "b_p": vec(inputs["L0_attn_b_bp"]),
        "b_q": vec(inputs["L0_attn_b_bq"]),
        "b_k": vec(inputs["L0_attn_b_bk"]),
        "b_v": vec(inputs["L0_attn_b_bv"]),
        "b_o": vec(inputs["L0_attn_b_bo"]),
        "b_attn_v": np.ascontiguousarray(
            np.asarray(inputs["L0_attn_a_b_attn"], dtype=np.float32)[2048:].reshape(1, -1)),
    }
    if "nc" not in _NC_CACHE:
        _NC_CACHE["nc"] = build_program()
    nc = _NC_CACHE["nc"]
    in_maps = [dict(common, x=np.ascontiguousarray(xT[2 * c:2 * c + 2]))
               for c in range(NCORES)]
    res = run_bass_kernel_spmd(nc, in_maps, core_ids=list(range(NCORES)))
    yT = np.concatenate([np.asarray(r["y"], dtype=np.float32)
                         for r in res.results], axis=0)
    return np.ascontiguousarray(np.swapaxes(yT, 1, 2))


# version 26
# speedup vs baseline: 1.0323x; 1.0123x over previous
import numpy as np
import ml_dtypes
import concourse.bass as bass
import concourse.mybir as mybir
from concourse.bass_utils import run_bass_kernel_spmd

F32 = mybir.dt.float32
BF16 = mybir.dt.bfloat16
AF = mybir.ActivationFunctionType
ALU = mybir.AluOpType
BF = ml_dtypes.bfloat16

NCORES = 8
SEQS = 2
T = 2048
C = 1024
NCH = 8
H = 16
DH = 64
FC = 4096
FCH = 32
G = 4
GT = 512
GF = 4
GTF = 512
NB = 512
ESH = -24.0
SC = 0.125

ENGS = ("tensor", "vector", "scalar", "gpsimd", "sync")


class Buf:
    __slots__ = ("name", "writer", "readers")

    def __init__(self, name):
        self.name = name
        self.writer = None
        self.readers = []


class Op:
    __slots__ = ("fn", "inc", "line")

    def __init__(self, fn, inc):
        self.fn = fn
        self.inc = inc
        import sys as _s
        f = _s._getframe(2)
        self.line = f.f_lineno


class Wait:
    __slots__ = ("src", "val")

    def __init__(self, src, val):
        self.src = src
        self.val = val


class SB:

    def __init__(self):
        self.plan = {e: [] for e in ENGS}
        self.cnt = {e: 0 for e in ENGS}
        self.waited = {e: {} for e in ENGS}

    def op(self, eng, fn, reads=(), writes=(), inc=1):
        need = {}
        dma_q = eng in ("sync", "gpsimd")

        def add(pe, pv):
            if (pe != eng or dma_q) and pv > need.get(pe, 0):
                need[pe] = pv

        for b in reads:
            if b.writer is not None:
                add(*b.writer)
        for b in writes:
            if b.writer is not None:
                add(*b.writer)
            for r in b.readers:
                add(*r)
        for pe in sorted(need):
            pv = need[pe]
            if self.waited[eng].get(pe, 0) < pv:
                self.plan[eng].append(Wait(pe, pv))
                self.waited[eng][pe] = pv
        self.plan[eng].append(Op(fn, inc))
        self.cnt[eng] += inc
        newv = self.cnt[eng]
        for b in writes:
            b.writer = (eng, newv)
            b.readers = []
        for b in reads:
            b.readers.append((eng, newv))


def build_program(dbg=False):
    nc = bass.Bass("TRN2", target_bir_lowering=False, debug=False)

    x_d = nc.dram_tensor("x", [SEQS, C, T], F32, kind="ExternalInput").ap()
    y_d = nc.dram_tensor("y", [SEQS, C, T], F32, kind="ExternalOutput").ap()
    wts = {}
    for nm, k, f in [("wt_attn", C, 3 * C), ("wt_proj", C, C), ("wt_fc", C, FC),
                     ("wt_mlp", FC, C), ("wt_p", C, C), ("wt_q", C, C),
                     ("wt_k", C, C), ("wt_v", C, C), ("wt_o", C, C),
                     ("wt_ffn_fc", C, FC), ("wt_ffn_proj", FC, C)]:
        wts[nm] = nc.dram_tensor(nm, [k, f], F32, kind="ExternalInput").ap()
    wcols = {}
    woff = [0]

    def walloc(key, width):
        wcols[key] = (woff[0], width)
        woff[0] += width
    for _nm, _kd, _fd in [("wt_attn", C, 2 * C), ("wt_proj", C, C),
                          ("wt_fc", C, FC), ("wt_p", C, C), ("wt_q", C, C),
                          ("wt_k", C, C), ("wt_v", C, C), ("wt_o", C, C),
                          ("wt_ffn_fc", C, FC)]:
        for _m in range(_fd // 128):
            walloc((_nm, _m), (_kd // 128) * 128)
    for _nm in ("wt_mlp", "wt_ffn_proj"):
        for _m in range(NCH):
            for _h in range(2):
                walloc((_nm, _m, _h), 16 * 128)
    for _h in range(4):
        walloc(("wvs", _h), NCH * 256)
    wscr = nc.dram_tensor("wscr", [128, woff[0]], BF16, kind="Internal").ap()

    bss = {}
    for nm, f in [("b_attn", 3 * C), ("b_proj", C), ("b_fc", FC), ("b_mlp", C),
                  ("b_p", C), ("b_q", C), ("b_k", C), ("b_v", C), ("b_o", C)]:
        bss[nm] = nc.dram_tensor(nm, [128, f // 128], F32,
                                 kind="ExternalInput").ap()
    bv_row_d = nc.dram_tensor("b_attn_v", [1, C], F32, kind="ExternalInput").ap()

    ii = np.arange(896)[None, :]
    pp = np.arange(128)[:, None]
    mask_np = (ii >= pp + 384).astype(BF)
    e_np = np.zeros((128, NCH, 16), dtype=BF)
    f_np = np.zeros((16, NCH, 128), dtype=BF)
    for c in range(NCH):
        for p in range(128):
            h = 2 * c + p // 64
            e_np[p, c, h] = 1
            f_np[h, c, p] = 1
    ones_col_d = nc.inline_tensor(np.ones((128, 1), dtype=BF), name="d_onc").ap()
    esh_d = nc.inline_tensor(np.full((128, 1), ESH, dtype=np.float32),
                             name="d_esh").ap()
    eps5_d = nc.inline_tensor(np.full((128, 1), 1e-5, dtype=np.float32),
                              name="d_eps5").ap()
    eps10_d = nc.inline_tensor(np.full((128, 1), 1e-10, dtype=np.float32),
                               name="d_eps10").ap()
    one_d = nc.inline_tensor(np.ones((128, 1), dtype=np.float32),
                             name="d_one").ap()
    ones_row_d = nc.inline_tensor(np.ones((1, 128), dtype=BF), name="d_onr").ap()
    mask_d = nc.inline_tensor(np.ascontiguousarray(mask_np), name="d_mask").ap()
    e_d = nc.inline_tensor(np.ascontiguousarray(e_np), name="d_eind").ap()
    f_d = nc.inline_tensor(np.ascontiguousarray(f_np), name="d_find").ap()

    K = 1024
    SB_BASE = 16512
    A = nc.alloc_sbuf_tensor_at

    def at(name, shape, dtype, off):
        return A(name, shape, dtype, offset=SB_BASE + off)

    XS = at("XS", [128, NCH, T], F32, 0)
    KB = at("KB", [128, NCH, T], BF16, 64 * K)
    SPKY = at("SPKY", [128, NCH, GTF], F32, 64 * K)
    SPKM = at("SPKM", [128, NCH, GTF], F32, 64 * K + 16 * K)
    VB = at("VB", [128, 16, C], BF16, 96 * K)
    HBF = at("HBF", [128, FCH, GTF], BF16, 96 * K)
    VC = at("VC", [128, NCH, T], BF16, 96 * K)
    UG = at("UG", [128, NCH, GT], BF16, 130 * K)
    YG = at("YG", [128, NCH, GT], BF16, 138 * K)
    XBG = at("XBG", [128, NCH, GT], BF16, 146 * K)
    WROW = [at(f"WROW{i}", [16, GT], BF16, 146 * K + 5 * K + i * K)
            for i in range(3)]
    SQG = at("SQG", [128, NCH, GT], BF16, 154 * K)
    SRND = at("SRND", [33, NB], F32, 154 * K)
    SRNDB = at("SRNDB", [33, NB], BF16, 154 * K + 2 * K)
    SRNDB2 = at("SRNDB2", [1, NB], BF16, 154 * K + 3 * K)
    ABS0 = at("ABS0", [16, GT], F32, 154 * K)
    ABS1 = at("ABS1", [16, GT], F32, 154 * K + 2 * K)
    ABS2 = at("ABS2", [16, GT], F32, 154 * K + 4 * K)
    ABSM = at("ABSM", [16, GT], F32, 154 * K + 6 * K)
    QP = at("QP", [128, 2, NB], BF16, 162 * K)
    BVR = at("BVR", [1, C], BF16, 162 * K)
    PHAT = at("PHAT", [128, 4, NB], BF16, 164 * K)
    RS32 = at("RS32", [128, 2, NB], F32, 168 * K)
    BVRF = at("BVRF", [1, C], F32, 168 * K)
    RSB = at("RSB", [128, NB], F32, 172 * K)
    MRB = at("MRB", [128, 2, GT], F32, 174 * K)
    ST_S = at("ST_S", [1, GT], F32, 178 * K)
    ST_Q = at("ST_Q", [1, GT], F32, 180 * K)
    ST_R = at("ST_R", [1, GT], F32, 182 * K)
    ST_T = at("ST_T", [1, GT], F32, 154 * K)
    STB_M = at("STB_M", [1, GT], BF16, 184 * K)
    STB_R = at("STB_R", [1, GT], BF16, 185 * K)
    SBF = at("SBF", [128, 4, GTF], F32, 138 * K)
    XGC = at("XGC", [128, 2, GTF], F32, 186 * K)
    PSTG = at("PSTG", [128, 2, NCH, 128], BF16, 186 * K)
    WR = [at(f"WR{i}", [128, 16, 128], BF16, 190 * K + 4 * K * i) for i in range(3)]
    WR3 = at("WR3", [128, 16, 128], BF16, 164 * K)
    cb = 202 * K
    ONESC = at("ONESC", [128, 1], BF16, cb); cb += 32
    ONESR = at("ONESR", [1, 128], BF16, cb); cb += 256
    EIND = at("EIND", [128, NCH, 16], BF16, cb); cb += 256
    BIAS = at("BIAS", [128, 112], F32, cb); cb += 448
    MASK = at("MASK", [128, 896], BF16, cb); cb += 1792
    ESHC = at("ESHC", [128, 1], F32, cb); cb += 32
    EPS5C = at("EPS5C", [128, 1], F32, cb); cb += 32
    EPS10C = at("EPS10C", [128, 1], F32, cb); cb += 32
    EONEC = at("EONEC", [128, 1], F32, cb); cb += 32
    FIND = at("FIND", [16, NCH, 128], BF16, cb); cb += 2048
    assert cb <= 208 * K - 128, cb

    bias_off = {}
    off = 0
    for nm, f in [("b_attn", 3 * C), ("b_proj", C), ("b_fc", FC), ("b_mlp", C),
                  ("b_p", C), ("b_q", C), ("b_k", C), ("b_v", C), ("b_o", C)]:
        bias_off[nm] = off
        off += f // 128

    PSA = nc.alloc_psum_tensor("PSA", [128, 2, NB], F32)
    PSB = nc.alloc_psum_tensor("PSB", [128, 2, NB], F32)
    PS47 = [nc.alloc_psum_tensor(f"PS{i}", [128, NB], F32) for i in range(4, 8)]
    PS = [PSA[:, 0, :], PSA[:, 1, :], PSB[:, 0, :], PSB[:, 1, :]] + \
         [t[:, :] for t in PS47]

    sb = SB()
    bufs = {}

    def B(name):
        if name not in bufs:
            bufs[name] = Buf(name)
        return bufs[name]

    alias = { "HBF": "VB", "VC": "VB",
             "WROW0": "XBG", "WROW1": "XBG", "WROW2": "XBG",
             "SRND": "SQG", "SRNDB": "SQG", "SRNDB2": "SQG", "ST_T": "SQG",
             "ABS0": "SQG", "ABS1": "SQG", "ABS2": "SQG", "ABSM": "SQG",
             "BVR": "QP", "SBF": "YG", "PSTG": "XGC", "BVRF": "RS32",
             "WR3": "PHAT"}

    def RB(name):
        return B(alias.get(name, name))

    psb = [B(f"ps{i}") for i in range(8)]
    XSB = [B(f"XS_{i}") for i in range(G)]
    UGC = [B(f"UG_{c}") for c in range(NCH)]

    def xsg(lo):
        return XSB[lo // GT]

    for dst, src in [(ONESC, ones_col_d), (ONESR, ones_row_d), (MASK, mask_d),
                     (EIND, e_d), (FIND, f_d), (ESHC, esh_d),
                     (EPS5C, eps5_d), (EPS10C, eps10_d), (EONEC, one_d)]:
        sb.op("sync", lambda e, dst=dst, src=src: e.dma_start(out=dst[:], in_=src),
              writes=[B(dst.name)], inc=16)
    for nm in bias_off:
        o, w = bias_off[nm], bss[nm].shape[1]
        src = bss[nm]
        sb.op("sync", lambda e, o=o, w=w, src=src:
              e.dma_start(out=BIAS[:, o:o + w], in_=src),
              writes=[B("BIAS")], inc=16)

    def fill_order():
        yield ("wt_attn", C, 2 * C)
        yield ("wvs", None, None)
        yield ("wt_proj", C, C)
        yield ("wt_fc", C, FC)
        yield ("wt_mlp", None, None)
        yield ("wt_ffn_fc", C, FC)
        yield ("wt_ffn_proj", None, None)
        yield ("wt_p", C, C)
        yield ("wt_q", C, C)
        yield ("wt_k", C, C)
        yield ("wt_v", C, C)
        yield ("wt_o", C, C)

    fill_i = [0]
    pending_store = [None]

    def fill_tile(load_fn, store_fn):
        load_fn()
        if pending_store[0] is not None:
            pending_store[0]()
        pending_store[0] = store_fn

    for nm, kd, fd in fill_order():
        if nm == "wvs":
            for h in range(4):
                i = fill_i[0] % len(WR)
                fill_i[0] += 1
                slot, wbuf = WR[i], RB(f"WR{i}")
                o, w = wcols[("wvs", h)]
                src = wts["wt_attn"][:, 2 * C + h * 256:2 * C + (h + 1) * 256]
                src = src.rearrange("(c p) f -> p c f", p=128)
                fill_tile(
                    lambda slot=slot, src=src, wbuf=wbuf: sb.op(
                        "gpsimd", lambda e: e.dma_start(
                            out=slot.ap().rearrange("p a b -> p (a b)")
                            .rearrange("p (c f) -> p c f", c=NCH)[:, :, :], in_=src),
                        writes=[wbuf], inc=16),
                    lambda slot=slot, o=o, w=w, wbuf=wbuf, h=h: sb.op(
                        "gpsimd", lambda e: e.dma_start(
                            out=wscr[:, o:o + w],
                            in_=slot.ap().rearrange("p a b -> p (a b)")[:, 0:w]),
                        reads=[wbuf], writes=[B(("wvs", h))], inc=16))
            continue
        if nm in ("wt_mlp", "wt_ffn_proj"):
            for m in range(NCH):
                for h in range(2):
                    for hh in range(2):
                        i = fill_i[0] % 2
                        fill_i[0] += 1
                        stg = PSTG[:, i, :, :]
                        sbuf_ = RB("PSTG")
                        r0 = h * 2048 + hh * 1024
                        src = wts[nm][r0:r0 + 1024, m * 128:(m + 1) * 128]
                        src = src.rearrange("(c p) f -> p c f", p=128)
                        o, w = wcols[(nm, m, h)]
                        oo = o + hh * 1024
                        fill_tile(
                            lambda stg=stg, src=src, sbuf_=sbuf_: sb.op(
                                "gpsimd", lambda e: e.dma_start(
                                    out=stg, in_=src),
                                writes=[sbuf_], inc=16),
                            lambda stg=stg, oo=oo, sbuf_=sbuf_, nm=nm, m=m, h=h:
                            sb.op("gpsimd", lambda e: e.dma_start(
                                out=wscr[:, oo:oo + 1024],
                                in_=stg.rearrange("p a b -> p (a b)")),
                                reads=[sbuf_], writes=[B((nm, m, h))], inc=16))
            continue
        use_ring = nm == "wt_attn"
        for m in range(fd // 128):
            src = wts[nm][:, m * 128:(m + 1) * 128]
            src = src.rearrange("(c p) f -> p c f", p=128)
            o, w = wcols[(nm, m)]
            if use_ring:
                i = fill_i[0] % len(WR)
                fill_i[0] += 1
                slot, wbuf = WR[i], RB(f"WR{i}")
                fill_tile(
                    lambda slot=slot, src=src, wbuf=wbuf: sb.op(
                        "gpsimd", lambda e: e.dma_start(
                            out=slot[:, 0:NCH, :], in_=src), writes=[wbuf], inc=16),
                    lambda slot=slot, o=o, w=w, wbuf=wbuf, nm=nm, m=m: sb.op(
                        "gpsimd", lambda e: e.dma_start(
                            out=wscr[:, o:o + w],
                            in_=slot.ap().rearrange("p a b -> p (a b)")[:, 0:w]),
                        reads=[wbuf], writes=[B((nm, m))], inc=16))
            else:
                i = fill_i[0] % 2
                fill_i[0] += 1
                stg = PSTG[:, i, :, :]
                sbuf_ = RB("PSTG")
                fill_tile(
                    lambda stg=stg, src=src, sbuf_=sbuf_: sb.op(
                        "gpsimd", lambda e: e.dma_start(out=stg, in_=src),
                        writes=[sbuf_], inc=16),
                    lambda stg=stg, o=o, w=w, sbuf_=sbuf_, nm=nm, m=m: sb.op(
                        "gpsimd", lambda e: e.dma_start(
                            out=wscr[:, o:o + w],
                            in_=stg.rearrange("p a b -> p (a b)")),
                        reads=[sbuf_], writes=[B((nm, m))], inc=16))
    if pending_store[0] is not None:
        pending_store[0]()
        pending_store[0] = None

    def bias_ap(nm, chunk):
        return BIAS[:, bias_off[nm] + chunk:bias_off[nm] + chunk + 1]

    wr_i = [0]
    RING_N = [3]
    wpair_cache = {}

    def ring_slot():
        i = wr_i[0] % RING_N[0]
        wr_i[0] += 1
        slots = WR + [WR3]
        return slots[i], RB(f"WR{i}" if i < 3 else "WR3")

    def wtile_pair(nm, m0):
        key = (nm, m0)
        if key in wpair_cache:
            return wpair_cache[key]
        slot, wbuf = ring_slot()
        o, w = wcols[(nm, m0)]
        o2, w2 = wcols[(nm, m0 + 1)]
        assert o2 == o + w
        sb.op("sync", lambda e, slot=slot, o=o, w=w, w2=w2: e.dma_start(
            out=slot.ap().rearrange("p a b -> p (a b)")[:, 0:w + w2],
            in_=wscr[:, o:o + w + w2]),
            reads=[B((nm, m0)), B((nm, m0 + 1))], writes=[wbuf], inc=16)
        wpair_cache.clear()
        wpair_cache[key] = (slot, wbuf)
        return slot, wbuf

    def wtile_small(nm, m):
        m0 = m - (m % 2)
        slot, wbuf = wtile_pair(nm, m0)
        sub = slot.ap()[:, 8 * (m % 2):8 * (m % 2) + NCH, :]
        return sub, wbuf

    def wtile_large(nm, m, half):
        slot, wbuf = ring_slot()
        o, w = wcols[(nm, m, half)]
        sb.op("sync", lambda e, slot=slot, o=o, w=w: e.dma_start(
            out=slot.ap().rearrange("p a b -> p (a b)"), in_=wscr[:, o:o + w]),
            reads=[B((nm, m, half))], writes=[wbuf], inc=16)
        wpair_cache.clear()
        return slot, wbuf

    def newton_rsqrt(src, dst, tmp, rbufs, n):
        rbuf = rbufs
        I32 = mybir.dt.int32
        sb.op("vector", lambda e: e.tensor_single_scalar(
            dst.bitcast(I32), src.bitcast(I32), 1, ALU.arith_shift_right),
            reads=rbuf, writes=rbuf)
        sb.op("vector", lambda e: e.tensor_scalar(
            dst.bitcast(I32), dst.bitcast(I32), -1, 0x5F3759DF, ALU.mult, ALU.add),
            reads=rbuf, writes=rbuf)
        for _ in range(2):
            sb.op("vector", lambda e: e.tensor_mul(tmp, dst, dst),
                  reads=rbuf, writes=rbuf)
            sb.op("vector", lambda e: e.tensor_mul(tmp, tmp, src),
                  reads=rbuf, writes=rbuf)
            sb.op("vector", lambda e: e.tensor_scalar(tmp, tmp, -0.5, 1.5,
                                                      ALU.mult, ALU.add),
                  reads=rbuf, writes=rbuf)
            sb.op("vector", lambda e: e.tensor_mul(dst, dst, tmp),
                  reads=rbuf, writes=rbuf)

    def ln_pre(src_t, src_buf, lo, n):
        xbg, sqg = RB("XBG"), RB("SQG")
        sb.op("scalar", lambda e: e.activation(
            XBG[:, :, 0:n], src_t[:, :, lo:lo + n], AF.Copy),
            reads=[src_buf], writes=[xbg])
        sb.op("scalar", lambda e: e.activation(
            SQG[:, :, 0:n], src_t[:, :, lo:lo + n], AF.Square),
            reads=[src_buf], writes=[sqg])

    def ln_stats(src_t, src_buf, lo, n, eps, pre_done=False):
        stat = B("STAT_ROWS")
        xbg, sqg = RB("XBG"), RB("SQG")
        if not pre_done:
            ln_pre(src_t, src_buf, lo, n)
        nblk = (n + NB - 1) // NB
        for bi in range(nblk):
            c0, cw = bi * NB, min(NB, n - bi * NB)
            for c in range(NCH):
                sb.op("tensor", lambda e, c=c, c0=c0, cw=cw: e.matmul(
                    PS[0][0:1, 0:cw], ONESC[:], XBG[:, c, c0:c0 + cw],
                    start=(c == 0), stop=(c == NCH - 1)),
                    reads=[xbg, B("ONESC")], writes=[psb[0]])
            for c in range(NCH):
                sb.op("tensor", lambda e, c=c, c0=c0, cw=cw: e.matmul(
                    PS[1][0:1, 0:cw], ONESC[:], SQG[:, c, c0:c0 + cw],
                    start=(c == 0), stop=(c == NCH - 1)),
                    reads=[sqg, B("ONESC")], writes=[psb[1]])
            epsc = EPS5C if eps > 1e-7 else EPS10C
            sb.op("scalar", lambda e, c0=c0, cw=cw: e.activation(
                ST_S[0:1, c0:c0 + cw], PS[0][0:1, 0:cw], AF.Identity,
                scale=1.0 / C),
                reads=[psb[0]], writes=[stat])
            sb.op("scalar", lambda e, c0=c0, cw=cw, epsc=epsc: e.activation(
                ST_Q[0:1, c0:c0 + cw], PS[1][0:1, 0:cw], AF.Identity,
                bias=epsc[0:1, 0:1], scale=1.0 / C),
                reads=[psb[1], B(epsc.name)], writes=[stat])
        sn = ST_S[0:1, 0:n]
        qn = ST_Q[0:1, 0:n]
        rn = ST_R[0:1, 0:n]
        tn = ST_T[0:1, 0:n]
        sb.op("vector", lambda e: e.tensor_mul(tn, sn, sn),
              reads=[stat, sqg], writes=[stat, sqg])
        sb.op("vector", lambda e: e.tensor_sub(qn, qn, tn),
              reads=[stat, sqg], writes=[stat])
        newton_rsqrt(qn, rn, tn, [stat, sqg], n)
        sb.op("vector", lambda e: e.tensor_copy(STB_M[0:1, 0:n], sn),
              reads=[stat], writes=[stat])
        sb.op("vector", lambda e: e.tensor_copy(STB_R[0:1, 0:n], rn),
              reads=[stat], writes=[stat])

    def ln_finish(src_t, src_buf, lo, n, dst_t, dst_buf, dst_lo=None,
                  f32_dst=False):
        if dst_lo is None:
            dst_lo = 0
        stat = B("STAT_ROWS")
        mrb = RB("MRB")
        nblk = (n + NB - 1) // NB
        for bi in range(nblk):
            c0, cw = bi * NB, min(NB, n - bi * NB)
            sb.op("tensor", lambda e, c0=c0, cw=cw: e.matmul(
                PS[2][:, 0:cw], ONESR[:], STB_M[0:1, c0:c0 + cw],
                start=True, stop=True), reads=[stat, B("ONESR")], writes=[psb[2]])
            sb.op("tensor", lambda e, c0=c0, cw=cw: e.matmul(
                PS[3][:, 0:cw], ONESR[:], STB_R[0:1, c0:c0 + cw],
                start=True, stop=True), reads=[stat, B("ONESR")], writes=[psb[3]])
            sb.op("scalar", lambda e, c0=c0, cw=cw: e.activation(
                MRB[:, 0, c0:c0 + cw], PS[2][:, 0:cw], AF.Copy),
                reads=[psb[2]], writes=[mrb])
            sb.op("scalar", lambda e, c0=c0, cw=cw: e.activation(
                MRB[:, 1, c0:c0 + cw], PS[3][:, 0:cw], AF.Copy),
                reads=[psb[3]], writes=[mrb])
        if dst_t is UG and not f32_dst:
            for c in range(NCH):
                sb.op("vector", lambda e, c=c: e.tensor_sub(
                    dst_t[:, c, dst_lo:dst_lo + n], src_t[:, c, lo:lo + n],
                    MRB[:, 0, 0:n]),
                    reads=[src_buf, mrb], writes=[UGC[c]])
                sb.op("vector", lambda e, c=c: e.tensor_mul(
                    dst_t[:, c, dst_lo:dst_lo + n],
                    dst_t[:, c, dst_lo:dst_lo + n], MRB[:, 1, 0:n]),
                    reads=[mrb, UGC[c]], writes=[UGC[c]])
            return
        mb = MRB[:, 0, 0:n].unsqueeze(1).broadcast_to([128, NCH, n])
        rb = MRB[:, 1, 0:n].unsqueeze(1).broadcast_to([128, NCH, n])
        sb.op("vector", lambda e: e.tensor_sub(
            dst_t[:, :, dst_lo:dst_lo + n], src_t[:, :, lo:lo + n], mb),
            reads=[src_buf, mrb], writes=[dst_buf])
        sb.op("vector", lambda e: e.tensor_mul(
            dst_t[:, :, dst_lo:dst_lo + n], dst_t[:, :, dst_lo:dst_lo + n], rb),
            reads=[mrb, dst_buf], writes=[dst_buf])

    def layernorm(src_t, src_buf, lo, n, eps, dst_t, dst_buf, dst_lo=None,
                  f32_dst=False, pre_done=False):
        ln_stats(src_t, src_buf, lo, n, eps, pre_done=pre_done)
        ln_finish(src_t, src_buf, lo, n, dst_t, dst_buf, dst_lo=dst_lo,
                  f32_dst=f32_dst)

    def proj_fm(nm_wt, src_t, src_buf, n, mlist, evac, banks=(4, 5, 6, 7),
                src_lo=0, post_m=None):
        per_chunk = src_t is UG
        """For each m in mlist: psum = Wt_m^T @ src[:, :, src_lo:src_lo+n]
        (accumulated over K chunks), then evac(m, c0, cw, bank)."""
        kdim = wts[nm_wt].shape[0]
        nblk = (n + NB - 1) // NB
        bi_rr = [0]
        for m in mlist:
            if kdim == C:
                slot, wbuf = wtile_small(nm_wt, m)
                halves = [(slot, wbuf, 0, NCH)]
            else:
                s0, b0 = wtile_large(nm_wt, m, 0)
                s1, b1 = wtile_large(nm_wt, m, 1)
                halves = [(s0, b0, 0, 16), (s1, b1, 16, 16)]
            for bi in range(nblk):
                c0, cw = bi * NB, min(NB, n - bi * NB)
                p = banks[bi_rr[0] % len(banks)]
                bi_rr[0] += 1
                nk = kdim // 128
                ki = 0
                for slot, wbuf, kofs, kn in halves:
                    for k in range(kn):
                        sbf_ = UGC[kofs + k] if per_chunk else src_buf
                        sb.op("tensor", lambda e, slot=slot, k=k, kofs=kofs,
                              c0=c0, cw=cw, p=p, ki=ki: e.matmul(
                                  PS[p][:, 0:cw], slot[:, k, :],
                                  src_t[:, kofs + k, src_lo + c0:src_lo + c0 + cw],
                                  start=(ki == 0), stop=(ki == nk - 1)),
                              reads=[sbf_, wbuf], writes=[psb[p]])
                        ki += 1
                evac(m, c0, cw, p)
            if post_m is not None:
                post_m()

    def evac_act(dst_t, dst_buf, nm_bias, func=None, dst_lo=0, scale=1.0):
        def ev(m, c0, cw, p):
            bias = bias_ap(nm_bias, m) if nm_bias else 0.0
            nonlocal func
            f = func if func is not None else (AF.Identity if nm_bias else AF.Copy)
            sb.op("scalar", lambda e, m=m, c0=c0, cw=cw, p=p, f=f, bias=bias:
                  e.activation(
                      dst_t[:, m, dst_lo + c0:dst_lo + c0 + cw], PS[p][:, 0:cw],
                      f, bias=bias, scale=scale),
                  reads=[psb[p], B("BIAS")], writes=[dst_buf])
        return ev

    dbg_outs = []

    def tap(name, sb_tensor, shape, buf):
        if not dbg:
            return
        d = nc.dram_tensor("dbg_" + name, shape, F32, kind="ExternalOutput").ap()
        rb = [B(f"XS_{i}") for i in range(G)] if sb_tensor is XS else [buf]
        sb.op("gpsimd", lambda e: e.dma_start(out=d, in_=sb_tensor[:, :, :]),
              reads=rb, inc=16)
        dbg_outs.append(name)

    def ev_stage_add_xs(nm_bias, xs_lo):
        def ev(m, c0, cw, p):
            sl = m % 2
            sb.op("scalar", lambda e, m=m, c0=c0, cw=cw, p=p, sl=sl: e.activation(
                RS32[:, sl, 0:cw], PS[p][:, 0:cw], AF.Identity,
                bias=bias_ap(nm_bias, m)),
                reads=[psb[p], B("BIAS")], writes=[RB("RS32")])
            sb.op("vector", lambda e, m=m, c0=c0, cw=cw, sl=sl: e.tensor_add(
                XS[:, m, xs_lo + c0:xs_lo + c0 + cw],
                XS[:, m, xs_lo + c0:xs_lo + c0 + cw], RS32[:, sl, 0:cw]),
                reads=[RB("RS32"), xsg(xs_lo)], writes=[xsg(xs_lo)])
        return ev

    def phase_A(s):
        RING_N[0] = 3
        for g in range(G):
            lo = g * GT
            src = x_d[s, :, lo:lo + GT].rearrange("(c p) t -> p c t", p=128)
            sb.op("sync", lambda e, src=src, lo=lo:
                  e.dma_start(out=XS[:, :, lo:lo + GT], in_=src),
                  writes=[xsg(lo)], inc=16)
        ln_pre(XS, xsg(0), 0, GT)
        for g in range(G):
            lo = g * GT
            layernorm(XS, xsg(lo), lo, GT, 1e-5, XS, xsg(lo), dst_lo=lo,
                      f32_dst=True, pre_done=True)
            if g + 1 < G:
                ln_pre(XS, xsg(lo + GT), lo + GT, GT)
        if dbg and s == 0:
            tap("t", XS, [128, NCH, T], B("XS"))
        sb.op("sync", lambda e: e.dma_start(
            out=BVRF[:, :], in_=bv_row_d),
            writes=[RB("BVRF")], inc=16)
        sb.op("vector", lambda e: e.tensor_copy(BVR[:, :], BVRF[:, :]),
              reads=[RB("BVRF")], writes=[RB("BVR")])
        ln_pre(XS, xsg(0), 0, GT)
        layernorm(XS, xsg(0), 0, GT, 1e-10, UG, RB("UG"), pre_done=True)
        for g in range(G):
            lo = g * GT
            if g + 1 < G:
                ln_pre(XS, xsg(lo + GT), lo + GT, GT)

            def ev_k(m, c0, cw, p, lo=lo, g=g):
                er = ([B("SPKY"), B("SPKM")]
                      if (g == 0 and m == 8) else [])
                sb.op("scalar", lambda e, m=m, c0=c0, cw=cw, p=p: e.activation(
                    KB[:, m - 8, lo + c0:lo + c0 + cw], PS[p][:, 0:cw],
                    AF.Identity, bias=bias_ap("b_attn", m)),
                    reads=[psb[p], B("BIAS")] + er, writes=[RB("KB")])
            proj_fm("wt_attn", UG, RB("UG"), GT, range(8, 16), ev_k)
            if g + 1 < G:
                ln_stats(XS, xsg(lo + GT), lo + GT, GT, 1e-10, pre_done=True)
            for nb in range(4):
                i = wr_i[0] % len(WR)
                wr_i[0] += 1
                vslot, vbuf = WR[i], RB(f"WR{i}")
                wpair_cache.clear()
                o, w = wcols[("wvs", nb)]
                sb.op("sync", lambda e, vslot=vslot, o=o, w=w: e.dma_start(
                    out=vslot.ap().rearrange("p a b -> p (a b)")[:, 0:w],
                    in_=wscr[:, o:o + w]),
                    reads=[B(("wvs", nb))], writes=[vbuf], inc=16)
                vv = vslot.ap().rearrange("p a b -> p (a b)")[:, 0:2048]
                vv = vv.rearrange("p (c f) -> p c f", c=NCH)
                for ti in range(4):
                    gti = 4 * g + ti
                    p = 4 + ti % 2
                    for k in range(NCH):
                        sb.op("tensor", lambda e, k=k, ti=ti, p=p, vv=vv: e.matmul(
                            PS[p][:, 0:256], UG[:, k, ti * 128:(ti + 1) * 128],
                            vv[:, k, :], start=(k == 0), stop=False),
                            reads=[RB("UG"), vbuf], writes=[psb[p]])
                    sb.op("tensor", lambda e, nb=nb, p=p: e.matmul(
                        PS[p][:, 0:256], ONESR[:, :], BVR[0:1, nb * 256:(nb + 1) * 256],
                        start=False, stop=True),
                        reads=[RB("BVR"), B("ONESR")], writes=[psb[p]])
                    sb.op("scalar", lambda e, gti=gti, nb=nb, p=p: e.activation(
                        VB[:, gti, nb * 256:(nb + 1) * 256], PS[p][:, 0:256], AF.Copy),
                        reads=[psb[p]], writes=[RB("VB")])
            if g + 1 < G:
                ln_finish(XS, xsg(lo + GT), lo + GT, GT, UG, RB("UG"))
        if dbg and s == 0:
            tap("k", KB, [128, NCH, T], RB("KB"))
            tap("v", VB, [128, 16, C], RB("VB"))
        for g in range(G):
            lo = g * GT
            cmax = 4 * g + 4
            layernorm(XS, xsg(lo), lo, GT, 1e-10, UG, RB("UG"))
            if dbg and s == 0 and g == 0:
                tap("ua2g0", UG, [128, NCH, GT], RB("UG"))
            pending_ep = [None]
            for hp in range(NCH):
                qs = hp % 2
                pv = 4 + hp % 2
                slot, wbuf = wtile_small("wt_attn", hp)
                for k in range(NCH):
                    sb.op("tensor", lambda e, slot=slot, k=k: e.matmul(
                        PS[7][:, :], slot[:, k, :], UG[:, k, 0:GT],
                        start=(k == 0), stop=(k == NCH - 1)),
                        reads=[RB("UG"), wbuf], writes=[psb[7]])
                sb.op("scalar", lambda e, hp=hp, qs=qs: e.activation(
                    QP[:, qs, :], PS[7][:, :], AF.Identity,
                    bias=bias_ap("b_attn", hp)),
                    reads=[psb[7], B("BIAS")], writes=[RB("QP")])
                if dbg and s == 0 and g == 0 and hp == 0:
                    tap("qp0", QP, [128, 2, NB], RB("QP"))

                def st_pair(c, hp=hp, qs=qs):
                    pe_ = 2 * (c % 2)
                    po_ = pe_ + 1
                    sb.op("tensor", lambda e, c=c, hp=hp, qs=qs, pe_=pe_: e.matmul(
                        PS[pe_][:, :], KB[0:64, hp, c * 128:(c + 1) * 128],
                        QP[0:64, qs, :], start=True, stop=True,
                        tile_position=(0, 0)),
                        reads=[RB("KB"), RB("QP")], writes=[psb[pe_]])
                    sb.op("tensor", lambda e, c=c, hp=hp, qs=qs, po_=po_: e.matmul(
                        PS[po_][:, :], KB[64:128, hp, c * 128:(c + 1) * 128],
                        QP[64:128, qs, :], start=True, stop=True,
                        tile_position=(64, 0)),
                        reads=[RB("KB"), RB("QP")], writes=[psb[po_]])

                def finish_c(c, hp=hp, pv=pv, cmax=cmax, g=g):
                    pe_ = 2 * (c % 2)
                    po_ = pe_ + 1
                    je, jo = pe_, po_
                    pair = PSA if c % 2 == 0 else PSB
                    sb.op("scalar", lambda e, pair=pair, je=je: e.activation(
                        PHAT[:, je:je + 2, :], pair[:, :, :], AF.Exp,
                        bias=ESHC[:, 0:1], scale=SC),
                        reads=[psb[pe_], psb[po_]], writes=[RB("PHAT")])
                    if c >= 4 * g:
                        k0 = 128 * c - 512 * g
                        ms = MASK[:, 384 - k0:384 - k0 + NB]
                        msb = ms.unsqueeze(1).broadcast_to([128, 2, NB])
                        sb.op("vector", lambda e, je=je, msb=msb: e.tensor_mul(
                            PHAT[:, je:je + 2, :], PHAT[:, je:je + 2, :], msb),
                            reads=[RB("PHAT"), B("MASK")], writes=[RB("PHAT")])
                    st = (c == 0)
                    sp = (c == cmax - 1)
                    sb.op("tensor", lambda e, c=c, hp=hp, je=je, st=st, sp=sp,
                          pv=pv: e.matmul(
                        PS[pv][0:64, :], VB[:, c, 128 * hp:128 * hp + 64],
                        PHAT[:, je, :], start=st, stop=sp, tile_position=(0, 0),
                        skip_group_check=True),
                        reads=[RB("VB"), RB("PHAT")], writes=[psb[pv]])
                    sb.op("tensor", lambda e, c=c, hp=hp, jo=jo, st=st, sp=sp,
                          pv=pv: e.matmul(
                        PS[pv][64:128, :], VB[:, c, 128 * hp + 64:128 * hp + 128],
                        PHAT[:, jo, :], start=st, stop=sp, tile_position=(0, 64),
                        skip_group_check=True),
                        reads=[RB("VB"), RB("PHAT")], writes=[psb[pv]])
                    sb.op("tensor", lambda e, je=je, st=st, sp=sp: e.matmul(
                        PS[6][0:1, :], ONESC[:, :], PHAT[:, je, :],
                        start=st, stop=sp, tile_position=(0, 0),
                        skip_group_check=True),
                        reads=[B("ONESC"), RB("PHAT")], writes=[psb[6]])
                    sb.op("tensor", lambda e, jo=jo, st=st, sp=sp: e.matmul(
                        PS[6][32:33, :], ONESC[:, :], PHAT[:, jo, :],
                        start=st, stop=sp, tile_position=(0, 32),
                        skip_group_check=True),
                        reads=[B("ONESC"), RB("PHAT")], writes=[psb[6]])

                st_pair(0)
                if pending_ep[0] is not None:
                    pending_ep[0]()
                    pending_ep[0] = None
                for c in range(cmax):
                    if c + 1 < cmax:
                        st_pair(c + 1)
                    finish_c(c)

                def epilogue(hp=hp, pv=pv):
                    sb.op("vector", lambda e: e.tensor_copy(
                        SRND[0:1, :], PS[6][0:1, :]),
                        reads=[psb[6]], writes=[RB("SRND")])
                    sb.op("vector", lambda e: e.tensor_copy(
                        SRND[32:33, :], PS[6][32:33, :]),
                        reads=[psb[6]], writes=[RB("SRND")])
                    sb.op("vector", lambda e: e.reciprocal(
                        SRND[0:33, :], SRND[0:33, :]),
                        reads=[RB("SRND")], writes=[RB("SRND")])
                    sb.op("vector", lambda e: e.tensor_copy(
                        SRNDB[0:33, :], SRND[0:33, :]),
                        reads=[RB("SRND")], writes=[RB("SRND")])
                    sb.op("vector", lambda e: e.tensor_copy(
                        SRNDB2[0:1, :], SRNDB[32:33, :]),
                        reads=[RB("SRND")], writes=[RB("SRND")])
                    sb.op("tensor", lambda e: e.matmul(
                        PS[7][0:64, :], ONESR[0:1, 0:64], SRNDB[0:1, :],
                        start=True, stop=True, tile_position=(0, 0)),
                        reads=[RB("SRND"), B("ONESR")], writes=[psb[7]])
                    sb.op("tensor", lambda e: e.matmul(
                        PS[7][64:128, :], ONESR[0:1, 0:64], SRNDB2[0:1, :],
                        start=True, stop=True, tile_position=(0, 64)),
                        reads=[RB("SRND"), B("ONESR")], writes=[psb[7]])
                    sb.op("scalar", lambda e: e.activation(
                        RSB[:, :], PS[7][:, :], AF.Copy),
                        reads=[psb[7]], writes=[B("RSB")])
                    sb.op("vector", lambda e, hp=hp, pv=pv: e.tensor_mul(
                        YG[:, hp, :], PS[pv][:, :], RSB[:, :]),
                        reads=[psb[pv], B("RSB")], writes=[RB("YG")])
                pending_ep[0] = epilogue
            pending_ep[0]()
            if dbg and s == 0 and g == 0:
                tap("yg0", YG, [128, NCH, GT], RB("YG"))
            proj_fm("wt_proj", YG, RB("YG"), GT, range(NCH),
                    ev_stage_add_xs("b_proj", lo))
        if dbg and s == 0:
            tap("t2", XS, [128, NCH, T], B("XS"))
        RING_N[0] = 4
        ln_pre(XS, xsg(0), 0, GTF)
        layernorm(XS, xsg(0), 0, GTF, 1e-10, UG, RB("UG"), pre_done=True)
        for gf in range(GF):
            lo = gf * GTF
            if gf + 1 < GF:
                ln_pre(XS, xsg(lo + GTF), lo + GTF, GTF)
            if dbg and s == 0 and gf == 0:
                tap("u2g0", UG, [128, NCH, GT], RB("UG"))
            proj_fm("wt_fc", UG, RB("UG"), GTF, range(FCH),
                    evac_act(HBF, RB("HBF"), "b_fc", func=AF.Gelu_apprx_tanh))
            if gf + 1 < GF:
                ln_stats(XS, xsg(lo + GTF), lo + GTF, GTF, 1e-10, pre_done=True)
            if dbg and s == 0 and gf == 0:
                tap("hg0", HBF, [128, FCH, GTF], RB("HBF"))
            proj_fm("wt_mlp", HBF, RB("HBF"), GTF, range(NCH),
                    ev_stage_add_xs("b_mlp", lo))
            if gf + 1 < GF:
                ln_finish(XS, xsg(lo + GTF), lo + GTF, GTF, UG, RB("UG"))
            if dbg and s == 0 and gf == 0:
                tap("t3g0", XS, [128, NCH, T], B("XS"))
            for c in range(NCH):
                src = x_d[s, c * 128:(c + 1) * 128, lo:lo + GTF]
                sl = c % 2
                sb.op("gpsimd", lambda e, src=src, sl=sl:
                      e.dma_start(out=XGC[:, sl, :], in_=src),
                      writes=[RB("XGC")], inc=16)
                sb.op("vector", lambda e, c=c, sl=sl, lo=lo: e.tensor_add(
                    XS[:, c, lo:lo + GTF], XS[:, c, lo:lo + GTF], XGC[:, sl, :]),
                    reads=[RB("XGC"), xsg(lo)], writes=[xsg(lo)])

    def phase_FFN(s, last):
        RING_N[0] = 4
        ln_pre(XS, xsg(0), 0, GTF)
        layernorm(XS, xsg(0), 0, GTF, 1e-5, UG, RB("UG"), pre_done=True)
        pend = []

        def drain():
            for _ in range(2):
                if pend:
                    pend.pop(0)()
        for gf in range(GF):
            lo = gf * GTF
            xsb = xsg(lo)
            if gf + 1 < GF:
                ln_pre(XS, xsg(lo + GTF), lo + GTF, GTF)

            def ev_relu2(m, c0, cw, p):
                sl = m % 2
                sb.op("scalar", lambda e, m=m, cw=cw, p=p, sl=sl: e.activation(
                    RS32[:, sl, 0:cw], PS[p][:, 0:cw], AF.Relu),
                    reads=[psb[p]], writes=[RB("RS32")])
                sb.op("vector", lambda e, m=m, cw=cw, sl=sl: e.tensor_mul(
                    HBF[:, m, 0:cw], RS32[:, sl, 0:cw], RS32[:, sl, 0:cw]),
                    reads=[RB("RS32")], writes=[RB("HBF")])
            proj_fm("wt_ffn_fc", UG, RB("UG"), GTF, range(FCH), ev_relu2,
                    post_m=drain)
            if gf + 1 < GF:
                ln_stats(XS, xsg(lo + GTF), lo + GTF, GTF, 1e-5, pre_done=True)
            while pend:
                pend.pop(0)()

            yb = RB("SPKY")
            extra0 = [B("KB")] if gf == 0 else []

            def ev_y(m, c0, cw, p):
                er = extra0 if m == 0 else []
                sb.op("scalar", lambda e, m=m, cw=cw, p=p: e.activation(
                    SPKY[:, m, 0:cw], PS[p][:, 0:cw], AF.Copy),
                    reads=[psb[p]] + er, writes=[yb])
            proj_fm("wt_ffn_proj", HBF, RB("HBF"), GTF, range(NCH), ev_y)
            if gf + 1 < GF:
                ln_finish(XS, xsg(lo + GTF), lo + GTF, GTF, UG, RB("UG"))
            mb, sbb = B("SPKM"), RB("SBF")
            for h in (0, 1):
                Y = SPKY[:, 4 * h:4 * h + 4, :]
                M_ = SPKM[:, 4 * h:4 * h + 4, :]
                S_ = SBF[:, :, :]
                XSg = XS[:, 4 * h:4 * h + 4, lo:lo + GTF]
                ONE = EONEC[:, 0:1].unsqueeze(1).broadcast_to([128, 4, GTF])
                er0 = extra0 if (gf == 0 and h == 0) else []

                def gt(src, src_b, S_=S_, ONE=ONE):
                    pend.append(lambda src=src, src_b=src_b: sb.op(
                        "vector", lambda e: e.tensor_tensor(
                            S_, src, ONE, ALU.is_gt),
                        reads=[src_b, B("EONEC")], writes=[sbb]))

                def acc_xs(XSg=XSg, S_=S_, xsb=xsb):
                    pend.append(lambda: sb.op(
                        "vector", lambda e: e.scalar_tensor_tensor(
                            XSg, S_, 0.2, XSg, ALU.mult, ALU.add),
                        reads=[sbb, xsb], writes=[xsb]))

                gt(Y, yb)
                acc_xs()
                pend.append(lambda M_=M_, Y=Y, er0=er0: sb.op(
                    "vector", lambda e: e.scalar_tensor_tensor(
                        M_, Y, 0.9, Y, ALU.mult, ALU.add),
                    reads=[yb] + er0, writes=[mb]))
                pend.append(lambda M_=M_, S_=S_: sb.op(
                    "vector", lambda e: e.tensor_sub(M_, M_, S_),
                    reads=[sbb, mb], writes=[mb]))
                for _t in range(3):
                    gt(M_, mb)
                    acc_xs()
                    pend.append(lambda M_=M_, Y=Y: sb.op(
                        "vector", lambda e: e.scalar_tensor_tensor(
                            M_, M_, 0.9, Y, ALU.mult, ALU.add),
                        reads=[yb, mb], writes=[mb]))
                    pend.append(lambda M_=M_, S_=S_: sb.op(
                        "vector", lambda e: e.tensor_sub(M_, M_, S_),
                        reads=[sbb, mb], writes=[mb]))
                gt(M_, mb)
                acc_xs()
        while pend:
            pend.pop(0)()
        if last:
            for gf in range(GF):
                lo = gf * GTF
                dst = y_d[s, :, lo:lo + GTF].rearrange("(c p) t -> p c t", p=128)
                sb.op("gpsimd", lambda e, dst=dst, lo=lo: e.dma_start(
                    out=dst, in_=XS[:, :, lo:lo + GTF]), reads=[xsg(lo)], inc=16)

    def phase_C(s):
        RING_N[0] = 3
        for g in range(G):
            lo = g * GT
            layernorm(XS, xsg(lo), lo, GT, 1e-5, UG, RB("UG"))
            proj_fm("wt_p", UG, RB("UG"), GT, range(NCH),
                    evac_act(XBG, RB("XBG"), "b_p"))

            def ev_into(dst_t, dst_b, nmb, lo=lo, g=g):
                def ev(m, c0, cw, p):
                    er = ([B("SPKY"), B("SPKM")]
                          if (g == 0 and m == 0) else [])
                    sb.op("scalar", lambda e, m=m, c0=c0, cw=cw, p=p: e.activation(
                        dst_t[:, m, lo + c0:lo + c0 + cw], PS[p][:, 0:cw],
                        AF.Identity, bias=bias_ap(nmb, m)),
                        reads=[psb[p], B("BIAS")] + er, writes=[dst_b])
                return ev
            proj_fm("wt_k", XBG, RB("XBG"), GT, range(NCH),
                    ev_into(KB, RB("KB"), "b_k"))
            proj_fm("wt_v", XBG, RB("XBG"), GT, range(NCH),
                    ev_into(VC, RB("VB"), "b_v"))
        for g in range(G):
            lo = g * GT
            layernorm(XS, xsg(lo), lo, GT, 1e-5, UG, RB("UG"))
            proj_fm("wt_p", UG, RB("UG"), GT, range(NCH),
                    evac_act(XBG, RB("XBG"), "b_p"))
            proj_fm("wt_q", XBG, RB("XBG"), GT, range(NCH),
                    evac_act(YG, RB("YG"), "b_q"))
            for n_idx, dlt in ((0, -1), (1, 0), (2, 1)):
                for c in range(NCH):
                    jj = c % 2
                    lo_s = lo + dlt
                    if 0 <= lo_s and lo_s + GT <= T:
                        sb.op("vector", lambda e, c=c, jj=jj, lo_s=lo_s: e.tensor_mul(
                            PHAT[:, jj, :], YG[:, c, :], KB[:, c, lo_s:lo_s + GT]),
                            reads=[RB("YG"), RB("KB")], writes=[RB("PHAT")])
                    elif lo_s < 0:
                        sb.op("vector", lambda e, c=c, jj=jj: e.tensor_mul(
                            PHAT[:, jj, 1:GT], YG[:, c, 1:GT], KB[:, c, 0:GT - 1]),
                            reads=[RB("YG"), RB("KB")], writes=[RB("PHAT")])
                        sb.op("vector", lambda e, c=c, jj=jj: e.tensor_mul(
                            PHAT[:, jj, 0:1], YG[:, c, 0:1], KB[:, c, T - 1:T]),
                            reads=[RB("YG"), RB("KB")], writes=[RB("PHAT")])
                    else:
                        sb.op("vector", lambda e, c=c, jj=jj, lo_s=lo_s: e.tensor_mul(
                            PHAT[:, jj, 0:GT - 1], YG[:, c, 0:GT - 1],
                            KB[:, c, lo_s:T]),
                            reads=[RB("YG"), RB("KB")], writes=[RB("PHAT")])
                        sb.op("vector", lambda e, c=c, jj=jj: e.tensor_mul(
                            PHAT[:, jj, GT - 1:GT], YG[:, c, GT - 1:GT],
                            KB[:, c, 0:1]),
                            reads=[RB("YG"), RB("KB")], writes=[RB("PHAT")])
                    sb.op("tensor", lambda e, c=c, jj=jj, n_idx=n_idx: e.matmul(
                        PS[n_idx][0:16, :], EIND[:, c, :], PHAT[:, jj, :],
                        start=(c == 0), stop=(c == NCH - 1)),
                        reads=[RB("PHAT"), B("EIND")], writes=[psb[n_idx]])
            ABSn = (ABS0, ABS1, ABS2)
            for n in range(3):
                sb.op("vector", lambda e, n=n: e.tensor_copy(
                    ABSn[n][:, :], PS[n][0:16, :]),
                    reads=[psb[n]], writes=[RB("ABS0")])
            sb.op("vector", lambda e: e.tensor_max(ABSM[:, :], ABS0[:, :], ABS1[:, :]),
                  reads=[RB("ABS0")], writes=[RB("ABSM")])
            sb.op("vector", lambda e: e.tensor_max(ABSM[:, :], ABSM[:, :], ABS2[:, :]),
                  reads=[RB("ABS0")], writes=[RB("ABSM")])
            for n in range(3):
                sb.op("vector", lambda e, n=n: e.tensor_sub(
                    ABSn[n][:, :], ABSn[n][:, :], ABSM[:, :]),
                    reads=[RB("ABS0"), RB("ABSM")], writes=[RB("ABS0")])
                sb.op("scalar", lambda e, n=n: e.activation(
                    ABSn[n][:, :], ABSn[n][:, :], AF.Exp, scale=SC),
                    reads=[RB("ABS0")], writes=[RB("ABS0")])
            sb.op("vector", lambda e: e.tensor_add(ABSM[:, :], ABS0[:, :], ABS1[:, :]),
                  reads=[RB("ABS0")], writes=[RB("ABSM")])
            sb.op("vector", lambda e: e.tensor_add(ABSM[:, :], ABSM[:, :], ABS2[:, :]),
                  reads=[RB("ABS0")], writes=[RB("ABSM")])
            sb.op("vector", lambda e: e.reciprocal(
                RS32[0:16, 0, :], ABSM[:, :]),
                reads=[RB("ABSM")], writes=[RB("RS32")])
            for n in range(3):
                sb.op("vector", lambda e, n=n: e.tensor_mul(
                    WROW[n][:, :], ABSn[n][:, :], RS32[0:16, 0, :]),
                    reads=[RB("ABS0"), RB("RS32")], writes=[RB("WROW0")])
            for c in range(NCH):
                for n, dlt in ((0, -1), (1, 0), (2, 1)):
                    wslot = 2 + n % 2
                    sb.op("tensor", lambda e, c=c, n=n, wslot=wslot: e.matmul(
                        PS[3][:, :], FIND[:, c, :], WROW[n][:, :],
                        start=True, stop=True),
                        reads=[RB("WROW0"), B("FIND")], writes=[psb[3]])
                    sb.op("scalar", lambda e, wslot=wslot: e.activation(
                        PHAT[:, wslot, :], PS[3][:, :], AF.Copy),
                        reads=[psb[3]], writes=[RB("PHAT")])
                    lo_s = lo + dlt
                    dsti = 0 if n == 0 else 1
                    dst = RS32[:, dsti, :]

                    def ml(out_ap, w_ap, v_ap, extra_reads=()):
                        sb.op("vector", lambda e, out_ap=out_ap, w_ap=w_ap,
                              v_ap=v_ap: e.tensor_mul(out_ap, w_ap, v_ap),
                              reads=[RB("PHAT"), RB("VB")], writes=[RB("RS32")])
                    if 0 <= lo_s and lo_s + GT <= T:
                        ml(dst, PHAT[:, wslot, :], VC[:, c, lo_s:lo_s + GT])
                    elif lo_s < 0:
                        ml(RS32[:, dsti, 1:GT], PHAT[:, wslot, 1:GT],
                           VC[:, c, 0:GT - 1])
                        ml(RS32[:, dsti, 0:1], PHAT[:, wslot, 0:1],
                           VC[:, c, T - 1:T])
                    else:
                        ml(RS32[:, dsti, 0:GT - 1], PHAT[:, wslot, 0:GT - 1],
                           VC[:, c, lo_s:T])
                        ml(RS32[:, dsti, GT - 1:GT], PHAT[:, wslot, GT - 1:GT],
                           VC[:, c, 0:1])
                    if n == 1:
                        sb.op("vector", lambda e: e.tensor_add(
                            RS32[:, 0, :], RS32[:, 0, :], RS32[:, 1, :]),
                            reads=[RB("RS32")], writes=[RB("RS32")])
                    elif n == 2:
                        sb.op("vector", lambda e, c=c: e.tensor_add(
                            UG[:, c, :], RS32[:, 0, :], RS32[:, 1, :]),
                            reads=[RB("RS32")], writes=[RB("UG")])
            proj_fm("wt_o", UG, RB("UG"), GT, range(NCH),
                    ev_stage_add_xs("b_o", lo))

    import os as _os
    order = _os.environ.get("MK_PHASES", "ABCD")
    for s in range(SEQS):
        for ph in order:
            if ph == "A":
                phase_A(s)
                if s == 0:
                    tap("xs_A", XS, [128, NCH, T], B("XS"))
            elif ph == "B":
                phase_FFN(s, last=False)
                if s == 0:
                    tap("xs_B", XS, [128, NCH, T], B("XS"))
            elif ph == "C":
                phase_C(s)
                if s == 0:
                    tap("xs_C", XS, [128, NCH, T], B("XS"))
            elif ph == "D":
                phase_FFN(s, last=True)
                if s == 0:
                    tap("xs_D", XS, [128, NCH, T], B("XS"))
    sb.plan["sync"].append(Wait("sync", sb.cnt["sync"]))
    sb.plan["sync"].append(Wait("gpsimd", sb.cnt["gpsimd"]))

    sems = {e: nc.alloc_semaphore(f"sem_{e}") for e in ENGS}
    nops = {e: sum(1 for it in sb.plan[e] if isinstance(it, Op)) for e in ENGS}
    print("plan ops:", nops, "waits:",
          {e: len(sb.plan[e]) - nops[e] for e in ENGS})

    import os as _os2
    _imap = {}

    def replay(name):
        def run(eng):
            for idx, it in enumerate(sb.plan[name]):
                if isinstance(it, Wait):
                    ins = eng.wait_ge(sems[it.src], it.val)
                    if _os2.environ.get("MK_IMAP"):
                        _imap[ins.ins.name] = (name, idx, f"WAIT {it.src}>={it.val}")
                else:
                    ins = it.fn(eng)
                    ins.then_inc(sems[name], it.inc)
                    if _os2.environ.get("MK_IMAP"):
                        _imap[ins.ins.name] = (name, idx, "op")
        return run

    with nc.Block() as block:
        block.tensor(replay("tensor"))
        block.vector(replay("vector"))
        block.scalar(replay("scalar"))
        block.gpsimd(replay("gpsimd"))
        block.sync(replay("sync"))
    nc._mk_imap = _imap
    nc._mk_plan = sb.plan
    return nc


_NC_CACHE = {}


def kernel(**inputs):
    def tr(a):
        return np.ascontiguousarray(np.asarray(a, dtype=np.float32).T)

    def vec(a):
        a = np.asarray(a, dtype=np.float32)
        return np.ascontiguousarray(a.reshape(-1, 128).T)

    x = np.asarray(inputs["L0_attn_a_x"], dtype=np.float32)
    xT = np.ascontiguousarray(np.swapaxes(x, 1, 2))
    common = {
        "wt_attn": tr(inputs["L0_attn_a_w_attn"]),
        "wt_proj": tr(inputs["L0_attn_a_w_proj"]),
        "wt_fc": tr(inputs["L0_attn_a_w_fc"]),
        "wt_mlp": tr(inputs["L0_attn_a_w_mlp_proj"]),
        "wt_p": tr(inputs["L0_attn_b_Wp"]),
        "wt_q": tr(inputs["L0_attn_b_Wq"]),
        "wt_k": tr(inputs["L0_attn_b_Wk"]),
        "wt_v": tr(inputs["L0_attn_b_Wv"]),
        "wt_o": tr(inputs["L0_attn_b_Wo"]),
        "wt_ffn_fc": tr(inputs["L0_ffn_W_fc"]),
        "wt_ffn_proj": tr(inputs["L0_ffn_W_proj"]),
        "b_attn": vec(inputs["L0_attn_a_b_attn"]),
        "b_proj": vec(inputs["L0_attn_a_b_proj"]),
        "b_fc": vec(inputs["L0_attn_a_b_fc"]),
        "b_mlp": vec(inputs["L0_attn_a_b_mlp_proj"]),
        "b_p": vec(inputs["L0_attn_b_bp"]),
        "b_q": vec(inputs["L0_attn_b_bq"]),
        "b_k": vec(inputs["L0_attn_b_bk"]),
        "b_v": vec(inputs["L0_attn_b_bv"]),
        "b_o": vec(inputs["L0_attn_b_bo"]),
        "b_attn_v": np.ascontiguousarray(
            np.asarray(inputs["L0_attn_a_b_attn"], dtype=np.float32)[2048:].reshape(1, -1)),
    }
    if "nc" not in _NC_CACHE:
        _NC_CACHE["nc"] = build_program()
    nc = _NC_CACHE["nc"]
    in_maps = [dict(common, x=np.ascontiguousarray(xT[2 * c:2 * c + 2]))
               for c in range(NCORES)]
    res = run_bass_kernel_spmd(nc, in_maps, core_ids=list(range(NCORES)))
    yT = np.concatenate([np.asarray(r["y"], dtype=np.float32)
                         for r in res.results], axis=0)
    return np.ascontiguousarray(np.swapaxes(yT, 1, 2))


# version 27
# speedup vs baseline: 1.0684x; 1.0350x over previous
import numpy as np
import ml_dtypes
import concourse.bass as bass
import concourse.mybir as mybir
from concourse.bass_utils import run_bass_kernel_spmd

F32 = mybir.dt.float32
BF16 = mybir.dt.bfloat16
AF = mybir.ActivationFunctionType
ALU = mybir.AluOpType
BF = ml_dtypes.bfloat16

NCORES = 8
SEQS = 2
T = 2048
C = 1024
NCH = 8
H = 16
DH = 64
FC = 4096
FCH = 32
G = 4
GT = 512
GF = 4
GTF = 512
NB = 512
ESH = -24.0
SC = 0.125

ENGS = ("tensor", "vector", "scalar", "gpsimd", "sync")


class Buf:
    __slots__ = ("name", "writer", "readers")

    def __init__(self, name):
        self.name = name
        self.writer = None
        self.readers = []


class Op:
    __slots__ = ("fn", "inc", "line")

    def __init__(self, fn, inc):
        self.fn = fn
        self.inc = inc
        import sys as _s
        f = _s._getframe(2)
        self.line = f.f_lineno


class Wait:
    __slots__ = ("src", "val")

    def __init__(self, src, val):
        self.src = src
        self.val = val


class SB:

    def __init__(self):
        self.plan = {e: [] for e in ENGS}
        self.cnt = {e: 0 for e in ENGS}
        self.waited = {e: {} for e in ENGS}

    def op(self, eng, fn, reads=(), writes=(), inc=1):
        need = {}
        dma_q = eng in ("sync", "gpsimd")

        def add(pe, pv):
            if (pe != eng or dma_q) and pv > need.get(pe, 0):
                need[pe] = pv

        for b in reads:
            if b.writer is not None:
                add(*b.writer)
        for b in writes:
            if b.writer is not None:
                add(*b.writer)
            for r in b.readers:
                add(*r)
        for pe in sorted(need):
            pv = need[pe]
            if self.waited[eng].get(pe, 0) < pv:
                self.plan[eng].append(Wait(pe, pv))
                self.waited[eng][pe] = pv
        self.plan[eng].append(Op(fn, inc))
        self.cnt[eng] += inc
        newv = self.cnt[eng]
        for b in writes:
            b.writer = (eng, newv)
            b.readers = []
        for b in reads:
            b.readers.append((eng, newv))


def build_program(dbg=False):
    nc = bass.Bass("TRN2", target_bir_lowering=False, debug=False)

    x_d = nc.dram_tensor("x", [SEQS, C, T], F32, kind="ExternalInput").ap()
    y_d = nc.dram_tensor("y", [SEQS, C, T], F32, kind="ExternalOutput").ap()
    wts = {}
    for nm, k, f in [("wt_attn", C, 3 * C), ("wt_proj", C, C), ("wt_fc", C, FC),
                     ("wt_mlp", FC, C), ("wt_p", C, C), ("wt_q", C, C),
                     ("wt_k", C, C), ("wt_v", C, C), ("wt_o", C, C),
                     ("wt_ffn_fc", C, FC), ("wt_ffn_proj", FC, C)]:
        wts[nm] = nc.dram_tensor(nm, [k, f], F32, kind="ExternalInput").ap()
    wcols = {}
    woff = [0]

    def walloc(key, width):
        wcols[key] = (woff[0], width)
        woff[0] += width
    for _nm, _kd, _fd in [("wt_attn", C, 2 * C), ("wt_proj", C, C),
                          ("wt_fc", C, FC), ("wt_p", C, C), ("wt_q", C, C),
                          ("wt_k", C, C), ("wt_v", C, C), ("wt_o", C, C),
                          ("wt_ffn_fc", C, FC)]:
        for _m in range(_fd // 128):
            walloc((_nm, _m), (_kd // 128) * 128)
    for _nm in ("wt_mlp", "wt_ffn_proj"):
        for _m in range(NCH):
            for _h in range(2):
                walloc((_nm, _m, _h), 16 * 128)
    for _h in range(4):
        walloc(("wvs", _h), NCH * 256)
    wscr = nc.dram_tensor("wscr", [128, woff[0]], BF16, kind="Internal").ap()

    bss = {}
    for nm, f in [("b_attn", 3 * C), ("b_proj", C), ("b_fc", FC), ("b_mlp", C),
                  ("b_p", C), ("b_q", C), ("b_k", C), ("b_v", C), ("b_o", C)]:
        bss[nm] = nc.dram_tensor(nm, [128, f // 128], F32,
                                 kind="ExternalInput").ap()
    bv_row_d = nc.dram_tensor("b_attn_v", [1, C], F32, kind="ExternalInput").ap()

    ii = np.arange(896)[None, :]
    pp = np.arange(128)[:, None]
    mask_np = (ii >= pp + 384).astype(BF)
    e_np = np.zeros((128, NCH, 16), dtype=BF)
    f_np = np.zeros((16, NCH, 128), dtype=BF)
    for c in range(NCH):
        for p in range(128):
            h = 2 * c + p // 64
            e_np[p, c, h] = 1
            f_np[h, c, p] = 1
    ones_col_d = nc.inline_tensor(np.ones((128, 1), dtype=BF), name="d_onc").ap()
    esh_d = nc.inline_tensor(np.full((128, 1), ESH, dtype=np.float32),
                             name="d_esh").ap()
    eps5_d = nc.inline_tensor(np.full((128, 1), 1e-5, dtype=np.float32),
                              name="d_eps5").ap()
    eps10_d = nc.inline_tensor(np.full((128, 1), 1e-10, dtype=np.float32),
                               name="d_eps10").ap()
    one_d = nc.inline_tensor(np.ones((128, 1), dtype=np.float32),
                             name="d_one").ap()
    ones_row_d = nc.inline_tensor(np.ones((1, 128), dtype=BF), name="d_onr").ap()
    mask_d = nc.inline_tensor(np.ascontiguousarray(mask_np), name="d_mask").ap()
    e_d = nc.inline_tensor(np.ascontiguousarray(e_np), name="d_eind").ap()
    f_d = nc.inline_tensor(np.ascontiguousarray(f_np), name="d_find").ap()

    K = 1024
    SB_BASE = 16512
    A = nc.alloc_sbuf_tensor_at

    def at(name, shape, dtype, off):
        return A(name, shape, dtype, offset=SB_BASE + off)

    XS = at("XS", [128, NCH, T], F32, 0)
    KB = at("KB", [128, NCH, T], BF16, 64 * K)
    SPKY = at("SPKY", [128, NCH, GTF], F32, 64 * K)
    SPKM = at("SPKM", [128, NCH, GTF], F32, 64 * K + 16 * K)
    VB = at("VB", [128, 16, C], BF16, 96 * K)
    HBF = at("HBF", [128, FCH, GTF], BF16, 96 * K)
    VC = at("VC", [128, NCH, T], BF16, 96 * K)
    UG = at("UG", [128, NCH, GT], BF16, 130 * K)
    YG = at("YG", [128, NCH, GT], BF16, 138 * K)
    XBG = at("XBG", [128, NCH, GT], BF16, 146 * K)
    WROW = [at(f"WROW{i}", [16, GT], BF16, 146 * K + 5 * K + i * K)
            for i in range(3)]
    SQG = at("SQG", [128, NCH, GT], BF16, 154 * K)
    SRND = at("SRND", [33, NB], F32, 154 * K)
    SRNDB = at("SRNDB", [33, NB], BF16, 154 * K + 2 * K)
    SRNDB2 = at("SRNDB2", [1, NB], BF16, 154 * K + 3 * K)
    ABS0 = at("ABS0", [16, GT], F32, 154 * K)
    ABS1 = at("ABS1", [16, GT], F32, 154 * K + 2 * K)
    ABS2 = at("ABS2", [16, GT], F32, 154 * K + 4 * K)
    ABSM = at("ABSM", [16, GT], F32, 154 * K + 6 * K)
    QP = at("QP", [128, 2, NB], BF16, 162 * K)
    BVR = at("BVR", [1, C], BF16, 162 * K)
    PHAT = at("PHAT", [128, 4, NB], BF16, 164 * K)
    RS32 = at("RS32", [128, 2, NB], F32, 168 * K)
    BVRF = at("BVRF", [1, C], F32, 168 * K)
    RSB = at("RSB", [128, NB], F32, 172 * K)
    MRB = at("MRB", [128, 2, GT], F32, 174 * K)
    ST_S = at("ST_S", [1, GT], F32, 178 * K)
    ST_Q = at("ST_Q", [1, GT], F32, 180 * K)
    ST_R = at("ST_R", [1, GT], F32, 182 * K)
    ST_T = at("ST_T", [1, GT], F32, 154 * K)
    STB_M = at("STB_M", [1, GT], BF16, 184 * K)
    STB_R = at("STB_R", [1, GT], BF16, 185 * K)
    SBF = at("SBF", [128, 4, GTF], F32, 138 * K)
    XGC = at("XGC", [128, 2, GTF], F32, 186 * K)
    PSTG = at("PSTG", [128, 2, NCH, 128], BF16, 186 * K)
    WR = [at(f"WR{i}", [128, 16, 128], BF16, 190 * K + 4 * K * i) for i in range(3)]
    WR3 = at("WR3", [128, 16, 128], BF16, 164 * K)
    cb = 202 * K
    ONESC = at("ONESC", [128, 1], BF16, cb); cb += 32
    ONESR = at("ONESR", [1, 128], BF16, cb); cb += 256
    EIND = at("EIND", [128, NCH, 16], BF16, cb); cb += 256
    BIAS = at("BIAS", [128, 112], F32, cb); cb += 448
    MASK = at("MASK", [128, 896], BF16, cb); cb += 1792
    ESHC = at("ESHC", [128, 1], F32, cb); cb += 32
    EPS5C = at("EPS5C", [128, 1], F32, cb); cb += 32
    EPS10C = at("EPS10C", [128, 1], F32, cb); cb += 32
    EONEC = at("EONEC", [128, 1], F32, cb); cb += 32
    FIND = at("FIND", [16, NCH, 128], BF16, cb); cb += 2048
    assert cb <= 208 * K - 128, cb

    bias_off = {}
    off = 0
    for nm, f in [("b_attn", 3 * C), ("b_proj", C), ("b_fc", FC), ("b_mlp", C),
                  ("b_p", C), ("b_q", C), ("b_k", C), ("b_v", C), ("b_o", C)]:
        bias_off[nm] = off
        off += f // 128

    PSA = nc.alloc_psum_tensor("PSA", [128, 2, NB], F32)
    PSB = nc.alloc_psum_tensor("PSB", [128, 2, NB], F32)
    PS47 = [nc.alloc_psum_tensor(f"PS{i}", [128, NB], F32) for i in range(4, 8)]
    PS = [PSA[:, 0, :], PSA[:, 1, :], PSB[:, 0, :], PSB[:, 1, :]] + \
         [t[:, :] for t in PS47]

    sb = SB()
    bufs = {}

    def B(name):
        if name not in bufs:
            bufs[name] = Buf(name)
        return bufs[name]

    alias = { "HBF": "VB", "VC": "VB",
             "WROW0": "XBG", "WROW1": "XBG", "WROW2": "XBG",
             "SRND": "SQG", "SRNDB": "SQG", "SRNDB2": "SQG", "ST_T": "SQG",
             "ABS0": "SQG", "ABS1": "SQG", "ABS2": "SQG", "ABSM": "SQG",
             "BVR": "QP", "SBF": "YG", "PSTG": "XGC", "BVRF": "RS32",
             "WR3": "PHAT"}

    def RB(name):
        return B(alias.get(name, name))

    psb = [B(f"ps{i}") for i in range(8)]
    XSB = [B(f"XS_{i}") for i in range(G)]
    UGC = [B(f"UG_{c}") for c in range(NCH)]
    PH = [B(f"PH{j}") for j in range(4)]

    def xsg(lo):
        return XSB[lo // GT]

    for dst, src in [(ONESC, ones_col_d), (ONESR, ones_row_d), (MASK, mask_d),
                     (EIND, e_d), (FIND, f_d), (ESHC, esh_d),
                     (EPS5C, eps5_d), (EPS10C, eps10_d), (EONEC, one_d)]:
        sb.op("sync", lambda e, dst=dst, src=src: e.dma_start(out=dst[:], in_=src),
              writes=[B(dst.name)], inc=16)
    for nm in bias_off:
        o, w = bias_off[nm], bss[nm].shape[1]
        src = bss[nm]
        sb.op("sync", lambda e, o=o, w=w, src=src:
              e.dma_start(out=BIAS[:, o:o + w], in_=src),
              writes=[B("BIAS")], inc=16)

    def fill_order():
        yield ("wt_attn", C, 2 * C)
        yield ("wvs", None, None)
        yield ("wt_proj", C, C)
        yield ("wt_fc", C, FC)
        yield ("wt_mlp", None, None)
        yield ("wt_ffn_fc", C, FC)
        yield ("wt_ffn_proj", None, None)
        yield ("wt_p", C, C)
        yield ("wt_q", C, C)
        yield ("wt_k", C, C)
        yield ("wt_v", C, C)
        yield ("wt_o", C, C)

    fill_i = [0]
    pending_store = [None]

    def fill_tile(load_fn, store_fn):
        load_fn()
        if pending_store[0] is not None:
            pending_store[0]()
        pending_store[0] = store_fn

    for nm, kd, fd in fill_order():
        if nm == "wvs":
            for h in range(4):
                i = fill_i[0] % len(WR)
                fill_i[0] += 1
                slot, wbuf = WR[i], RB(f"WR{i}")
                o, w = wcols[("wvs", h)]
                src = wts["wt_attn"][:, 2 * C + h * 256:2 * C + (h + 1) * 256]
                src = src.rearrange("(c p) f -> p c f", p=128)
                fill_tile(
                    lambda slot=slot, src=src, wbuf=wbuf: sb.op(
                        "gpsimd", lambda e: e.dma_start(
                            out=slot.ap().rearrange("p a b -> p (a b)")
                            .rearrange("p (c f) -> p c f", c=NCH)[:, :, :], in_=src),
                        writes=[wbuf], inc=16),
                    lambda slot=slot, o=o, w=w, wbuf=wbuf, h=h: sb.op(
                        "gpsimd", lambda e: e.dma_start(
                            out=wscr[:, o:o + w],
                            in_=slot.ap().rearrange("p a b -> p (a b)")[:, 0:w]),
                        reads=[wbuf], writes=[B(("wvs", h))], inc=16))
            continue
        if nm in ("wt_mlp", "wt_ffn_proj"):
            for m in range(NCH):
                for h in range(2):
                    for hh in range(2):
                        i = fill_i[0] % 2
                        fill_i[0] += 1
                        stg = PSTG[:, i, :, :]
                        sbuf_ = RB("PSTG")
                        r0 = h * 2048 + hh * 1024
                        src = wts[nm][r0:r0 + 1024, m * 128:(m + 1) * 128]
                        src = src.rearrange("(c p) f -> p c f", p=128)
                        o, w = wcols[(nm, m, h)]
                        oo = o + hh * 1024
                        fill_tile(
                            lambda stg=stg, src=src, sbuf_=sbuf_: sb.op(
                                "gpsimd", lambda e: e.dma_start(
                                    out=stg, in_=src),
                                writes=[sbuf_], inc=16),
                            lambda stg=stg, oo=oo, sbuf_=sbuf_, nm=nm, m=m, h=h:
                            sb.op("gpsimd", lambda e: e.dma_start(
                                out=wscr[:, oo:oo + 1024],
                                in_=stg.rearrange("p a b -> p (a b)")),
                                reads=[sbuf_], writes=[B((nm, m, h))], inc=16))
            continue
        use_ring = nm == "wt_attn"
        for m in range(fd // 128):
            src = wts[nm][:, m * 128:(m + 1) * 128]
            src = src.rearrange("(c p) f -> p c f", p=128)
            o, w = wcols[(nm, m)]
            if use_ring:
                i = fill_i[0] % len(WR)
                fill_i[0] += 1
                slot, wbuf = WR[i], RB(f"WR{i}")
                fill_tile(
                    lambda slot=slot, src=src, wbuf=wbuf: sb.op(
                        "gpsimd", lambda e: e.dma_start(
                            out=slot[:, 0:NCH, :], in_=src), writes=[wbuf], inc=16),
                    lambda slot=slot, o=o, w=w, wbuf=wbuf, nm=nm, m=m: sb.op(
                        "gpsimd", lambda e: e.dma_start(
                            out=wscr[:, o:o + w],
                            in_=slot.ap().rearrange("p a b -> p (a b)")[:, 0:w]),
                        reads=[wbuf], writes=[B((nm, m))], inc=16))
            else:
                i = fill_i[0] % 2
                fill_i[0] += 1
                stg = PSTG[:, i, :, :]
                sbuf_ = RB("PSTG")
                fill_tile(
                    lambda stg=stg, src=src, sbuf_=sbuf_: sb.op(
                        "gpsimd", lambda e: e.dma_start(out=stg, in_=src),
                        writes=[sbuf_], inc=16),
                    lambda stg=stg, o=o, w=w, sbuf_=sbuf_, nm=nm, m=m: sb.op(
                        "gpsimd", lambda e: e.dma_start(
                            out=wscr[:, o:o + w],
                            in_=stg.rearrange("p a b -> p (a b)")),
                        reads=[sbuf_], writes=[B((nm, m))], inc=16))
    if pending_store[0] is not None:
        pending_store[0]()
        pending_store[0] = None

    def bias_ap(nm, chunk):
        return BIAS[:, bias_off[nm] + chunk:bias_off[nm] + chunk + 1]

    wr_i = [0]
    RING_N = [3]
    wpair_cache = {}

    def ring_slot():
        i = wr_i[0] % RING_N[0]
        wr_i[0] += 1
        if i < 3:
            return WR[i], RB(f"WR{i}"), []
        return WR3, RB("WR3"), list(PH)

    def wtile_pair(nm, m0):
        key = (nm, m0)
        if key in wpair_cache:
            return wpair_cache[key]
        slot, wbuf, wex = ring_slot()
        o, w = wcols[(nm, m0)]
        o2, w2 = wcols[(nm, m0 + 1)]
        assert o2 == o + w
        sb.op("sync", lambda e, slot=slot, o=o, w=w, w2=w2: e.dma_start(
            out=slot.ap().rearrange("p a b -> p (a b)")[:, 0:w + w2],
            in_=wscr[:, o:o + w + w2]),
            reads=[B((nm, m0)), B((nm, m0 + 1))], writes=[wbuf] + wex, inc=16)
        wpair_cache.clear()
        wpair_cache[key] = (slot, wbuf)
        return slot, wbuf

    def wtile_small(nm, m):
        m0 = m - (m % 2)
        slot, wbuf = wtile_pair(nm, m0)
        sub = slot.ap()[:, 8 * (m % 2):8 * (m % 2) + NCH, :]
        return sub, wbuf

    def wtile_large(nm, m, half):
        slot, wbuf, wex = ring_slot()
        o, w = wcols[(nm, m, half)]
        sb.op("sync", lambda e, slot=slot, o=o, w=w: e.dma_start(
            out=slot.ap().rearrange("p a b -> p (a b)"), in_=wscr[:, o:o + w]),
            reads=[B((nm, m, half))], writes=[wbuf] + wex, inc=16)
        wpair_cache.clear()
        return slot, wbuf

    def newton_rsqrt(src, dst, tmp, rbufs, n):
        rbuf = rbufs
        I32 = mybir.dt.int32
        sb.op("vector", lambda e: e.tensor_single_scalar(
            dst.bitcast(I32), src.bitcast(I32), 1, ALU.arith_shift_right),
            reads=rbuf, writes=rbuf)
        sb.op("vector", lambda e: e.tensor_scalar(
            dst.bitcast(I32), dst.bitcast(I32), -1, 0x5F3759DF, ALU.mult, ALU.add),
            reads=rbuf, writes=rbuf)
        for _ in range(2):
            sb.op("vector", lambda e: e.tensor_mul(tmp, dst, dst),
                  reads=rbuf, writes=rbuf)
            sb.op("vector", lambda e: e.tensor_mul(tmp, tmp, src),
                  reads=rbuf, writes=rbuf)
            sb.op("vector", lambda e: e.tensor_scalar(tmp, tmp, -0.5, 1.5,
                                                      ALU.mult, ALU.add),
                  reads=rbuf, writes=rbuf)
            sb.op("vector", lambda e: e.tensor_mul(dst, dst, tmp),
                  reads=rbuf, writes=rbuf)

    def ln_pre(src_t, src_buf, lo, n):
        xbg, sqg = RB("XBG"), RB("SQG")
        sb.op("scalar", lambda e: e.activation(
            XBG[:, :, 0:n], src_t[:, :, lo:lo + n], AF.Copy),
            reads=[src_buf], writes=[xbg])
        sb.op("scalar", lambda e: e.activation(
            SQG[:, :, 0:n], src_t[:, :, lo:lo + n], AF.Square),
            reads=[src_buf], writes=[sqg])

    def ln_stats(src_t, src_buf, lo, n, eps, pre_done=False):
        stat = B("STAT_ROWS")
        xbg, sqg = RB("XBG"), RB("SQG")
        if not pre_done:
            ln_pre(src_t, src_buf, lo, n)
        nblk = (n + NB - 1) // NB
        for bi in range(nblk):
            c0, cw = bi * NB, min(NB, n - bi * NB)
            for c in range(NCH):
                sb.op("tensor", lambda e, c=c, c0=c0, cw=cw: e.matmul(
                    PS[0][0:1, 0:cw], ONESC[:], XBG[:, c, c0:c0 + cw],
                    start=(c == 0), stop=(c == NCH - 1)),
                    reads=[xbg, B("ONESC")], writes=[psb[0]])
            for c in range(NCH):
                sb.op("tensor", lambda e, c=c, c0=c0, cw=cw: e.matmul(
                    PS[1][0:1, 0:cw], ONESC[:], SQG[:, c, c0:c0 + cw],
                    start=(c == 0), stop=(c == NCH - 1)),
                    reads=[sqg, B("ONESC")], writes=[psb[1]])
            epsc = EPS5C if eps > 1e-7 else EPS10C
            sb.op("scalar", lambda e, c0=c0, cw=cw: e.activation(
                ST_S[0:1, c0:c0 + cw], PS[0][0:1, 0:cw], AF.Identity,
                scale=1.0 / C),
                reads=[psb[0]], writes=[stat])
            sb.op("scalar", lambda e, c0=c0, cw=cw, epsc=epsc: e.activation(
                ST_Q[0:1, c0:c0 + cw], PS[1][0:1, 0:cw], AF.Identity,
                bias=epsc[0:1, 0:1], scale=1.0 / C),
                reads=[psb[1], B(epsc.name)], writes=[stat])
        sn = ST_S[0:1, 0:n]
        qn = ST_Q[0:1, 0:n]
        rn = ST_R[0:1, 0:n]
        tn = ST_T[0:1, 0:n]
        sb.op("vector", lambda e: e.tensor_mul(tn, sn, sn),
              reads=[stat, sqg], writes=[stat, sqg])
        sb.op("vector", lambda e: e.tensor_sub(qn, qn, tn),
              reads=[stat, sqg], writes=[stat])
        newton_rsqrt(qn, rn, tn, [stat, sqg], n)
        sb.op("vector", lambda e: e.tensor_copy(STB_M[0:1, 0:n], sn),
              reads=[stat], writes=[stat])
        sb.op("vector", lambda e: e.tensor_copy(STB_R[0:1, 0:n], rn),
              reads=[stat], writes=[stat])

    def ln_finish(src_t, src_buf, lo, n, dst_t, dst_buf, dst_lo=None,
                  f32_dst=False):
        if dst_lo is None:
            dst_lo = 0
        stat = B("STAT_ROWS")
        mrb = RB("MRB")
        nblk = (n + NB - 1) // NB
        for bi in range(nblk):
            c0, cw = bi * NB, min(NB, n - bi * NB)
            sb.op("tensor", lambda e, c0=c0, cw=cw: e.matmul(
                PS[2][:, 0:cw], ONESR[:], STB_M[0:1, c0:c0 + cw],
                start=True, stop=True), reads=[stat, B("ONESR")], writes=[psb[2]])
            sb.op("tensor", lambda e, c0=c0, cw=cw: e.matmul(
                PS[3][:, 0:cw], ONESR[:], STB_R[0:1, c0:c0 + cw],
                start=True, stop=True), reads=[stat, B("ONESR")], writes=[psb[3]])
            sb.op("scalar", lambda e, c0=c0, cw=cw: e.activation(
                MRB[:, 0, c0:c0 + cw], PS[2][:, 0:cw], AF.Copy),
                reads=[psb[2]], writes=[mrb])
            sb.op("scalar", lambda e, c0=c0, cw=cw: e.activation(
                MRB[:, 1, c0:c0 + cw], PS[3][:, 0:cw], AF.Copy),
                reads=[psb[3]], writes=[mrb])
        if dst_t is UG and not f32_dst:
            for c in range(NCH):
                sb.op("vector", lambda e, c=c: e.tensor_sub(
                    dst_t[:, c, dst_lo:dst_lo + n], src_t[:, c, lo:lo + n],
                    MRB[:, 0, 0:n]),
                    reads=[src_buf, mrb], writes=[UGC[c]])
                sb.op("vector", lambda e, c=c: e.tensor_mul(
                    dst_t[:, c, dst_lo:dst_lo + n],
                    dst_t[:, c, dst_lo:dst_lo + n], MRB[:, 1, 0:n]),
                    reads=[mrb, UGC[c]], writes=[UGC[c]])
            return
        mb = MRB[:, 0, 0:n].unsqueeze(1).broadcast_to([128, NCH, n])
        rb = MRB[:, 1, 0:n].unsqueeze(1).broadcast_to([128, NCH, n])
        sb.op("vector", lambda e: e.tensor_sub(
            dst_t[:, :, dst_lo:dst_lo + n], src_t[:, :, lo:lo + n], mb),
            reads=[src_buf, mrb], writes=[dst_buf])
        sb.op("vector", lambda e: e.tensor_mul(
            dst_t[:, :, dst_lo:dst_lo + n], dst_t[:, :, dst_lo:dst_lo + n], rb),
            reads=[mrb, dst_buf], writes=[dst_buf])

    def layernorm(src_t, src_buf, lo, n, eps, dst_t, dst_buf, dst_lo=None,
                  f32_dst=False, pre_done=False):
        ln_stats(src_t, src_buf, lo, n, eps, pre_done=pre_done)
        ln_finish(src_t, src_buf, lo, n, dst_t, dst_buf, dst_lo=dst_lo,
                  f32_dst=f32_dst)

    def proj_fm(nm_wt, src_t, src_buf, n, mlist, evac, banks=(4, 5, 6, 7),
                src_lo=0, post_m=None):
        per_chunk = src_t is UG
        """For each m in mlist: psum = Wt_m^T @ src[:, :, src_lo:src_lo+n]
        (accumulated over K chunks), then evac(m, c0, cw, bank)."""
        kdim = wts[nm_wt].shape[0]
        nblk = (n + NB - 1) // NB
        bi_rr = [0]
        for m in mlist:
            if kdim == C:
                slot, wbuf = wtile_small(nm_wt, m)
                halves = [(slot, wbuf, 0, NCH)]
            else:
                s0, b0 = wtile_large(nm_wt, m, 0)
                s1, b1 = wtile_large(nm_wt, m, 1)
                halves = [(s0, b0, 0, 16), (s1, b1, 16, 16)]
            for bi in range(nblk):
                c0, cw = bi * NB, min(NB, n - bi * NB)
                p = banks[bi_rr[0] % len(banks)]
                bi_rr[0] += 1
                nk = kdim // 128
                ki = 0
                for slot, wbuf, kofs, kn in halves:
                    for k in range(kn):
                        sbf_ = UGC[kofs + k] if per_chunk else src_buf
                        sb.op("tensor", lambda e, slot=slot, k=k, kofs=kofs,
                              c0=c0, cw=cw, p=p, ki=ki: e.matmul(
                                  PS[p][:, 0:cw], slot[:, k, :],
                                  src_t[:, kofs + k, src_lo + c0:src_lo + c0 + cw],
                                  start=(ki == 0), stop=(ki == nk - 1)),
                              reads=[sbf_, wbuf], writes=[psb[p]])
                        ki += 1
                evac(m, c0, cw, p)
            if post_m is not None:
                post_m()

    def evac_act(dst_t, dst_buf, nm_bias, func=None, dst_lo=0, scale=1.0):
        def ev(m, c0, cw, p):
            bias = bias_ap(nm_bias, m) if nm_bias else 0.0
            nonlocal func
            f = func if func is not None else (AF.Identity if nm_bias else AF.Copy)
            sb.op("scalar", lambda e, m=m, c0=c0, cw=cw, p=p, f=f, bias=bias:
                  e.activation(
                      dst_t[:, m, dst_lo + c0:dst_lo + c0 + cw], PS[p][:, 0:cw],
                      f, bias=bias, scale=scale),
                  reads=[psb[p], B("BIAS")], writes=[dst_buf])
        return ev

    dbg_outs = []

    def tap(name, sb_tensor, shape, buf):
        if not dbg:
            return
        d = nc.dram_tensor("dbg_" + name, shape, F32, kind="ExternalOutput").ap()
        rb = [B(f"XS_{i}") for i in range(G)] if sb_tensor is XS else [buf]
        sb.op("gpsimd", lambda e: e.dma_start(out=d, in_=sb_tensor[:, :, :]),
              reads=rb, inc=16)
        dbg_outs.append(name)

    def ev_stage_add_xs(nm_bias, xs_lo):
        def ev(m, c0, cw, p):
            sl = m % 2
            sb.op("scalar", lambda e, m=m, c0=c0, cw=cw, p=p, sl=sl: e.activation(
                RS32[:, sl, 0:cw], PS[p][:, 0:cw], AF.Identity,
                bias=bias_ap(nm_bias, m)),
                reads=[psb[p], B("BIAS")], writes=[RB("RS32")])
            sb.op("vector", lambda e, m=m, c0=c0, cw=cw, sl=sl: e.tensor_add(
                XS[:, m, xs_lo + c0:xs_lo + c0 + cw],
                XS[:, m, xs_lo + c0:xs_lo + c0 + cw], RS32[:, sl, 0:cw]),
                reads=[RB("RS32"), xsg(xs_lo)], writes=[xsg(xs_lo)])
        return ev

    def phase_A(s):
        RING_N[0] = 3
        for g in range(G):
            lo = g * GT
            src = x_d[s, :, lo:lo + GT].rearrange("(c p) t -> p c t", p=128)
            sb.op("sync", lambda e, src=src, lo=lo:
                  e.dma_start(out=XS[:, :, lo:lo + GT], in_=src),
                  writes=[xsg(lo)], inc=16)
        ln_pre(XS, xsg(0), 0, GT)
        for g in range(G):
            lo = g * GT
            layernorm(XS, xsg(lo), lo, GT, 1e-5, XS, xsg(lo), dst_lo=lo,
                      f32_dst=True, pre_done=True)
            if g + 1 < G:
                ln_pre(XS, xsg(lo + GT), lo + GT, GT)
        if dbg and s == 0:
            tap("t", XS, [128, NCH, T], B("XS"))
        sb.op("sync", lambda e: e.dma_start(
            out=BVRF[:, :], in_=bv_row_d),
            writes=[RB("BVRF")], inc=16)
        sb.op("vector", lambda e: e.tensor_copy(BVR[:, :], BVRF[:, :]),
              reads=[RB("BVRF")], writes=[RB("BVR")])
        ln_pre(XS, xsg(0), 0, GT)
        layernorm(XS, xsg(0), 0, GT, 1e-10, UG, RB("UG"), pre_done=True)
        for g in range(G):
            lo = g * GT
            if g + 1 < G:
                ln_pre(XS, xsg(lo + GT), lo + GT, GT)

            def ev_k(m, c0, cw, p, lo=lo, g=g):
                er = ([B("SPKY"), B("SPKM")]
                      if (g == 0 and m == 8) else [])
                sb.op("scalar", lambda e, m=m, c0=c0, cw=cw, p=p: e.activation(
                    KB[:, m - 8, lo + c0:lo + c0 + cw], PS[p][:, 0:cw],
                    AF.Identity, bias=bias_ap("b_attn", m)),
                    reads=[psb[p], B("BIAS")] + er, writes=[RB("KB")])
            proj_fm("wt_attn", UG, RB("UG"), GT, range(8, 16), ev_k)
            if g + 1 < G:
                ln_stats(XS, xsg(lo + GT), lo + GT, GT, 1e-10, pre_done=True)
            for nb in range(4):
                i = wr_i[0] % len(WR)
                wr_i[0] += 1
                vslot, vbuf = WR[i], RB(f"WR{i}")
                wpair_cache.clear()
                o, w = wcols[("wvs", nb)]
                sb.op("sync", lambda e, vslot=vslot, o=o, w=w: e.dma_start(
                    out=vslot.ap().rearrange("p a b -> p (a b)")[:, 0:w],
                    in_=wscr[:, o:o + w]),
                    reads=[B(("wvs", nb))], writes=[vbuf], inc=16)
                vv = vslot.ap().rearrange("p a b -> p (a b)")[:, 0:2048]
                vv = vv.rearrange("p (c f) -> p c f", c=NCH)
                for ti in range(4):
                    gti = 4 * g + ti
                    p = 4 + ti % 2
                    for k in range(NCH):
                        sb.op("tensor", lambda e, k=k, ti=ti, p=p, vv=vv: e.matmul(
                            PS[p][:, 0:256], UG[:, k, ti * 128:(ti + 1) * 128],
                            vv[:, k, :], start=(k == 0), stop=False),
                            reads=[RB("UG"), vbuf], writes=[psb[p]])
                    sb.op("tensor", lambda e, nb=nb, p=p: e.matmul(
                        PS[p][:, 0:256], ONESR[:, :], BVR[0:1, nb * 256:(nb + 1) * 256],
                        start=False, stop=True),
                        reads=[RB("BVR"), B("ONESR")], writes=[psb[p]])
                    sb.op("scalar", lambda e, gti=gti, nb=nb, p=p: e.activation(
                        VB[:, gti, nb * 256:(nb + 1) * 256], PS[p][:, 0:256], AF.Copy),
                        reads=[psb[p]], writes=[RB("VB")])
            if g + 1 < G:
                ln_finish(XS, xsg(lo + GT), lo + GT, GT, UG, RB("UG"))
        if dbg and s == 0:
            tap("k", KB, [128, NCH, T], RB("KB"))
            tap("v", VB, [128, 16, C], RB("VB"))
        for g in range(G):
            lo = g * GT
            cmax = 4 * g + 4
            layernorm(XS, xsg(lo), lo, GT, 1e-10, UG, RB("UG"))
            if dbg and s == 0 and g == 0:
                tap("ua2g0", UG, [128, NCH, GT], RB("UG"))
            pending_ep = [None]
            for hp in range(NCH):
                qs = hp % 2
                pv = 4 + hp % 2
                slot, wbuf = wtile_small("wt_attn", hp)
                for k in range(NCH):
                    sb.op("tensor", lambda e, slot=slot, k=k: e.matmul(
                        PS[7][:, :], slot[:, k, :], UG[:, k, 0:GT],
                        start=(k == 0), stop=(k == NCH - 1)),
                        reads=[RB("UG"), wbuf], writes=[psb[7]])
                sb.op("scalar", lambda e, hp=hp, qs=qs: e.activation(
                    QP[:, qs, :], PS[7][:, :], AF.Identity,
                    bias=bias_ap("b_attn", hp)),
                    reads=[psb[7], B("BIAS")], writes=[RB("QP")])
                if dbg and s == 0 and g == 0 and hp == 0:
                    tap("qp0", QP, [128, 2, NB], RB("QP"))

                def st_pair(c, hp=hp, qs=qs):
                    pe_ = 2 * (c % 2)
                    po_ = pe_ + 1
                    sb.op("tensor", lambda e, c=c, hp=hp, qs=qs, pe_=pe_: e.matmul(
                        PS[pe_][:, :], KB[0:64, hp, c * 128:(c + 1) * 128],
                        QP[0:64, qs, :], start=True, stop=True,
                        tile_position=(0, 0)),
                        reads=[RB("KB"), RB("QP")], writes=[psb[pe_]])
                    sb.op("tensor", lambda e, c=c, hp=hp, qs=qs, po_=po_: e.matmul(
                        PS[po_][:, :], KB[64:128, hp, c * 128:(c + 1) * 128],
                        QP[64:128, qs, :], start=True, stop=True,
                        tile_position=(64, 0)),
                        reads=[RB("KB"), RB("QP")], writes=[psb[po_]])

                def finish_c(c, hp=hp, pv=pv, cmax=cmax, g=g):
                    pe_ = 2 * (c % 2)
                    st = (c == 0)
                    sp = (c == cmax - 1)
                    diag = c >= 4 * g
                    k0 = 128 * c - 512 * g
                    for j, vlo, tp, prow in ((pe_, 0, (0, 0), 0),
                                             (pe_ + 1, 64, (0, 64), 32)):
                        ew = ([RB("PHAT")]
                              if (g == 0 and hp == 0 and c == 0 and j == pe_)
                              else [])
                        sb.op("scalar", lambda e, j=j: e.activation(
                            PHAT[:, j, :], PS[j][:, :], AF.Exp,
                            bias=ESHC[:, 0:1], scale=SC),
                            reads=[psb[j]], writes=[PH[j]] + ew)
                        if diag:
                            ms = MASK[:, 384 - k0:384 - k0 + NB]
                            sb.op("vector", lambda e, j=j, ms=ms: e.tensor_mul(
                                PHAT[:, j, :], PHAT[:, j, :], ms),
                                reads=[PH[j], B("MASK")], writes=[PH[j]])
                        sb.op("tensor", lambda e, c=c, hp=hp, j=j, st=st,
                              sp=sp, pv=pv, vlo=vlo, tp=tp: e.matmul(
                            PS[pv][vlo:vlo + 64, :],
                            VB[:, c, 128 * hp + vlo:128 * hp + vlo + 64],
                            PHAT[:, j, :], start=st, stop=sp, tile_position=tp,
                            skip_group_check=True),
                            reads=[RB("VB"), PH[j]], writes=[psb[pv]])
                        sb.op("tensor", lambda e, j=j, st=st, sp=sp,
                              prow=prow: e.matmul(
                            PS[6][prow:prow + 1, :], ONESC[:, :], PHAT[:, j, :],
                            start=st, stop=sp, tile_position=(0, prow),
                            skip_group_check=True),
                            reads=[B("ONESC"), PH[j]], writes=[psb[6]])

                st_pair(0)
                if pending_ep[0] is not None:
                    pending_ep[0]()
                    pending_ep[0] = None
                for c in range(cmax):
                    if c + 1 < cmax:
                        st_pair(c + 1)
                    finish_c(c)

                def epilogue(hp=hp, pv=pv):
                    sb.op("vector", lambda e: e.tensor_copy(
                        SRND[0:1, :], PS[6][0:1, :]),
                        reads=[psb[6]], writes=[RB("SRND")])
                    sb.op("vector", lambda e: e.tensor_copy(
                        SRND[32:33, :], PS[6][32:33, :]),
                        reads=[psb[6]], writes=[RB("SRND")])
                    sb.op("vector", lambda e: e.reciprocal(
                        SRND[0:33, :], SRND[0:33, :]),
                        reads=[RB("SRND")], writes=[RB("SRND")])
                    sb.op("vector", lambda e: e.tensor_copy(
                        SRNDB[0:33, :], SRND[0:33, :]),
                        reads=[RB("SRND")], writes=[RB("SRND")])
                    sb.op("vector", lambda e: e.tensor_copy(
                        SRNDB2[0:1, :], SRNDB[32:33, :]),
                        reads=[RB("SRND")], writes=[RB("SRND")])
                    sb.op("tensor", lambda e: e.matmul(
                        PS[7][0:64, :], ONESR[0:1, 0:64], SRNDB[0:1, :],
                        start=True, stop=True, tile_position=(0, 0)),
                        reads=[RB("SRND"), B("ONESR")], writes=[psb[7]])
                    sb.op("tensor", lambda e: e.matmul(
                        PS[7][64:128, :], ONESR[0:1, 0:64], SRNDB2[0:1, :],
                        start=True, stop=True, tile_position=(0, 64)),
                        reads=[RB("SRND"), B("ONESR")], writes=[psb[7]])
                    sb.op("scalar", lambda e: e.activation(
                        RSB[:, :], PS[7][:, :], AF.Copy),
                        reads=[psb[7]], writes=[B("RSB")])
                    sb.op("vector", lambda e, hp=hp, pv=pv: e.tensor_mul(
                        YG[:, hp, :], PS[pv][:, :], RSB[:, :]),
                        reads=[psb[pv], B("RSB")], writes=[RB("YG")])
                pending_ep[0] = epilogue
            pending_ep[0]()
            if dbg and s == 0 and g == 0:
                tap("yg0", YG, [128, NCH, GT], RB("YG"))
            proj_fm("wt_proj", YG, RB("YG"), GT, range(NCH),
                    ev_stage_add_xs("b_proj", lo))
        if dbg and s == 0:
            tap("t2", XS, [128, NCH, T], B("XS"))
        RING_N[0] = 4
        ln_pre(XS, xsg(0), 0, GTF)
        layernorm(XS, xsg(0), 0, GTF, 1e-10, UG, RB("UG"), pre_done=True)
        for gf in range(GF):
            lo = gf * GTF
            if gf + 1 < GF:
                ln_pre(XS, xsg(lo + GTF), lo + GTF, GTF)
            if dbg and s == 0 and gf == 0:
                tap("u2g0", UG, [128, NCH, GT], RB("UG"))
            proj_fm("wt_fc", UG, RB("UG"), GTF, range(FCH),
                    evac_act(HBF, RB("HBF"), "b_fc", func=AF.Gelu_apprx_tanh))
            if gf + 1 < GF:
                ln_stats(XS, xsg(lo + GTF), lo + GTF, GTF, 1e-10, pre_done=True)
            if dbg and s == 0 and gf == 0:
                tap("hg0", HBF, [128, FCH, GTF], RB("HBF"))
            proj_fm("wt_mlp", HBF, RB("HBF"), GTF, range(NCH),
                    ev_stage_add_xs("b_mlp", lo))
            if gf + 1 < GF:
                ln_finish(XS, xsg(lo + GTF), lo + GTF, GTF, UG, RB("UG"))
            if dbg and s == 0 and gf == 0:
                tap("t3g0", XS, [128, NCH, T], B("XS"))
            for c in range(NCH):
                src = x_d[s, c * 128:(c + 1) * 128, lo:lo + GTF]
                sl = c % 2
                sb.op("gpsimd", lambda e, src=src, sl=sl:
                      e.dma_start(out=XGC[:, sl, :], in_=src),
                      writes=[RB("XGC")], inc=16)
                sb.op("vector", lambda e, c=c, sl=sl, lo=lo: e.tensor_add(
                    XS[:, c, lo:lo + GTF], XS[:, c, lo:lo + GTF], XGC[:, sl, :]),
                    reads=[RB("XGC"), xsg(lo)], writes=[xsg(lo)])

    def phase_FFN(s, last):
        RING_N[0] = 4
        ln_pre(XS, xsg(0), 0, GTF)
        layernorm(XS, xsg(0), 0, GTF, 1e-5, UG, RB("UG"), pre_done=True)
        pend = []

        def drain():
            for _ in range(2):
                if pend:
                    pend.pop(0)()
        for gf in range(GF):
            lo = gf * GTF
            xsb = xsg(lo)
            if gf + 1 < GF:
                ln_pre(XS, xsg(lo + GTF), lo + GTF, GTF)

            def ev_relu2(m, c0, cw, p):
                sl = m % 2
                sb.op("scalar", lambda e, m=m, cw=cw, p=p, sl=sl: e.activation(
                    RS32[:, sl, 0:cw], PS[p][:, 0:cw], AF.Relu),
                    reads=[psb[p]], writes=[RB("RS32")])
                sb.op("vector", lambda e, m=m, cw=cw, sl=sl: e.tensor_mul(
                    HBF[:, m, 0:cw], RS32[:, sl, 0:cw], RS32[:, sl, 0:cw]),
                    reads=[RB("RS32")], writes=[RB("HBF")])
            proj_fm("wt_ffn_fc", UG, RB("UG"), GTF, range(FCH), ev_relu2,
                    post_m=drain)
            if gf + 1 < GF:
                ln_stats(XS, xsg(lo + GTF), lo + GTF, GTF, 1e-5, pre_done=True)
            while pend:
                pend.pop(0)()

            yb = RB("SPKY")
            extra0 = [B("KB")] if gf == 0 else []

            def ev_y(m, c0, cw, p):
                er = extra0 if m == 0 else []
                sb.op("scalar", lambda e, m=m, cw=cw, p=p: e.activation(
                    SPKY[:, m, 0:cw], PS[p][:, 0:cw], AF.Copy),
                    reads=[psb[p]] + er, writes=[yb])
            proj_fm("wt_ffn_proj", HBF, RB("HBF"), GTF, range(NCH), ev_y)
            if gf + 1 < GF:
                ln_finish(XS, xsg(lo + GTF), lo + GTF, GTF, UG, RB("UG"))
            mb, sbb = B("SPKM"), RB("SBF")
            for h in (0, 1):
                Y = SPKY[:, 4 * h:4 * h + 4, :]
                M_ = SPKM[:, 4 * h:4 * h + 4, :]
                S_ = SBF[:, :, :]
                XSg = XS[:, 4 * h:4 * h + 4, lo:lo + GTF]
                ONE = EONEC[:, 0:1].unsqueeze(1).broadcast_to([128, 4, GTF])
                er0 = extra0 if (gf == 0 and h == 0) else []

                def gt(src, src_b, S_=S_, ONE=ONE):
                    pend.append(lambda src=src, src_b=src_b: sb.op(
                        "vector", lambda e: e.tensor_tensor(
                            S_, src, ONE, ALU.is_gt),
                        reads=[src_b, B("EONEC")], writes=[sbb]))

                def acc_xs(XSg=XSg, S_=S_, xsb=xsb):
                    pend.append(lambda: sb.op(
                        "vector", lambda e: e.scalar_tensor_tensor(
                            XSg, S_, 0.2, XSg, ALU.mult, ALU.add),
                        reads=[sbb, xsb], writes=[xsb]))

                gt(Y, yb)
                acc_xs()
                pend.append(lambda M_=M_, Y=Y, er0=er0: sb.op(
                    "vector", lambda e: e.scalar_tensor_tensor(
                        M_, Y, 0.9, Y, ALU.mult, ALU.add),
                    reads=[yb] + er0, writes=[mb]))
                pend.append(lambda M_=M_, S_=S_: sb.op(
                    "vector", lambda e: e.tensor_sub(M_, M_, S_),
                    reads=[sbb, mb], writes=[mb]))
                for _t in range(3):
                    gt(M_, mb)
                    acc_xs()
                    pend.append(lambda M_=M_, Y=Y: sb.op(
                        "vector", lambda e: e.scalar_tensor_tensor(
                            M_, M_, 0.9, Y, ALU.mult, ALU.add),
                        reads=[yb, mb], writes=[mb]))
                    pend.append(lambda M_=M_, S_=S_: sb.op(
                        "vector", lambda e: e.tensor_sub(M_, M_, S_),
                        reads=[sbb, mb], writes=[mb]))
                gt(M_, mb)
                acc_xs()
        while pend:
            pend.pop(0)()
        if last:
            for gf in range(GF):
                lo = gf * GTF
                dst = y_d[s, :, lo:lo + GTF].rearrange("(c p) t -> p c t", p=128)
                sb.op("gpsimd", lambda e, dst=dst, lo=lo: e.dma_start(
                    out=dst, in_=XS[:, :, lo:lo + GTF]), reads=[xsg(lo)], inc=16)

    def phase_C(s):
        RING_N[0] = 3
        for g in range(G):
            lo = g * GT
            layernorm(XS, xsg(lo), lo, GT, 1e-5, UG, RB("UG"))
            proj_fm("wt_p", UG, RB("UG"), GT, range(NCH),
                    evac_act(XBG, RB("XBG"), "b_p"))

            def ev_into(dst_t, dst_b, nmb, lo=lo, g=g):
                def ev(m, c0, cw, p):
                    er = ([B("SPKY"), B("SPKM")]
                          if (g == 0 and m == 0) else [])
                    sb.op("scalar", lambda e, m=m, c0=c0, cw=cw, p=p: e.activation(
                        dst_t[:, m, lo + c0:lo + c0 + cw], PS[p][:, 0:cw],
                        AF.Identity, bias=bias_ap(nmb, m)),
                        reads=[psb[p], B("BIAS")] + er, writes=[dst_b])
                return ev
            proj_fm("wt_k", XBG, RB("XBG"), GT, range(NCH),
                    ev_into(KB, RB("KB"), "b_k"))
            proj_fm("wt_v", XBG, RB("XBG"), GT, range(NCH),
                    ev_into(VC, RB("VB"), "b_v"))
        for g in range(G):
            lo = g * GT
            layernorm(XS, xsg(lo), lo, GT, 1e-5, UG, RB("UG"))
            proj_fm("wt_p", UG, RB("UG"), GT, range(NCH),
                    evac_act(XBG, RB("XBG"), "b_p"))
            proj_fm("wt_q", XBG, RB("XBG"), GT, range(NCH),
                    evac_act(YG, RB("YG"), "b_q"))
            for n_idx, dlt in ((0, -1), (1, 0), (2, 1)):
                for c in range(NCH):
                    jj = c % 2
                    lo_s = lo + dlt
                    phw = (list(PH) if (g == 0 and n_idx == 0 and c == 0)
                           else [])
                    if 0 <= lo_s and lo_s + GT <= T:
                        sb.op("vector", lambda e, c=c, jj=jj, lo_s=lo_s: e.tensor_mul(
                            PHAT[:, jj, :], YG[:, c, :], KB[:, c, lo_s:lo_s + GT]),
                            reads=[RB("YG"), RB("KB")],
                            writes=[RB("PHAT")] + phw)
                    elif lo_s < 0:
                        sb.op("vector", lambda e, c=c, jj=jj: e.tensor_mul(
                            PHAT[:, jj, 1:GT], YG[:, c, 1:GT], KB[:, c, 0:GT - 1]),
                            reads=[RB("YG"), RB("KB")],
                            writes=[RB("PHAT")] + phw)
                        sb.op("vector", lambda e, c=c, jj=jj: e.tensor_mul(
                            PHAT[:, jj, 0:1], YG[:, c, 0:1], KB[:, c, T - 1:T]),
                            reads=[RB("YG"), RB("KB")], writes=[RB("PHAT")])
                    else:
                        sb.op("vector", lambda e, c=c, jj=jj, lo_s=lo_s: e.tensor_mul(
                            PHAT[:, jj, 0:GT - 1], YG[:, c, 0:GT - 1],
                            KB[:, c, lo_s:T]),
                            reads=[RB("YG"), RB("KB")], writes=[RB("PHAT")])
                        sb.op("vector", lambda e, c=c, jj=jj: e.tensor_mul(
                            PHAT[:, jj, GT - 1:GT], YG[:, c, GT - 1:GT],
                            KB[:, c, 0:1]),
                            reads=[RB("YG"), RB("KB")], writes=[RB("PHAT")])
                    sb.op("tensor", lambda e, c=c, jj=jj, n_idx=n_idx: e.matmul(
                        PS[n_idx][0:16, :], EIND[:, c, :], PHAT[:, jj, :],
                        start=(c == 0), stop=(c == NCH - 1)),
                        reads=[RB("PHAT"), B("EIND")], writes=[psb[n_idx]])
            ABSn = (ABS0, ABS1, ABS2)
            for n in range(3):
                sb.op("vector", lambda e, n=n: e.tensor_copy(
                    ABSn[n][:, :], PS[n][0:16, :]),
                    reads=[psb[n]], writes=[RB("ABS0")])
            sb.op("vector", lambda e: e.tensor_max(ABSM[:, :], ABS0[:, :], ABS1[:, :]),
                  reads=[RB("ABS0")], writes=[RB("ABSM")])
            sb.op("vector", lambda e: e.tensor_max(ABSM[:, :], ABSM[:, :], ABS2[:, :]),
                  reads=[RB("ABS0")], writes=[RB("ABSM")])
            for n in range(3):
                sb.op("vector", lambda e, n=n: e.tensor_sub(
                    ABSn[n][:, :], ABSn[n][:, :], ABSM[:, :]),
                    reads=[RB("ABS0"), RB("ABSM")], writes=[RB("ABS0")])
                sb.op("scalar", lambda e, n=n: e.activation(
                    ABSn[n][:, :], ABSn[n][:, :], AF.Exp, scale=SC),
                    reads=[RB("ABS0")], writes=[RB("ABS0")])
            sb.op("vector", lambda e: e.tensor_add(ABSM[:, :], ABS0[:, :], ABS1[:, :]),
                  reads=[RB("ABS0")], writes=[RB("ABSM")])
            sb.op("vector", lambda e: e.tensor_add(ABSM[:, :], ABSM[:, :], ABS2[:, :]),
                  reads=[RB("ABS0")], writes=[RB("ABSM")])
            sb.op("vector", lambda e: e.reciprocal(
                RS32[0:16, 0, :], ABSM[:, :]),
                reads=[RB("ABSM")], writes=[RB("RS32")])
            for n in range(3):
                sb.op("vector", lambda e, n=n: e.tensor_mul(
                    WROW[n][:, :], ABSn[n][:, :], RS32[0:16, 0, :]),
                    reads=[RB("ABS0"), RB("RS32")], writes=[RB("WROW0")])
            for c in range(NCH):
                for n, dlt in ((0, -1), (1, 0), (2, 1)):
                    wslot = 2 + n % 2
                    sb.op("tensor", lambda e, c=c, n=n, wslot=wslot: e.matmul(
                        PS[3][:, :], FIND[:, c, :], WROW[n][:, :],
                        start=True, stop=True),
                        reads=[RB("WROW0"), B("FIND")], writes=[psb[3]])
                    sb.op("scalar", lambda e, wslot=wslot: e.activation(
                        PHAT[:, wslot, :], PS[3][:, :], AF.Copy),
                        reads=[psb[3]], writes=[RB("PHAT")])
                    lo_s = lo + dlt
                    dsti = 0 if n == 0 else 1
                    dst = RS32[:, dsti, :]

                    def ml(out_ap, w_ap, v_ap, extra_reads=()):
                        sb.op("vector", lambda e, out_ap=out_ap, w_ap=w_ap,
                              v_ap=v_ap: e.tensor_mul(out_ap, w_ap, v_ap),
                              reads=[RB("PHAT"), RB("VB")], writes=[RB("RS32")])
                    if 0 <= lo_s and lo_s + GT <= T:
                        ml(dst, PHAT[:, wslot, :], VC[:, c, lo_s:lo_s + GT])
                    elif lo_s < 0:
                        ml(RS32[:, dsti, 1:GT], PHAT[:, wslot, 1:GT],
                           VC[:, c, 0:GT - 1])
                        ml(RS32[:, dsti, 0:1], PHAT[:, wslot, 0:1],
                           VC[:, c, T - 1:T])
                    else:
                        ml(RS32[:, dsti, 0:GT - 1], PHAT[:, wslot, 0:GT - 1],
                           VC[:, c, lo_s:T])
                        ml(RS32[:, dsti, GT - 1:GT], PHAT[:, wslot, GT - 1:GT],
                           VC[:, c, 0:1])
                    if n == 1:
                        sb.op("vector", lambda e: e.tensor_add(
                            RS32[:, 0, :], RS32[:, 0, :], RS32[:, 1, :]),
                            reads=[RB("RS32")], writes=[RB("RS32")])
                    elif n == 2:
                        sb.op("vector", lambda e, c=c: e.tensor_add(
                            UG[:, c, :], RS32[:, 0, :], RS32[:, 1, :]),
                            reads=[RB("RS32")], writes=[RB("UG")])
            proj_fm("wt_o", UG, RB("UG"), GT, range(NCH),
                    ev_stage_add_xs("b_o", lo))

    import os as _os
    order = _os.environ.get("MK_PHASES", "ABCD")
    for s in range(SEQS):
        for ph in order:
            if ph == "A":
                phase_A(s)
                if s == 0:
                    tap("xs_A", XS, [128, NCH, T], B("XS"))
            elif ph == "B":
                phase_FFN(s, last=False)
                if s == 0:
                    tap("xs_B", XS, [128, NCH, T], B("XS"))
            elif ph == "C":
                phase_C(s)
                if s == 0:
                    tap("xs_C", XS, [128, NCH, T], B("XS"))
            elif ph == "D":
                phase_FFN(s, last=True)
                if s == 0:
                    tap("xs_D", XS, [128, NCH, T], B("XS"))
    sb.plan["sync"].append(Wait("sync", sb.cnt["sync"]))
    sb.plan["sync"].append(Wait("gpsimd", sb.cnt["gpsimd"]))

    sems = {e: nc.alloc_semaphore(f"sem_{e}") for e in ENGS}
    nops = {e: sum(1 for it in sb.plan[e] if isinstance(it, Op)) for e in ENGS}
    print("plan ops:", nops, "waits:",
          {e: len(sb.plan[e]) - nops[e] for e in ENGS})

    import os as _os2
    _imap = {}

    def replay(name):
        def run(eng):
            for idx, it in enumerate(sb.plan[name]):
                if isinstance(it, Wait):
                    ins = eng.wait_ge(sems[it.src], it.val)
                    if _os2.environ.get("MK_IMAP"):
                        _imap[ins.ins.name] = (name, idx, f"WAIT {it.src}>={it.val}")
                else:
                    ins = it.fn(eng)
                    ins.then_inc(sems[name], it.inc)
                    if _os2.environ.get("MK_IMAP"):
                        _imap[ins.ins.name] = (name, idx, "op")
        return run

    with nc.Block() as block:
        block.tensor(replay("tensor"))
        block.vector(replay("vector"))
        block.scalar(replay("scalar"))
        block.gpsimd(replay("gpsimd"))
        block.sync(replay("sync"))
    nc._mk_imap = _imap
    nc._mk_plan = sb.plan
    return nc


_NC_CACHE = {}


def kernel(**inputs):
    def tr(a):
        return np.ascontiguousarray(np.asarray(a, dtype=np.float32).T)

    def vec(a):
        a = np.asarray(a, dtype=np.float32)
        return np.ascontiguousarray(a.reshape(-1, 128).T)

    x = np.asarray(inputs["L0_attn_a_x"], dtype=np.float32)
    xT = np.ascontiguousarray(np.swapaxes(x, 1, 2))
    common = {
        "wt_attn": tr(inputs["L0_attn_a_w_attn"]),
        "wt_proj": tr(inputs["L0_attn_a_w_proj"]),
        "wt_fc": tr(inputs["L0_attn_a_w_fc"]),
        "wt_mlp": tr(inputs["L0_attn_a_w_mlp_proj"]),
        "wt_p": tr(inputs["L0_attn_b_Wp"]),
        "wt_q": tr(inputs["L0_attn_b_Wq"]),
        "wt_k": tr(inputs["L0_attn_b_Wk"]),
        "wt_v": tr(inputs["L0_attn_b_Wv"]),
        "wt_o": tr(inputs["L0_attn_b_Wo"]),
        "wt_ffn_fc": tr(inputs["L0_ffn_W_fc"]),
        "wt_ffn_proj": tr(inputs["L0_ffn_W_proj"]),
        "b_attn": vec(inputs["L0_attn_a_b_attn"]),
        "b_proj": vec(inputs["L0_attn_a_b_proj"]),
        "b_fc": vec(inputs["L0_attn_a_b_fc"]),
        "b_mlp": vec(inputs["L0_attn_a_b_mlp_proj"]),
        "b_p": vec(inputs["L0_attn_b_bp"]),
        "b_q": vec(inputs["L0_attn_b_bq"]),
        "b_k": vec(inputs["L0_attn_b_bk"]),
        "b_v": vec(inputs["L0_attn_b_bv"]),
        "b_o": vec(inputs["L0_attn_b_bo"]),
        "b_attn_v": np.ascontiguousarray(
            np.asarray(inputs["L0_attn_a_b_attn"], dtype=np.float32)[2048:].reshape(1, -1)),
    }
    if "nc" not in _NC_CACHE:
        _NC_CACHE["nc"] = build_program()
    nc = _NC_CACHE["nc"]
    in_maps = [dict(common, x=np.ascontiguousarray(xT[2 * c:2 * c + 2]))
               for c in range(NCORES)]
    res = run_bass_kernel_spmd(nc, in_maps, core_ids=list(range(NCORES)))
    yT = np.concatenate([np.asarray(r["y"], dtype=np.float32)
                         for r in res.results], axis=0)
    return np.ascontiguousarray(np.swapaxes(yT, 1, 2))


# version 28
# speedup vs baseline: 1.2065x; 1.1292x over previous
import numpy as np
import ml_dtypes
import concourse.bass as bass
import concourse.mybir as mybir
from concourse.bass_utils import run_bass_kernel_spmd

F32 = mybir.dt.float32
BF16 = mybir.dt.bfloat16
AF = mybir.ActivationFunctionType
ALU = mybir.AluOpType
BF = ml_dtypes.bfloat16

NCORES = 8
SEQS = 2
T = 2048
C = 1024
NCH = 8
H = 16
DH = 64
FC = 4096
FCH = 32
G = 4
GT = 512
GF = 4
GTF = 512
NB = 512
ESH = -24.0
SC = 0.125

ENGS = ("tensor", "vector", "scalar", "gpsimd", "sync")


class Buf:
    __slots__ = ("name", "writer", "readers")

    def __init__(self, name):
        self.name = name
        self.writer = None
        self.readers = []


class Op:
    __slots__ = ("fn", "inc", "line")

    def __init__(self, fn, inc):
        self.fn = fn
        self.inc = inc
        import sys as _s
        f = _s._getframe(2)
        self.line = f.f_lineno


class Wait:
    __slots__ = ("src", "val")

    def __init__(self, src, val):
        self.src = src
        self.val = val


class SB:

    def __init__(self):
        self.plan = {e: [] for e in ENGS}
        self.cnt = {e: 0 for e in ENGS}
        self.waited = {e: {} for e in ENGS}

    def op(self, eng, fn, reads=(), writes=(), inc=1):
        need = {}
        dma_q = eng in ("sync", "gpsimd")

        def add(pe, pv):
            if (pe != eng or dma_q) and pv > need.get(pe, 0):
                need[pe] = pv

        for b in reads:
            if b.writer is not None:
                add(*b.writer)
        for b in writes:
            if b.writer is not None:
                add(*b.writer)
            for r in b.readers:
                add(*r)
        for pe in sorted(need):
            pv = need[pe]
            if self.waited[eng].get(pe, 0) < pv:
                self.plan[eng].append(Wait(pe, pv))
                self.waited[eng][pe] = pv
        self.plan[eng].append(Op(fn, inc))
        self.cnt[eng] += inc
        newv = self.cnt[eng]
        for b in writes:
            b.writer = (eng, newv)
            b.readers = []
        for b in reads:
            b.readers.append((eng, newv))


def build_program(dbg=False):
    nc = bass.Bass("TRN2", target_bir_lowering=False, debug=False)

    x_d = nc.dram_tensor("x", [SEQS, C, T], F32, kind="ExternalInput").ap()
    y_d = nc.dram_tensor("y", [SEQS, C, T], F32, kind="ExternalOutput").ap()
    wts = {}
    for nm, k, f in [("wt_attn", C, 3 * C), ("wt_proj", C, C), ("wt_fc", C, FC),
                     ("wt_mlp", FC, C), ("wt_p", C, C), ("wt_q", C, C),
                     ("wt_k", C, C), ("wt_v", C, C), ("wt_o", C, C),
                     ("wt_ffn_fc", C, FC), ("wt_ffn_proj", FC, C)]:
        wts[nm] = nc.dram_tensor(nm, [k, f], F32, kind="ExternalInput").ap()
    wcols = {}
    woff = [0]

    def walloc(key, width):
        wcols[key] = (woff[0], width)
        woff[0] += width
    for _nm, _kd, _fd in [("wt_attn", C, 2 * C), ("wt_proj", C, C),
                          ("wt_fc", C, FC), ("wt_p", C, C), ("wt_q", C, C),
                          ("wt_k", C, C), ("wt_v", C, C), ("wt_o", C, C),
                          ("wt_ffn_fc", C, FC)]:
        for _m in range(_fd // 128):
            walloc((_nm, _m), (_kd // 128) * 128)
    for _nm in ("wt_mlp", "wt_ffn_proj"):
        for _m in range(NCH):
            for _h in range(2):
                walloc((_nm, _m, _h), 16 * 128)
    for _h in range(4):
        walloc(("wvs", _h), NCH * 256)
    wscr = nc.dram_tensor("wscr", [128, woff[0]], BF16, kind="Internal").ap()

    bss = {}
    for nm, f in [("b_attn", 3 * C), ("b_proj", C), ("b_fc", FC), ("b_mlp", C),
                  ("b_p", C), ("b_q", C), ("b_k", C), ("b_v", C), ("b_o", C)]:
        bss[nm] = nc.dram_tensor(nm, [128, f // 128], F32,
                                 kind="ExternalInput").ap()
    bv_row_d = nc.dram_tensor("b_attn_v", [1, C], F32, kind="ExternalInput").ap()

    ii = np.arange(896)[None, :]
    pp = np.arange(128)[:, None]
    mask_np = (ii >= pp + 384).astype(BF)
    e_np = np.zeros((128, NCH, 16), dtype=BF)
    f_np = np.zeros((16, NCH, 128), dtype=BF)
    for c in range(NCH):
        for p in range(128):
            h = 2 * c + p // 64
            e_np[p, c, h] = 1
            f_np[h, c, p] = 1
    ones_col_d = nc.inline_tensor(np.ones((128, 1), dtype=BF), name="d_onc").ap()
    esh_d = nc.inline_tensor(np.full((128, 1), ESH, dtype=np.float32),
                             name="d_esh").ap()
    eps5_d = nc.inline_tensor(np.full((128, 1), 1e-5, dtype=np.float32),
                              name="d_eps5").ap()
    eps10_d = nc.inline_tensor(np.full((128, 1), 1e-10, dtype=np.float32),
                               name="d_eps10").ap()
    one_d = nc.inline_tensor(np.ones((128, 1), dtype=np.float32),
                             name="d_one").ap()
    ones_row_d = nc.inline_tensor(np.ones((1, 128), dtype=BF), name="d_onr").ap()
    mask_d = nc.inline_tensor(np.ascontiguousarray(mask_np), name="d_mask").ap()
    e_d = nc.inline_tensor(np.ascontiguousarray(e_np), name="d_eind").ap()
    f_d = nc.inline_tensor(np.ascontiguousarray(f_np), name="d_find").ap()

    K = 1024
    SB_BASE = 16512
    A = nc.alloc_sbuf_tensor_at

    def at(name, shape, dtype, off):
        return A(name, shape, dtype, offset=SB_BASE + off)

    XS = at("XS", [128, NCH, T], F32, 0)
    KB = at("KB", [128, NCH, T], BF16, 64 * K)
    SPKY = at("SPKY", [128, NCH, GTF], F32, 64 * K)
    SPKM = at("SPKM", [128, NCH, GTF], F32, 64 * K + 16 * K)
    VB = at("VB", [128, 16, C], BF16, 96 * K)
    HBF = at("HBF", [128, FCH, GTF], BF16, 96 * K)
    VC = at("VC", [128, NCH, T], BF16, 96 * K)
    UG = at("UG", [128, NCH, GT], BF16, 130 * K)
    YG = at("YG", [128, NCH, GT], BF16, 138 * K)
    XBG = at("XBG", [128, NCH, GT], BF16, 146 * K)
    WROW = [at(f"WROW{i}", [16, GT], BF16, 146 * K + 5 * K + i * K)
            for i in range(3)]
    SQG = at("SQG", [128, NCH, GT], BF16, 154 * K)
    SRND = at("SRND", [33, NB], F32, 154 * K)
    SRNDB = at("SRNDB", [33, NB], BF16, 154 * K + 2 * K)
    SRNDB2 = at("SRNDB2", [1, NB], BF16, 154 * K + 3 * K)
    ABS0 = at("ABS0", [16, GT], F32, 154 * K)
    ABS1 = at("ABS1", [16, GT], F32, 154 * K + 2 * K)
    ABS2 = at("ABS2", [16, GT], F32, 154 * K + 4 * K)
    ABSM = at("ABSM", [16, GT], F32, 154 * K + 6 * K)
    QP = at("QP", [128, 2, NB], BF16, 162 * K)
    BVR = at("BVR", [1, C], BF16, 162 * K)
    PHAT = at("PHAT", [128, 4, NB], BF16, 164 * K)
    RS32 = at("RS32", [128, 2, NB], F32, 168 * K)
    BVRF = at("BVRF", [1, C], F32, 168 * K)
    RSB = at("RSB", [128, NB], F32, 172 * K)
    MRB = at("MRB", [128, 2, GT], F32, 174 * K)
    ST_S = at("ST_S", [1, GT], F32, 178 * K)
    ST_Q = at("ST_Q", [1, GT], F32, 180 * K)
    ST_R = at("ST_R", [1, GT], F32, 182 * K)
    ST_T = at("ST_T", [1, GT], F32, 154 * K)
    STB_M = at("STB_M", [1, GT], BF16, 184 * K)
    STB_R = at("STB_R", [1, GT], BF16, 185 * K)
    SBF = at("SBF", [128, 4, GTF], F32, 138 * K)
    XGC = at("XGC", [128, 2, GTF], F32, 186 * K)
    PSTG = at("PSTG", [128, 2, NCH, 128], BF16, 186 * K)
    WR = [at(f"WR{i}", [128, 16, 128], BF16, 190 * K + 4 * K * i) for i in range(3)]
    WR3 = at("WR3", [128, 16, 128], BF16, 164 * K)
    cb = 202 * K
    ONESC = at("ONESC", [128, 1], BF16, cb); cb += 32
    ONESR = at("ONESR", [1, 128], BF16, cb); cb += 256
    EIND = at("EIND", [128, NCH, 16], BF16, cb); cb += 256
    BIAS = at("BIAS", [128, 112], F32, cb); cb += 448
    MASK = at("MASK", [128, 896], BF16, cb); cb += 1792
    ESHC = at("ESHC", [128, 1], F32, cb); cb += 32
    EPS5C = at("EPS5C", [128, 1], F32, cb); cb += 32
    EPS10C = at("EPS10C", [128, 1], F32, cb); cb += 32
    EONEC = at("EONEC", [128, 1], F32, cb); cb += 32
    FIND = at("FIND", [16, NCH, 128], BF16, cb); cb += 2048
    assert cb <= 208 * K - 128, cb

    bias_off = {}
    off = 0
    for nm, f in [("b_attn", 3 * C), ("b_proj", C), ("b_fc", FC), ("b_mlp", C),
                  ("b_p", C), ("b_q", C), ("b_k", C), ("b_v", C), ("b_o", C)]:
        bias_off[nm] = off
        off += f // 128

    PSA = nc.alloc_psum_tensor("PSA", [128, 2, NB], F32)
    PSB = nc.alloc_psum_tensor("PSB", [128, 2, NB], F32)
    PS47 = [nc.alloc_psum_tensor(f"PS{i}", [128, NB], F32) for i in range(4, 8)]
    PS = [PSA[:, 0, :], PSA[:, 1, :], PSB[:, 0, :], PSB[:, 1, :]] + \
         [t[:, :] for t in PS47]

    sb = SB()
    bufs = {}

    def B(name):
        if name not in bufs:
            bufs[name] = Buf(name)
        return bufs[name]

    alias = { "HBF": "VB", "VC": "VB",
             "WROW0": "XBG", "WROW1": "XBG", "WROW2": "XBG",
             "SRND": "SQG", "SRNDB": "SQG", "SRNDB2": "SQG", "ST_T": "SQG",
             "ABS0": "SQG", "ABS1": "SQG", "ABS2": "SQG", "ABSM": "SQG",
             "BVR": "QP", "SBF": "YG", "PSTG": "XGC", "BVRF": "RS32",
             "WR3": "PHAT"}

    def RB(name):
        return B(alias.get(name, name))

    psb = [B(f"ps{i}") for i in range(8)]
    XSB = [B(f"XS_{i}") for i in range(G)]
    UGC = [B(f"UG_{c}") for c in range(NCH)]
    PH = [B(f"PH{j}") for j in range(4)]

    def xsg(lo):
        return XSB[lo // GT]

    for dst, src in [(ONESC, ones_col_d), (ONESR, ones_row_d), (MASK, mask_d),
                     (EIND, e_d), (FIND, f_d), (ESHC, esh_d),
                     (EPS5C, eps5_d), (EPS10C, eps10_d), (EONEC, one_d)]:
        sb.op("sync", lambda e, dst=dst, src=src: e.dma_start(out=dst[:], in_=src),
              writes=[B(dst.name)], inc=16)
    for nm in bias_off:
        o, w = bias_off[nm], bss[nm].shape[1]
        src = bss[nm]
        sb.op("sync", lambda e, o=o, w=w, src=src:
              e.dma_start(out=BIAS[:, o:o + w], in_=src),
              writes=[B("BIAS")], inc=16)

    def fill_order():
        yield ("wt_attn", C, 2 * C)
        yield ("wvs", None, None)
        yield ("wt_proj", C, C)
        yield ("wt_fc", C, FC)
        yield ("wt_mlp", None, None)
        yield ("wt_ffn_fc", C, FC)
        yield ("wt_ffn_proj", None, None)
        yield ("wt_p", C, C)
        yield ("wt_q", C, C)
        yield ("wt_k", C, C)
        yield ("wt_v", C, C)
        yield ("wt_o", C, C)

    fill_i = [0]
    pending_store = [None]

    def fill_tile(load_fn, store_fn):
        load_fn()
        if pending_store[0] is not None:
            pending_store[0]()
        pending_store[0] = store_fn

    for nm, kd, fd in fill_order():
        if nm == "wvs":
            for h in range(4):
                i = fill_i[0] % len(WR)
                fill_i[0] += 1
                slot, wbuf = WR[i], RB(f"WR{i}")
                o, w = wcols[("wvs", h)]
                src = wts["wt_attn"][:, 2 * C + h * 256:2 * C + (h + 1) * 256]
                src = src.rearrange("(c p) f -> p c f", p=128)
                fill_tile(
                    lambda slot=slot, src=src, wbuf=wbuf: sb.op(
                        "gpsimd", lambda e: e.dma_start(
                            out=slot.ap().rearrange("p a b -> p (a b)")
                            .rearrange("p (c f) -> p c f", c=NCH)[:, :, :], in_=src),
                        writes=[wbuf], inc=16),
                    lambda slot=slot, o=o, w=w, wbuf=wbuf, h=h: sb.op(
                        "gpsimd", lambda e: e.dma_start(
                            out=wscr[:, o:o + w],
                            in_=slot.ap().rearrange("p a b -> p (a b)")[:, 0:w]),
                        reads=[wbuf], writes=[B(("wvs", h))], inc=16))
            continue
        if nm in ("wt_mlp", "wt_ffn_proj"):
            for m in range(NCH):
                for h in range(2):
                    for hh in range(2):
                        i = fill_i[0] % 2
                        fill_i[0] += 1
                        stg = PSTG[:, i, :, :]
                        sbuf_ = RB("PSTG")
                        r0 = h * 2048 + hh * 1024
                        src = wts[nm][r0:r0 + 1024, m * 128:(m + 1) * 128]
                        src = src.rearrange("(c p) f -> p c f", p=128)
                        o, w = wcols[(nm, m, h)]
                        oo = o + hh * 1024
                        fill_tile(
                            lambda stg=stg, src=src, sbuf_=sbuf_: sb.op(
                                "gpsimd", lambda e: e.dma_start(
                                    out=stg, in_=src),
                                writes=[sbuf_], inc=16),
                            lambda stg=stg, oo=oo, sbuf_=sbuf_, nm=nm, m=m, h=h:
                            sb.op("gpsimd", lambda e: e.dma_start(
                                out=wscr[:, oo:oo + 1024],
                                in_=stg.rearrange("p a b -> p (a b)")),
                                reads=[sbuf_], writes=[B((nm, m, h))], inc=16))
            continue
        use_ring = nm == "wt_attn"
        for m in range(fd // 128):
            src = wts[nm][:, m * 128:(m + 1) * 128]
            src = src.rearrange("(c p) f -> p c f", p=128)
            o, w = wcols[(nm, m)]
            if use_ring:
                i = fill_i[0] % len(WR)
                fill_i[0] += 1
                slot, wbuf = WR[i], RB(f"WR{i}")
                fill_tile(
                    lambda slot=slot, src=src, wbuf=wbuf: sb.op(
                        "gpsimd", lambda e: e.dma_start(
                            out=slot[:, 0:NCH, :], in_=src), writes=[wbuf], inc=16),
                    lambda slot=slot, o=o, w=w, wbuf=wbuf, nm=nm, m=m: sb.op(
                        "gpsimd", lambda e: e.dma_start(
                            out=wscr[:, o:o + w],
                            in_=slot.ap().rearrange("p a b -> p (a b)")[:, 0:w]),
                        reads=[wbuf], writes=[B((nm, m))], inc=16))
            else:
                i = fill_i[0] % 2
                fill_i[0] += 1
                stg = PSTG[:, i, :, :]
                sbuf_ = RB("PSTG")
                fill_tile(
                    lambda stg=stg, src=src, sbuf_=sbuf_: sb.op(
                        "gpsimd", lambda e: e.dma_start(out=stg, in_=src),
                        writes=[sbuf_], inc=16),
                    lambda stg=stg, o=o, w=w, sbuf_=sbuf_, nm=nm, m=m: sb.op(
                        "gpsimd", lambda e: e.dma_start(
                            out=wscr[:, o:o + w],
                            in_=stg.rearrange("p a b -> p (a b)")),
                        reads=[sbuf_], writes=[B((nm, m))], inc=16))
    if pending_store[0] is not None:
        pending_store[0]()
        pending_store[0] = None

    def bias_ap(nm, chunk):
        return BIAS[:, bias_off[nm] + chunk:bias_off[nm] + chunk + 1]

    wr_i = [0]
    RING_N = [3]
    wpair_cache = {}

    def ring_slot():
        i = wr_i[0] % RING_N[0]
        wr_i[0] += 1
        if i < 3:
            return WR[i], RB(f"WR{i}"), []
        return WR3, RB("WR3"), list(PH)

    def wtile_pair(nm, m0):
        key = (nm, m0)
        if key in wpair_cache:
            return wpair_cache[key]
        slot, wbuf, wex = ring_slot()
        o, w = wcols[(nm, m0)]
        o2, w2 = wcols[(nm, m0 + 1)]
        assert o2 == o + w
        sb.op("sync", lambda e, slot=slot, o=o, w=w, w2=w2: e.dma_start(
            out=slot.ap().rearrange("p a b -> p (a b)")[:, 0:w + w2],
            in_=wscr[:, o:o + w + w2]),
            reads=[B((nm, m0)), B((nm, m0 + 1))], writes=[wbuf] + wex, inc=16)
        wpair_cache.clear()
        wpair_cache[key] = (slot, wbuf)
        return slot, wbuf

    def wtile_small(nm, m):
        m0 = m - (m % 2)
        slot, wbuf = wtile_pair(nm, m0)
        sub = slot.ap()[:, 8 * (m % 2):8 * (m % 2) + NCH, :]
        return sub, wbuf

    def wtile_large(nm, m, half):
        slot, wbuf, wex = ring_slot()
        o, w = wcols[(nm, m, half)]
        sb.op("sync", lambda e, slot=slot, o=o, w=w: e.dma_start(
            out=slot.ap().rearrange("p a b -> p (a b)"), in_=wscr[:, o:o + w]),
            reads=[B((nm, m, half))], writes=[wbuf] + wex, inc=16)
        wpair_cache.clear()
        return slot, wbuf

    def newton_rsqrt(src, dst, tmp, rbufs, n):
        rbuf = rbufs
        I32 = mybir.dt.int32
        sb.op("vector", lambda e: e.tensor_single_scalar(
            dst.bitcast(I32), src.bitcast(I32), 1, ALU.arith_shift_right),
            reads=rbuf, writes=rbuf)
        sb.op("vector", lambda e: e.tensor_scalar(
            dst.bitcast(I32), dst.bitcast(I32), -1, 0x5F3759DF, ALU.mult, ALU.add),
            reads=rbuf, writes=rbuf)
        for _ in range(2):
            sb.op("vector", lambda e: e.tensor_mul(tmp, dst, dst),
                  reads=rbuf, writes=rbuf)
            sb.op("vector", lambda e: e.tensor_mul(tmp, tmp, src),
                  reads=rbuf, writes=rbuf)
            sb.op("vector", lambda e: e.tensor_scalar(tmp, tmp, -0.5, 1.5,
                                                      ALU.mult, ALU.add),
                  reads=rbuf, writes=rbuf)
            sb.op("vector", lambda e: e.tensor_mul(dst, dst, tmp),
                  reads=rbuf, writes=rbuf)

    def ln_pre(src_t, src_buf, lo, n):
        xbg, sqg = RB("XBG"), RB("SQG")
        sb.op("scalar", lambda e: e.activation(
            XBG[:, :, 0:n], src_t[:, :, lo:lo + n], AF.Copy),
            reads=[src_buf], writes=[xbg])
        sb.op("scalar", lambda e: e.activation(
            SQG[:, :, 0:n], src_t[:, :, lo:lo + n], AF.Square),
            reads=[src_buf], writes=[sqg])

    def ln_stats(src_t, src_buf, lo, n, eps, pre_done=False):
        stat = B("STAT_ROWS")
        xbg, sqg = RB("XBG"), RB("SQG")
        if not pre_done:
            ln_pre(src_t, src_buf, lo, n)
        nblk = (n + NB - 1) // NB
        for bi in range(nblk):
            c0, cw = bi * NB, min(NB, n - bi * NB)
            for c in range(NCH):
                sb.op("tensor", lambda e, c=c, c0=c0, cw=cw: e.matmul(
                    PS[0][0:1, 0:cw], ONESC[:], XBG[:, c, c0:c0 + cw],
                    start=(c == 0), stop=(c == NCH - 1)),
                    reads=[xbg, B("ONESC")], writes=[psb[0]])
            for c in range(NCH):
                sb.op("tensor", lambda e, c=c, c0=c0, cw=cw: e.matmul(
                    PS[1][0:1, 0:cw], ONESC[:], SQG[:, c, c0:c0 + cw],
                    start=(c == 0), stop=(c == NCH - 1)),
                    reads=[sqg, B("ONESC")], writes=[psb[1]])
            epsc = EPS5C if eps > 1e-7 else EPS10C
            sb.op("scalar", lambda e, c0=c0, cw=cw: e.activation(
                ST_S[0:1, c0:c0 + cw], PS[0][0:1, 0:cw], AF.Identity,
                scale=1.0 / C),
                reads=[psb[0]], writes=[stat])
            sb.op("scalar", lambda e, c0=c0, cw=cw, epsc=epsc: e.activation(
                ST_Q[0:1, c0:c0 + cw], PS[1][0:1, 0:cw], AF.Identity,
                bias=epsc[0:1, 0:1], scale=1.0 / C),
                reads=[psb[1], B(epsc.name)], writes=[stat])
        sn = ST_S[0:1, 0:n]
        qn = ST_Q[0:1, 0:n]
        rn = ST_R[0:1, 0:n]
        tn = ST_T[0:1, 0:n]
        sb.op("vector", lambda e: e.tensor_mul(tn, sn, sn),
              reads=[stat, sqg], writes=[stat, sqg])
        sb.op("vector", lambda e: e.tensor_sub(qn, qn, tn),
              reads=[stat, sqg], writes=[stat])
        newton_rsqrt(qn, rn, tn, [stat, sqg], n)
        sb.op("vector", lambda e: e.tensor_copy(STB_M[0:1, 0:n], sn),
              reads=[stat], writes=[stat])
        sb.op("vector", lambda e: e.tensor_copy(STB_R[0:1, 0:n], rn),
              reads=[stat], writes=[stat])

    def ln_finish(src_t, src_buf, lo, n, dst_t, dst_buf, dst_lo=None,
                  f32_dst=False):
        if dst_lo is None:
            dst_lo = 0
        stat = B("STAT_ROWS")
        mrb = RB("MRB")
        nblk = (n + NB - 1) // NB
        for bi in range(nblk):
            c0, cw = bi * NB, min(NB, n - bi * NB)
            sb.op("tensor", lambda e, c0=c0, cw=cw: e.matmul(
                PS[2][:, 0:cw], ONESR[:], STB_M[0:1, c0:c0 + cw],
                start=True, stop=True), reads=[stat, B("ONESR")], writes=[psb[2]])
            sb.op("tensor", lambda e, c0=c0, cw=cw: e.matmul(
                PS[3][:, 0:cw], ONESR[:], STB_R[0:1, c0:c0 + cw],
                start=True, stop=True), reads=[stat, B("ONESR")], writes=[psb[3]])
            sb.op("scalar", lambda e, c0=c0, cw=cw: e.activation(
                MRB[:, 0, c0:c0 + cw], PS[2][:, 0:cw], AF.Copy),
                reads=[psb[2]], writes=[mrb])
            sb.op("scalar", lambda e, c0=c0, cw=cw: e.activation(
                MRB[:, 1, c0:c0 + cw], PS[3][:, 0:cw], AF.Copy),
                reads=[psb[3]], writes=[mrb])
        if dst_t is UG and not f32_dst:
            for c in range(NCH):
                sb.op("vector", lambda e, c=c: e.tensor_sub(
                    dst_t[:, c, dst_lo:dst_lo + n], src_t[:, c, lo:lo + n],
                    MRB[:, 0, 0:n]),
                    reads=[src_buf, mrb], writes=[UGC[c]])
                sb.op("vector", lambda e, c=c: e.tensor_mul(
                    dst_t[:, c, dst_lo:dst_lo + n],
                    dst_t[:, c, dst_lo:dst_lo + n], MRB[:, 1, 0:n]),
                    reads=[mrb, UGC[c]], writes=[UGC[c]])
            return
        mb = MRB[:, 0, 0:n].unsqueeze(1).broadcast_to([128, NCH, n])
        rb = MRB[:, 1, 0:n].unsqueeze(1).broadcast_to([128, NCH, n])
        sb.op("vector", lambda e: e.tensor_sub(
            dst_t[:, :, dst_lo:dst_lo + n], src_t[:, :, lo:lo + n], mb),
            reads=[src_buf, mrb], writes=[dst_buf])
        sb.op("vector", lambda e: e.tensor_mul(
            dst_t[:, :, dst_lo:dst_lo + n], dst_t[:, :, dst_lo:dst_lo + n], rb),
            reads=[mrb, dst_buf], writes=[dst_buf])

    def layernorm(src_t, src_buf, lo, n, eps, dst_t, dst_buf, dst_lo=None,
                  f32_dst=False, pre_done=False):
        ln_stats(src_t, src_buf, lo, n, eps, pre_done=pre_done)
        ln_finish(src_t, src_buf, lo, n, dst_t, dst_buf, dst_lo=dst_lo,
                  f32_dst=f32_dst)

    def proj_fm(nm_wt, src_t, src_buf, n, mlist, evac, banks=(4, 5, 6, 7),
                src_lo=0, post_m=None):
        per_chunk = src_t is UG
        """For each m in mlist: psum = Wt_m^T @ src[:, :, src_lo:src_lo+n]
        (accumulated over K chunks), then evac(m, c0, cw, bank)."""
        kdim = wts[nm_wt].shape[0]
        nblk = (n + NB - 1) // NB
        bi_rr = [0]
        for m in mlist:
            if kdim == C:
                slot, wbuf = wtile_small(nm_wt, m)
                halves = [(slot, wbuf, 0, NCH)]
            else:
                s0, b0 = wtile_large(nm_wt, m, 0)
                s1, b1 = wtile_large(nm_wt, m, 1)
                halves = [(s0, b0, 0, 16), (s1, b1, 16, 16)]
            for bi in range(nblk):
                c0, cw = bi * NB, min(NB, n - bi * NB)
                p = banks[bi_rr[0] % len(banks)]
                bi_rr[0] += 1
                nk = kdim // 128
                ki = 0
                for slot, wbuf, kofs, kn in halves:
                    for k in range(kn):
                        sbf_ = UGC[kofs + k] if per_chunk else src_buf
                        sb.op("tensor", lambda e, slot=slot, k=k, kofs=kofs,
                              c0=c0, cw=cw, p=p, ki=ki: e.matmul(
                                  PS[p][:, 0:cw], slot[:, k, :],
                                  src_t[:, kofs + k, src_lo + c0:src_lo + c0 + cw],
                                  start=(ki == 0), stop=(ki == nk - 1)),
                              reads=[sbf_, wbuf], writes=[psb[p]])
                        ki += 1
                evac(m, c0, cw, p)
            if post_m is not None:
                post_m()

    def evac_act(dst_t, dst_buf, nm_bias, func=None, dst_lo=0, scale=1.0):
        def ev(m, c0, cw, p):
            bias = bias_ap(nm_bias, m) if nm_bias else 0.0
            nonlocal func
            f = func if func is not None else (AF.Identity if nm_bias else AF.Copy)
            sb.op("scalar", lambda e, m=m, c0=c0, cw=cw, p=p, f=f, bias=bias:
                  e.activation(
                      dst_t[:, m, dst_lo + c0:dst_lo + c0 + cw], PS[p][:, 0:cw],
                      f, bias=bias, scale=scale),
                  reads=[psb[p], B("BIAS")], writes=[dst_buf])
        return ev

    dbg_outs = []

    def tap(name, sb_tensor, shape, buf):
        if not dbg:
            return
        d = nc.dram_tensor("dbg_" + name, shape, F32, kind="ExternalOutput").ap()
        rb = [B(f"XS_{i}") for i in range(G)] if sb_tensor is XS else [buf]
        sb.op("gpsimd", lambda e: e.dma_start(out=d, in_=sb_tensor[:, :, :]),
              reads=rb, inc=16)
        dbg_outs.append(name)

    def ev_stage_add_xs(nm_bias, xs_lo):
        def ev(m, c0, cw, p):
            sl = m % 2
            sb.op("scalar", lambda e, m=m, c0=c0, cw=cw, p=p, sl=sl: e.activation(
                RS32[:, sl, 0:cw], PS[p][:, 0:cw], AF.Identity,
                bias=bias_ap(nm_bias, m)),
                reads=[psb[p], B("BIAS")], writes=[RB("RS32")])
            sb.op("vector", lambda e, m=m, c0=c0, cw=cw, sl=sl: e.tensor_add(
                XS[:, m, xs_lo + c0:xs_lo + c0 + cw],
                XS[:, m, xs_lo + c0:xs_lo + c0 + cw], RS32[:, sl, 0:cw]),
                reads=[RB("RS32"), xsg(xs_lo)], writes=[xsg(xs_lo)])
        return ev

    def phase_A(s):
        RING_N[0] = 3
        for g in range(G):
            lo = g * GT
            src = x_d[s, :, lo:lo + GT].rearrange("(c p) t -> p c t", p=128)
            sb.op("sync", lambda e, src=src, lo=lo:
                  e.dma_start(out=XS[:, :, lo:lo + GT], in_=src),
                  writes=[xsg(lo)], inc=16)
        ln_pre(XS, xsg(0), 0, GT)
        for g in range(G):
            lo = g * GT
            layernorm(XS, xsg(lo), lo, GT, 1e-5, XS, xsg(lo), dst_lo=lo,
                      f32_dst=True, pre_done=True)
            if g + 1 < G:
                ln_pre(XS, xsg(lo + GT), lo + GT, GT)
        if dbg and s == 0:
            tap("t", XS, [128, NCH, T], B("XS"))
        sb.op("sync", lambda e: e.dma_start(
            out=BVRF[:, :], in_=bv_row_d),
            writes=[RB("BVRF")], inc=16)
        sb.op("vector", lambda e: e.tensor_copy(BVR[:, :], BVRF[:, :]),
              reads=[RB("BVRF")], writes=[RB("BVR")])
        ln_pre(XS, xsg(0), 0, GT)
        layernorm(XS, xsg(0), 0, GT, 1e-10, UG, RB("UG"), pre_done=True)
        for g in range(G):
            lo = g * GT
            if g + 1 < G:
                ln_pre(XS, xsg(lo + GT), lo + GT, GT)

            def ev_k(m, c0, cw, p, lo=lo, g=g):
                er = ([B("SPKY"), B("SPKM")]
                      if (g == 0 and m == 8) else [])
                sb.op("scalar", lambda e, m=m, c0=c0, cw=cw, p=p: e.activation(
                    KB[:, m - 8, lo + c0:lo + c0 + cw], PS[p][:, 0:cw],
                    AF.Identity, bias=bias_ap("b_attn", m)),
                    reads=[psb[p], B("BIAS")] + er, writes=[RB("KB")])
            proj_fm("wt_attn", UG, RB("UG"), GT, range(8, 16), ev_k)
            if g + 1 < G:
                ln_stats(XS, xsg(lo + GT), lo + GT, GT, 1e-10, pre_done=True)
            for nb in range(4):
                i = wr_i[0] % len(WR)
                wr_i[0] += 1
                vslot, vbuf = WR[i], RB(f"WR{i}")
                wpair_cache.clear()
                o, w = wcols[("wvs", nb)]
                sb.op("sync", lambda e, vslot=vslot, o=o, w=w: e.dma_start(
                    out=vslot.ap().rearrange("p a b -> p (a b)")[:, 0:w],
                    in_=wscr[:, o:o + w]),
                    reads=[B(("wvs", nb))], writes=[vbuf], inc=16)
                vv = vslot.ap().rearrange("p a b -> p (a b)")[:, 0:2048]
                vv = vv.rearrange("p (c f) -> p c f", c=NCH)
                for ti in range(4):
                    gti = 4 * g + ti
                    p = 4 + ti % 2
                    for k in range(NCH):
                        sb.op("tensor", lambda e, k=k, ti=ti, p=p, vv=vv: e.matmul(
                            PS[p][:, 0:256], UG[:, k, ti * 128:(ti + 1) * 128],
                            vv[:, k, :], start=(k == 0), stop=False),
                            reads=[RB("UG"), vbuf], writes=[psb[p]])
                    sb.op("tensor", lambda e, nb=nb, p=p: e.matmul(
                        PS[p][:, 0:256], ONESR[:, :], BVR[0:1, nb * 256:(nb + 1) * 256],
                        start=False, stop=True),
                        reads=[RB("BVR"), B("ONESR")], writes=[psb[p]])
                    sb.op("scalar", lambda e, gti=gti, nb=nb, p=p: e.activation(
                        VB[:, gti, nb * 256:(nb + 1) * 256], PS[p][:, 0:256], AF.Copy),
                        reads=[psb[p]], writes=[RB("VB")])
            if g + 1 < G:
                ln_finish(XS, xsg(lo + GT), lo + GT, GT, UG, RB("UG"))
        if dbg and s == 0:
            tap("k", KB, [128, NCH, T], RB("KB"))
            tap("v", VB, [128, 16, C], RB("VB"))
        for g in range(G):
            lo = g * GT
            cmax = 4 * g + 4
            layernorm(XS, xsg(lo), lo, GT, 1e-10, UG, RB("UG"))
            if dbg and s == 0 and g == 0:
                tap("ua2g0", UG, [128, NCH, GT], RB("UG"))
            pending_ep = [None]
            for hp in range(NCH):
                qs = hp % 2
                pv = 4 + hp % 2
                slot, wbuf = wtile_small("wt_attn", hp)
                for k in range(NCH):
                    sb.op("tensor", lambda e, slot=slot, k=k: e.matmul(
                        PS[7][:, :], slot[:, k, :], UG[:, k, 0:GT],
                        start=(k == 0), stop=(k == NCH - 1)),
                        reads=[RB("UG"), wbuf], writes=[psb[7]])
                sb.op("scalar", lambda e, hp=hp, qs=qs: e.activation(
                    QP[:, qs, :], PS[7][:, :], AF.Identity,
                    bias=bias_ap("b_attn", hp)),
                    reads=[psb[7], B("BIAS")], writes=[RB("QP")])
                if dbg and s == 0 and g == 0 and hp == 0:
                    tap("qp0", QP, [128, 2, NB], RB("QP"))

                def st_pair(c, hp=hp, qs=qs):
                    pe_ = 2 * (c % 2)
                    po_ = pe_ + 1
                    sb.op("tensor", lambda e, c=c, hp=hp, qs=qs, pe_=pe_: e.matmul(
                        PS[pe_][:, :], KB[0:64, hp, c * 128:(c + 1) * 128],
                        QP[0:64, qs, :], start=True, stop=True,
                        tile_position=(0, 0)),
                        reads=[RB("KB"), RB("QP")], writes=[psb[pe_]])
                    sb.op("tensor", lambda e, c=c, hp=hp, qs=qs, po_=po_: e.matmul(
                        PS[po_][:, :], KB[64:128, hp, c * 128:(c + 1) * 128],
                        QP[64:128, qs, :], start=True, stop=True,
                        tile_position=(64, 0)),
                        reads=[RB("KB"), RB("QP")], writes=[psb[po_]])

                def finish_c(c, hp=hp, pv=pv, cmax=cmax, g=g):
                    pe_ = 2 * (c % 2)
                    st = (c == 0)
                    sp = (c == cmax - 1)
                    diag = c >= 4 * g
                    k0 = 128 * c - 512 * g
                    for j, vlo, tp, prow in ((pe_, 0, (0, 0), 0),
                                             (pe_ + 1, 64, (0, 64), 32)):
                        ew = ([RB("PHAT")]
                              if (g == 0 and hp == 0 and c == 0 and j == pe_)
                              else [])
                        sb.op("scalar", lambda e, j=j: e.activation(
                            PHAT[:, j, :], PS[j][:, :], AF.Exp,
                            bias=ESHC[:, 0:1], scale=SC),
                            reads=[psb[j]], writes=[PH[j]] + ew)
                        if diag:
                            ms = MASK[:, 384 - k0:384 - k0 + NB]
                            sb.op("vector", lambda e, j=j, ms=ms: e.tensor_mul(
                                PHAT[:, j, :], PHAT[:, j, :], ms),
                                reads=[PH[j], B("MASK")], writes=[PH[j]])
                        sb.op("tensor", lambda e, c=c, hp=hp, j=j, st=st,
                              sp=sp, pv=pv, vlo=vlo, tp=tp: e.matmul(
                            PS[pv][vlo:vlo + 64, :],
                            VB[:, c, 128 * hp + vlo:128 * hp + vlo + 64],
                            PHAT[:, j, :], start=st, stop=sp, tile_position=tp,
                            skip_group_check=True),
                            reads=[RB("VB"), PH[j]], writes=[psb[pv]])
                        sb.op("tensor", lambda e, j=j, st=st, sp=sp,
                              prow=prow: e.matmul(
                            PS[6][prow:prow + 1, :], ONESC[:, :], PHAT[:, j, :],
                            start=st, stop=sp, tile_position=(0, prow),
                            skip_group_check=True),
                            reads=[B("ONESC"), PH[j]], writes=[psb[6]])

                st_pair(0)
                if pending_ep[0] is not None:
                    pending_ep[0]()
                    pending_ep[0] = None
                for c in range(cmax):
                    if c + 1 < cmax:
                        st_pair(c + 1)
                    finish_c(c)

                def epilogue(hp=hp, pv=pv):
                    sb.op("vector", lambda e: e.tensor_copy(
                        SRND[0:1, :], PS[6][0:1, :]),
                        reads=[psb[6]], writes=[RB("SRND")])
                    sb.op("vector", lambda e: e.tensor_copy(
                        SRND[32:33, :], PS[6][32:33, :]),
                        reads=[psb[6]], writes=[RB("SRND")])
                    sb.op("vector", lambda e: e.reciprocal(
                        SRND[0:33, :], SRND[0:33, :]),
                        reads=[RB("SRND")], writes=[RB("SRND")])
                    sb.op("vector", lambda e: e.tensor_copy(
                        SRNDB[0:33, :], SRND[0:33, :]),
                        reads=[RB("SRND")], writes=[RB("SRND")])
                    sb.op("vector", lambda e: e.tensor_copy(
                        SRNDB2[0:1, :], SRNDB[32:33, :]),
                        reads=[RB("SRND")], writes=[RB("SRND")])
                    sb.op("tensor", lambda e: e.matmul(
                        PS[7][0:64, :], ONESR[0:1, 0:64], SRNDB[0:1, :],
                        start=True, stop=True, tile_position=(0, 0)),
                        reads=[RB("SRND"), B("ONESR")], writes=[psb[7]])
                    sb.op("tensor", lambda e: e.matmul(
                        PS[7][64:128, :], ONESR[0:1, 0:64], SRNDB2[0:1, :],
                        start=True, stop=True, tile_position=(0, 64)),
                        reads=[RB("SRND"), B("ONESR")], writes=[psb[7]])
                    sb.op("scalar", lambda e: e.activation(
                        RSB[:, :], PS[7][:, :], AF.Copy),
                        reads=[psb[7]], writes=[B("RSB")])
                    sb.op("vector", lambda e, hp=hp, pv=pv: e.tensor_mul(
                        YG[:, hp, :], PS[pv][:, :], RSB[:, :]),
                        reads=[psb[pv], B("RSB")], writes=[RB("YG")])
                pending_ep[0] = epilogue
            pending_ep[0]()
            if dbg and s == 0 and g == 0:
                tap("yg0", YG, [128, NCH, GT], RB("YG"))
            proj_fm("wt_proj", YG, RB("YG"), GT, range(NCH),
                    ev_stage_add_xs("b_proj", lo))
        if dbg and s == 0:
            tap("t2", XS, [128, NCH, T], B("XS"))
        RING_N[0] = 4
        ln_pre(XS, xsg(0), 0, GTF)
        layernorm(XS, xsg(0), 0, GTF, 1e-10, UG, RB("UG"), pre_done=True)
        for gf in range(GF):
            lo = gf * GTF
            if gf + 1 < GF:
                ln_pre(XS, xsg(lo + GTF), lo + GTF, GTF)
            if dbg and s == 0 and gf == 0:
                tap("u2g0", UG, [128, NCH, GT], RB("UG"))
            proj_fm("wt_fc", UG, RB("UG"), GTF, range(FCH),
                    evac_act(HBF, RB("HBF"), "b_fc", func=AF.Gelu_apprx_tanh))
            if gf + 1 < GF:
                ln_stats(XS, xsg(lo + GTF), lo + GTF, GTF, 1e-10, pre_done=True)
            if dbg and s == 0 and gf == 0:
                tap("hg0", HBF, [128, FCH, GTF], RB("HBF"))
            proj_fm("wt_mlp", HBF, RB("HBF"), GTF, range(NCH),
                    ev_stage_add_xs("b_mlp", lo))
            if gf + 1 < GF:
                ln_finish(XS, xsg(lo + GTF), lo + GTF, GTF, UG, RB("UG"))
            if dbg and s == 0 and gf == 0:
                tap("t3g0", XS, [128, NCH, T], B("XS"))
            for c in range(NCH):
                src = x_d[s, c * 128:(c + 1) * 128, lo:lo + GTF]
                sl = c % 2
                sb.op("gpsimd", lambda e, src=src, sl=sl:
                      e.dma_start(out=XGC[:, sl, :], in_=src),
                      writes=[RB("XGC")], inc=16)
                sb.op("vector", lambda e, c=c, sl=sl, lo=lo: e.tensor_add(
                    XS[:, c, lo:lo + GTF], XS[:, c, lo:lo + GTF], XGC[:, sl, :]),
                    reads=[RB("XGC"), xsg(lo)], writes=[xsg(lo)])

    def phase_FFN(s, last):
        RING_N[0] = 4
        ln_pre(XS, xsg(0), 0, GTF)
        layernorm(XS, xsg(0), 0, GTF, 1e-5, UG, RB("UG"), pre_done=True)
        pend = []

        def drain():
            for _ in range(2):
                if pend:
                    pend.pop(0)()
        for gf in range(GF):
            lo = gf * GTF
            xsb = xsg(lo)
            if gf + 1 < GF:
                ln_pre(XS, xsg(lo + GTF), lo + GTF, GTF)

            def ev_relu2(m, c0, cw, p):
                sl = m % 2
                sb.op("scalar", lambda e, m=m, cw=cw, p=p, sl=sl: e.activation(
                    RS32[:, sl, 0:cw], PS[p][:, 0:cw], AF.Relu),
                    reads=[psb[p]], writes=[RB("RS32")])
                sb.op("scalar", lambda e, m=m, cw=cw, sl=sl: e.activation(
                    HBF[:, m, 0:cw], RS32[:, sl, 0:cw], AF.Square),
                    reads=[RB("RS32")], writes=[RB("HBF")])
            proj_fm("wt_ffn_fc", UG, RB("UG"), GTF, range(FCH), ev_relu2,
                    post_m=drain)
            if gf + 1 < GF:
                ln_stats(XS, xsg(lo + GTF), lo + GTF, GTF, 1e-5, pre_done=True)
            while pend:
                pend.pop(0)()

            yb = RB("SPKY")
            extra0 = [B("KB")] if gf == 0 else []

            def ev_y(m, c0, cw, p):
                er = extra0 if m == 0 else []
                sb.op("scalar", lambda e, m=m, cw=cw, p=p: e.activation(
                    SPKY[:, m, 0:cw], PS[p][:, 0:cw], AF.Copy),
                    reads=[psb[p]] + er, writes=[yb])
            proj_fm("wt_ffn_proj", HBF, RB("HBF"), GTF, range(NCH), ev_y)
            if gf + 1 < GF:
                ln_finish(XS, xsg(lo + GTF), lo + GTF, GTF, UG, RB("UG"))
            mb, sbb = B("SPKM"), RB("SBF")
            for h in (0, 1):
                Y = SPKY[:, 4 * h:4 * h + 4, :]
                M_ = SPKM[:, 4 * h:4 * h + 4, :]
                S_ = SBF[:, :, :]
                XSg = XS[:, 4 * h:4 * h + 4, lo:lo + GTF]
                ONE = EONEC[:, 0:1].unsqueeze(1).broadcast_to([128, 4, GTF])
                er0 = extra0 if (gf == 0 and h == 0) else []

                def gt(src, src_b, S_=S_, ONE=ONE):
                    pend.append(lambda src=src, src_b=src_b: sb.op(
                        "vector", lambda e: e.tensor_tensor(
                            S_, src, ONE, ALU.is_gt),
                        reads=[src_b, B("EONEC")], writes=[sbb]))

                def acc_xs(XSg=XSg, S_=S_, xsb=xsb):
                    pend.append(lambda: sb.op(
                        "vector", lambda e: e.scalar_tensor_tensor(
                            XSg, S_, 0.2, XSg, ALU.mult, ALU.add),
                        reads=[sbb, xsb], writes=[xsb]))

                gt(Y, yb)
                acc_xs()
                pend.append(lambda M_=M_, Y=Y, er0=er0: sb.op(
                    "vector", lambda e: e.scalar_tensor_tensor(
                        M_, Y, 0.9, Y, ALU.mult, ALU.add),
                    reads=[yb] + er0, writes=[mb]))
                pend.append(lambda M_=M_, S_=S_: sb.op(
                    "vector", lambda e: e.tensor_sub(M_, M_, S_),
                    reads=[sbb, mb], writes=[mb]))
                for _t in range(3):
                    gt(M_, mb)
                    acc_xs()
                    pend.append(lambda M_=M_, Y=Y: sb.op(
                        "vector", lambda e: e.scalar_tensor_tensor(
                            M_, M_, 0.9, Y, ALU.mult, ALU.add),
                        reads=[yb, mb], writes=[mb]))
                    pend.append(lambda M_=M_, S_=S_: sb.op(
                        "vector", lambda e: e.tensor_sub(M_, M_, S_),
                        reads=[sbb, mb], writes=[mb]))
                gt(M_, mb)
                acc_xs()
        while pend:
            pend.pop(0)()
        if last:
            for gf in range(GF):
                lo = gf * GTF
                dst = y_d[s, :, lo:lo + GTF].rearrange("(c p) t -> p c t", p=128)
                sb.op("gpsimd", lambda e, dst=dst, lo=lo: e.dma_start(
                    out=dst, in_=XS[:, :, lo:lo + GTF]), reads=[xsg(lo)], inc=16)

    def phase_C(s):
        RING_N[0] = 3
        for g in range(G):
            lo = g * GT
            layernorm(XS, xsg(lo), lo, GT, 1e-5, UG, RB("UG"))
            proj_fm("wt_p", UG, RB("UG"), GT, range(NCH),
                    evac_act(XBG, RB("XBG"), "b_p"))

            def ev_into(dst_t, dst_b, nmb, lo=lo, g=g):
                def ev(m, c0, cw, p):
                    er = ([B("SPKY"), B("SPKM")]
                          if (g == 0 and m == 0) else [])
                    sb.op("scalar", lambda e, m=m, c0=c0, cw=cw, p=p: e.activation(
                        dst_t[:, m, lo + c0:lo + c0 + cw], PS[p][:, 0:cw],
                        AF.Identity, bias=bias_ap(nmb, m)),
                        reads=[psb[p], B("BIAS")] + er, writes=[dst_b])
                return ev
            proj_fm("wt_k", XBG, RB("XBG"), GT, range(NCH),
                    ev_into(KB, RB("KB"), "b_k"))
            proj_fm("wt_v", XBG, RB("XBG"), GT, range(NCH),
                    ev_into(VC, RB("VB"), "b_v"))
        for g in range(G):
            lo = g * GT
            layernorm(XS, xsg(lo), lo, GT, 1e-5, UG, RB("UG"))
            proj_fm("wt_p", UG, RB("UG"), GT, range(NCH),
                    evac_act(XBG, RB("XBG"), "b_p"))
            proj_fm("wt_q", XBG, RB("XBG"), GT, range(NCH),
                    evac_act(YG, RB("YG"), "b_q"))
            for n_idx, dlt in ((0, -1), (1, 0), (2, 1)):
                for c in range(NCH):
                    jj = c % 2
                    lo_s = lo + dlt
                    phw = (list(PH) if (g == 0 and n_idx == 0 and c == 0)
                           else [])
                    if 0 <= lo_s and lo_s + GT <= T:
                        sb.op("vector", lambda e, c=c, jj=jj, lo_s=lo_s: e.tensor_mul(
                            PHAT[:, jj, :], YG[:, c, :], KB[:, c, lo_s:lo_s + GT]),
                            reads=[RB("YG"), RB("KB")],
                            writes=[RB("PHAT")] + phw)
                    elif lo_s < 0:
                        sb.op("vector", lambda e, c=c, jj=jj: e.tensor_mul(
                            PHAT[:, jj, 1:GT], YG[:, c, 1:GT], KB[:, c, 0:GT - 1]),
                            reads=[RB("YG"), RB("KB")],
                            writes=[RB("PHAT")] + phw)
                        sb.op("vector", lambda e, c=c, jj=jj: e.tensor_mul(
                            PHAT[:, jj, 0:1], YG[:, c, 0:1], KB[:, c, T - 1:T]),
                            reads=[RB("YG"), RB("KB")], writes=[RB("PHAT")])
                    else:
                        sb.op("vector", lambda e, c=c, jj=jj, lo_s=lo_s: e.tensor_mul(
                            PHAT[:, jj, 0:GT - 1], YG[:, c, 0:GT - 1],
                            KB[:, c, lo_s:T]),
                            reads=[RB("YG"), RB("KB")], writes=[RB("PHAT")])
                        sb.op("vector", lambda e, c=c, jj=jj: e.tensor_mul(
                            PHAT[:, jj, GT - 1:GT], YG[:, c, GT - 1:GT],
                            KB[:, c, 0:1]),
                            reads=[RB("YG"), RB("KB")], writes=[RB("PHAT")])
                    sb.op("tensor", lambda e, c=c, jj=jj, n_idx=n_idx: e.matmul(
                        PS[n_idx][0:16, :], EIND[:, c, :], PHAT[:, jj, :],
                        start=(c == 0), stop=(c == NCH - 1)),
                        reads=[RB("PHAT"), B("EIND")], writes=[psb[n_idx]])
            ABSn = (ABS0, ABS1, ABS2)
            for n in range(3):
                sb.op("vector", lambda e, n=n: e.tensor_copy(
                    ABSn[n][:, :], PS[n][0:16, :]),
                    reads=[psb[n]], writes=[RB("ABS0")])
            sb.op("vector", lambda e: e.tensor_max(ABSM[:, :], ABS0[:, :], ABS1[:, :]),
                  reads=[RB("ABS0")], writes=[RB("ABSM")])
            sb.op("vector", lambda e: e.tensor_max(ABSM[:, :], ABSM[:, :], ABS2[:, :]),
                  reads=[RB("ABS0")], writes=[RB("ABSM")])
            for n in range(3):
                sb.op("vector", lambda e, n=n: e.tensor_sub(
                    ABSn[n][:, :], ABSn[n][:, :], ABSM[:, :]),
                    reads=[RB("ABS0"), RB("ABSM")], writes=[RB("ABS0")])
                sb.op("scalar", lambda e, n=n: e.activation(
                    ABSn[n][:, :], ABSn[n][:, :], AF.Exp, scale=SC),
                    reads=[RB("ABS0")], writes=[RB("ABS0")])
            sb.op("vector", lambda e: e.tensor_add(ABSM[:, :], ABS0[:, :], ABS1[:, :]),
                  reads=[RB("ABS0")], writes=[RB("ABSM")])
            sb.op("vector", lambda e: e.tensor_add(ABSM[:, :], ABSM[:, :], ABS2[:, :]),
                  reads=[RB("ABS0")], writes=[RB("ABSM")])
            sb.op("vector", lambda e: e.reciprocal(
                RS32[0:16, 0, :], ABSM[:, :]),
                reads=[RB("ABSM")], writes=[RB("RS32")])
            for n in range(3):
                sb.op("vector", lambda e, n=n: e.tensor_mul(
                    WROW[n][:, :], ABSn[n][:, :], RS32[0:16, 0, :]),
                    reads=[RB("ABS0"), RB("RS32")], writes=[RB("WROW0")])
            for c in range(NCH):
                for n, dlt in ((0, -1), (1, 0), (2, 1)):
                    wslot = 2 + n % 2
                    sb.op("tensor", lambda e, c=c, n=n, wslot=wslot: e.matmul(
                        PS[3][:, :], FIND[:, c, :], WROW[n][:, :],
                        start=True, stop=True),
                        reads=[RB("WROW0"), B("FIND")], writes=[psb[3]])
                    sb.op("scalar", lambda e, wslot=wslot: e.activation(
                        PHAT[:, wslot, :], PS[3][:, :], AF.Copy),
                        reads=[psb[3]], writes=[RB("PHAT")])
                    lo_s = lo + dlt
                    dsti = 0 if n == 0 else 1
                    dst = RS32[:, dsti, :]

                    def ml(out_ap, w_ap, v_ap, extra_reads=()):
                        sb.op("vector", lambda e, out_ap=out_ap, w_ap=w_ap,
                              v_ap=v_ap: e.tensor_mul(out_ap, w_ap, v_ap),
                              reads=[RB("PHAT"), RB("VB")], writes=[RB("RS32")])
                    if 0 <= lo_s and lo_s + GT <= T:
                        ml(dst, PHAT[:, wslot, :], VC[:, c, lo_s:lo_s + GT])
                    elif lo_s < 0:
                        ml(RS32[:, dsti, 1:GT], PHAT[:, wslot, 1:GT],
                           VC[:, c, 0:GT - 1])
                        ml(RS32[:, dsti, 0:1], PHAT[:, wslot, 0:1],
                           VC[:, c, T - 1:T])
                    else:
                        ml(RS32[:, dsti, 0:GT - 1], PHAT[:, wslot, 0:GT - 1],
                           VC[:, c, lo_s:T])
                        ml(RS32[:, dsti, GT - 1:GT], PHAT[:, wslot, GT - 1:GT],
                           VC[:, c, 0:1])
                    if n == 1:
                        sb.op("vector", lambda e: e.tensor_add(
                            RS32[:, 0, :], RS32[:, 0, :], RS32[:, 1, :]),
                            reads=[RB("RS32")], writes=[RB("RS32")])
                    elif n == 2:
                        sb.op("vector", lambda e, c=c: e.tensor_add(
                            UG[:, c, :], RS32[:, 0, :], RS32[:, 1, :]),
                            reads=[RB("RS32")], writes=[RB("UG")])
            proj_fm("wt_o", UG, RB("UG"), GT, range(NCH),
                    ev_stage_add_xs("b_o", lo))

    import os as _os
    order = _os.environ.get("MK_PHASES", "ABCD")
    for s in range(SEQS):
        for ph in order:
            if ph == "A":
                phase_A(s)
                if s == 0:
                    tap("xs_A", XS, [128, NCH, T], B("XS"))
            elif ph == "B":
                phase_FFN(s, last=False)
                if s == 0:
                    tap("xs_B", XS, [128, NCH, T], B("XS"))
            elif ph == "C":
                phase_C(s)
                if s == 0:
                    tap("xs_C", XS, [128, NCH, T], B("XS"))
            elif ph == "D":
                phase_FFN(s, last=True)
                if s == 0:
                    tap("xs_D", XS, [128, NCH, T], B("XS"))
    sb.plan["sync"].append(Wait("sync", sb.cnt["sync"]))
    sb.plan["sync"].append(Wait("gpsimd", sb.cnt["gpsimd"]))

    sems = {e: nc.alloc_semaphore(f"sem_{e}") for e in ENGS}
    nops = {e: sum(1 for it in sb.plan[e] if isinstance(it, Op)) for e in ENGS}
    print("plan ops:", nops, "waits:",
          {e: len(sb.plan[e]) - nops[e] for e in ENGS})

    import os as _os2
    _imap = {}

    def replay(name):
        def run(eng):
            for idx, it in enumerate(sb.plan[name]):
                if isinstance(it, Wait):
                    ins = eng.wait_ge(sems[it.src], it.val)
                    if _os2.environ.get("MK_IMAP"):
                        _imap[ins.ins.name] = (name, idx, f"WAIT {it.src}>={it.val}")
                else:
                    ins = it.fn(eng)
                    ins.then_inc(sems[name], it.inc)
                    if _os2.environ.get("MK_IMAP"):
                        _imap[ins.ins.name] = (name, idx, "op")
        return run

    with nc.Block() as block:
        block.tensor(replay("tensor"))
        block.vector(replay("vector"))
        block.scalar(replay("scalar"))
        block.gpsimd(replay("gpsimd"))
        block.sync(replay("sync"))
    nc._mk_imap = _imap
    nc._mk_plan = sb.plan
    return nc


_NC_CACHE = {}


def kernel(**inputs):
    def tr(a):
        return np.ascontiguousarray(np.asarray(a, dtype=np.float32).T)

    def vec(a):
        a = np.asarray(a, dtype=np.float32)
        return np.ascontiguousarray(a.reshape(-1, 128).T)

    x = np.asarray(inputs["L0_attn_a_x"], dtype=np.float32)
    xT = np.ascontiguousarray(np.swapaxes(x, 1, 2))
    common = {
        "wt_attn": tr(inputs["L0_attn_a_w_attn"]),
        "wt_proj": tr(inputs["L0_attn_a_w_proj"]),
        "wt_fc": tr(inputs["L0_attn_a_w_fc"]),
        "wt_mlp": tr(inputs["L0_attn_a_w_mlp_proj"]),
        "wt_p": tr(inputs["L0_attn_b_Wp"]),
        "wt_q": tr(inputs["L0_attn_b_Wq"]),
        "wt_k": tr(inputs["L0_attn_b_Wk"]),
        "wt_v": tr(inputs["L0_attn_b_Wv"]),
        "wt_o": tr(inputs["L0_attn_b_Wo"]),
        "wt_ffn_fc": tr(inputs["L0_ffn_W_fc"]),
        "wt_ffn_proj": tr(inputs["L0_ffn_W_proj"]),
        "b_attn": vec(inputs["L0_attn_a_b_attn"]),
        "b_proj": vec(inputs["L0_attn_a_b_proj"]),
        "b_fc": vec(inputs["L0_attn_a_b_fc"]),
        "b_mlp": vec(inputs["L0_attn_a_b_mlp_proj"]),
        "b_p": vec(inputs["L0_attn_b_bp"]),
        "b_q": vec(inputs["L0_attn_b_bq"]),
        "b_k": vec(inputs["L0_attn_b_bk"]),
        "b_v": vec(inputs["L0_attn_b_bv"]),
        "b_o": vec(inputs["L0_attn_b_bo"]),
        "b_attn_v": np.ascontiguousarray(
            np.asarray(inputs["L0_attn_a_b_attn"], dtype=np.float32)[2048:].reshape(1, -1)),
    }
    if "nc" not in _NC_CACHE:
        _NC_CACHE["nc"] = build_program()
    nc = _NC_CACHE["nc"]
    in_maps = [dict(common, x=np.ascontiguousarray(xT[2 * c:2 * c + 2]))
               for c in range(NCORES)]
    res = run_bass_kernel_spmd(nc, in_maps, core_ids=list(range(NCORES)))
    yT = np.concatenate([np.asarray(r["y"], dtype=np.float32)
                         for r in res.results], axis=0)
    return np.ascontiguousarray(np.swapaxes(yT, 1, 2))
